# version 03
# baseline (speedup 1.0000x reference)
import numpy as np
import ml_dtypes
from contextlib import ExitStack

import concourse.bass as bass
import concourse.mybir as mybir
from concourse.bass_utils import run_bass_kernel_spmd

F32 = mybir.dt.float32
F32R = mybir.dt.float32r
BF16 = mybir.dt.bfloat16
AF = mybir.ActivationFunctionType
ALU = mybir.AluOpType
AX = mybir.AxisListType

NCORES = 8
B, TSEQ, DM = 4, 4096, 2048
T = 2048
H = 16
NKC = 16
NTC = 16
HID = 8192
NM = 64
GFF = 8
NGRP = NM // GFF
EPS_LN = 1e-5
EPS_DEN = 1e-6


class Sem:
    def __init__(self, es, nc, name):
        self.h = es.enter_context(nc.semaphore(name))
        self.n = 0


def build_program(debug=False, phases=9):
    nc = bass.Bass("TRN2", target_bir_lowering=False)

    x_d = nc.dram_tensor("x", [T, DM], F32, kind="ExternalInput").ap()
    wqkv_d = nc.dram_tensor("w_qkv", [3, 128, NKC * 2048], F32, kind="ExternalInput").ap()
    wout_d = nc.dram_tensor("w_out", [128, NKC * DM], F32, kind="ExternalInput").ap()
    wg_d = nc.dram_tensor("w_gT", [NM, 128, NKC * 128], F32, kind="ExternalInput").ap()
    wu_d = nc.dram_tensor("w_uT", [NM, 128, NKC * 128], F32, kind="ExternalInput").ap()
    wd_d = nc.dram_tensor("w_dT", [NM, 128, DM], F32, kind="ExternalInput").ap()
    dec_d = nc.dram_tensor("decay", [1, H], F32, kind="ExternalInput").ap()
    sel_d = nc.dram_tensor("sel", [256, 32], F32, kind="ExternalInput").ap()
    id_d = nc.dram_tensor("ident", [128, 128], F32, kind="ExternalInput").ap()
    out_d = nc.dram_tensor("out", [T, DM], F32, kind="ExternalOutput").ap()

    spk = "ExternalOutput" if debug else "Internal"
    sp_q = nc.dram_tensor("sp_q", [DM, T], F32, kind=spk).ap()
    sp_k = nc.dram_tensor("sp_k", [DM, T], F32, kind=spk).ap()
    sp_kv = nc.dram_tensor("sp_kv", [DM, T], F32, kind=spk).ap()
    sp_x2 = nc.dram_tensor("sp_x2", [T, DM], F32, kind=spk).ap()
    sp_h = nc.dram_tensor("sp_h", [DM, T], BF16, kind=spk).ap()
    if debug:
        dbg_xnf = nc.dram_tensor("dbg_xnf", [128, NKC * 2048], BF16, kind="ExternalOutput").ap()
        dbg_ab = nc.dram_tensor("dbg_ab", [128, NKC * 2048], BF16, kind="ExternalOutput").ap()
        dbg_carry = nc.dram_tensor("dbg_carry", [128, 32], F32, kind="ExternalOutput").ap()
    cc_in_t = nc.dram_tensor("cc_in", [32, 128], F32)
    cc_out_t = nc.dram_tensor("cc_out", [256, 128], F32)
    cc_in = cc_in_t.ap()
    cc_out = cc_out_t.ap()

    ES = ExitStack()

    idb = ES.enter_context(nc.sbuf_tensor("idb", [128, 128], BF16))
    idf = ES.enter_context(nc.sbuf_tensor("idf", [128, 128], F32))
    dsb = ES.enter_context(nc.sbuf_tensor("dsb", [1, H], F32))
    dcol = ES.enter_context(nc.sbuf_tensor("dcol", [128, H], F32))
    sel_sb = ES.enter_context(nc.sbuf_tensor("sel_sb", [128, 64], F32))
    gat = ES.enter_context(nc.sbuf_tensor("gat", [128, 256], F32))
    ones_c = ES.enter_context(nc.sbuf_tensor("ones_c", [128, 1], BF16))
    ones_r = ES.enter_context(nc.sbuf_tensor("ones_r", [1, 128], F32R))
    ones_f = ES.enter_context(nc.sbuf_tensor("ones_f", [1, 128], F32))
    carry_acc = ES.enter_context(nc.sbuf_tensor("carry_acc", [128, 32], F32))
    carry_cols = ES.enter_context(nc.sbuf_tensor("carry_cols", [128, 32], F32))
    con_st = ES.enter_context(nc.sbuf_tensor("con_st", [32, 128], F32))
    stat = ES.enter_context(nc.sbuf_tensor("stat", [128, 8], F32))
    epsl = ES.enter_context(nc.sbuf_tensor("epsl", [128, 1], F32))

    sem = lambda name: Sem(ES, nc, name)
    s_ld = sem("s_ld")
    s_cst = sem("s_cst")
    s_idb = sem("s_idb")
    s_wqs = sem("s_wqs")
    s_wos = sem("s_wos")
    s_x = [sem("s_x0"), sem("s_x1")]
    s_sp = sem("s_sp")
    s_rd = sem("s_rd")
    s_ccw = sem("s_ccw")
    s_ccd = sem("s_ccd")
    s_gat = sem("s_gat")
    s_x2w = sem("s_x2w")
    s_hfl = sem("s_hfl")
    s_hh = sem("s_hh")
    s_w5 = [sem("s_w50"), sem("s_w51")]
    s_out = sem("s_out")
    s_pe = sem("s_pe")
    s_dve = sem("s_dve")
    s_act = sem("s_act")
    s_pool = sem("s_pool")

    block = ES.enter_context(nc.Block())

    S12 = ExitStack()
    xs = [S12.enter_context(nc.sbuf_tensor(f"xs{i}", [128, DM], F32)) for i in range(2)]
    xn_tm = [S12.enter_context(nc.sbuf_tensor(f"xn_tm{i}", [128, DM], BF16)) for i in range(2)]
    xnf = S12.enter_context(nc.sbuf_tensor("xnf", [128, NKC * 2048], BF16))
    wq_sb = S12.enter_context(nc.sbuf_tensor("wq_sb", [128, NKC * 2048], BF16))
    stg = [S12.enter_context(nc.sbuf_tensor(f"stg{i}", [128, 2048], F32)) for i in range(2)]
    e_st = S12.enter_context(nc.sbuf_tensor("e_st", [128, 2048], F32))
    r_st = S12.enter_context(nc.sbuf_tensor("r_st", [128, 2048], F32))
    kre = S12.enter_context(nc.sbuf_tensor("kre", [128, 2048], F32))

    P01 = ExitStack()
    pS = P01.enter_context(nc.psum_tensor("pS", [128, 512], F32))
    pT = [P01.enter_context(nc.psum_tensor(f"pT{i}", [128, 2048], BF16)) for i in range(2)]

    xnf_v = xnf[:].rearrange("p (k t) -> p k t", k=NKC)
    wq_v = wq_sb[:].rearrange("p (k n) -> p k n", k=NKC)

    @block.sync
    def _(sync):
        sync.dma_start(out=idf[:], in_=id_d[:]).then_inc(s_ld.h, 16)
        sync.dma_start(out=dsb[:], in_=dec_d[:]).then_inc(s_ld.h, 16)
        sync.dma_start(out=sel_sb[:, 0:32], in_=sel_d[0:128, :]).then_inc(s_ld.h, 16)
        sync.dma_start(out=sel_sb[:, 32:64], in_=sel_d[128:256, :]).then_inc(s_ld.h, 16)
    s_ld.n = 64

    @block.gpsimd
    def _(g):
        g.dma_start(out=idb[:], in_=id_d[:]).then_inc(s_idb.h, 16)
    s_idb.n = 16

    @block.vector
    def _(v):
        v.memset(ones_c[:], 1.0)
        v.memset(ones_f[:], 1.0)
        v.memset(carry_acc[:], 0.0)
        v.memset(epsl[:], EPS_LN)
        v.drain().then_inc(s_dve.h, 1)
    s_dve.n += 1
    dve_consts = s_dve.n

    @block.scalar
    def _(s):
        s.wait_ge(s_ld.h, 64)
        s.activation(dsb[:], dsb[:], AF.Sigmoid)
        s.wait_ge(s_dve.h, 1)
        s.activation(ones_r[:], ones_f[:], AF.Identity)
        s.drain().then_inc(s_act.h, 1)
    s_act.n += 1
    act_sig = s_act.n

    @block.tensor
    def _(t):
        t.wait_ge(s_act.h, act_sig)
        t.wait_ge(s_dve.h, dve_consts)
        t.matmul(pS[:, 0:H], ones_f[:], dsb[:], start=True, stop=True).then_inc(s_pe.h, 1)
    s_pe.n += 1
    pe_dcol = s_pe.n

    @block.vector
    def _(v):
        v.wait_ge(s_pe.h, pe_dcol)
        v.tensor_copy(dcol[:], pS[:, 0:H])
        v.drain().then_inc(s_dve.h, 1)
    s_dve.n += 1

    act_b1 = s_act.n
    dve_b1 = s_dve.n
    pe_b1 = s_pe.n

    @block.sync
    def _(sync):
        if phases < 1:
            return
        for tc in range(NTC):
            if tc >= 2:
                sync.wait_ge(s_act.h, act_b1 + 3 * (tc - 2) + 3)
            sync.dma_start(out=xs[tc % 2][:], in_=x_d[tc * 128:(tc + 1) * 128, :]) \
                .then_inc(s_x[tc % 2].h, 16)
            s_x[tc % 2].n += 16

    @block.scalar
    def _(s):
        if phases < 1:
            return
        for tc in range(NTC):
            s.wait_ge(s_x[tc % 2].h, 16 * (tc // 2 + 1))
            s.activation(e_st[:], xs[tc % 2][:], AF.Square, accum_out=stat[:, 1:2])
            s.drain().then_inc(s_act.h, 1)
            s.wait_ge(s_dve.h, dve_b1 + 3 * tc + 1)
            s.activation(stat[:, 4:5], stat[:, 3:4], AF.Sqrt, bias=epsl[:])
            s.drain().then_inc(s_act.h, 1)
            s.wait_ge(s_dve.h, dve_b1 + 3 * tc + 2)
            if tc >= 2:
                s.wait_ge(s_pe.h, pe_b1 + tc - 1)
            s.activation(xn_tm[tc % 2][:], xs[tc % 2][:], AF.Identity,
                         bias=stat[:, 6:7], scale=stat[:, 5:6])
            s.drain().then_inc(s_act.h, 1)
            s_act.n += 3

    @block.vector
    def _(v):
        if phases < 1:
            return
        for tc in range(NTC):
            v.wait_ge(s_x[tc % 2].h, 16 * (tc // 2 + 1))
            v.reduce_sum(stat[:, 0:1], xs[tc % 2][:], axis=AX.X)
            v.wait_ge(s_act.h, act_b1 + 3 * tc + 1)
            v.drain()
            v.tensor_scalar(stat[:, 2:3], stat[:, 0:1], 1.0 / DM, None, ALU.mult)
            v.drain()
            v.tensor_tensor(stat[:, 3:4], stat[:, 2:3], stat[:, 2:3], ALU.mult)
            v.drain()
            v.scalar_tensor_tensor(stat[:, 3:4], stat[:, 1:2], 1.0 / DM,
                                   stat[:, 3:4], ALU.mult, ALU.subtract)
            v.drain().then_inc(s_dve.h, 1)
            v.wait_ge(s_act.h, act_b1 + 3 * tc + 2)
            v.reciprocal(stat[:, 5:6], stat[:, 4:5])
            v.drain()
            v.scalar_tensor_tensor(stat[:, 6:7], stat[:, 2:3], -1.0,
                                   stat[:, 5:6], ALU.mult, ALU.mult)
            v.drain().then_inc(s_dve.h, 1)
            v.wait_ge(s_pe.h, pe_b1 + tc + 1)
            src = pT[tc % 2][:].rearrange("p (k t) -> p k t", k=NKC)
            v.tensor_copy(xnf_v[:, :, tc * 128:(tc + 1) * 128], src)
            v.drain().then_inc(s_dve.h, 1)
            s_dve.n += 3

    @block.tensor
    def _(t):
        if phases < 1:
            return
        t.wait_ge(s_idb.h, 16)
        for tc in range(NTC):
            t.wait_ge(s_act.h, act_b1 + 3 * tc + 3)
            if tc >= 2:
                t.wait_ge(s_dve.h, dve_b1 + 3 * (tc - 2) + 3)
            for k in range(NKC):
                mm = t.transpose(pT[tc % 2][:, k * 128:(k + 1) * 128],
                                 xn_tm[tc % 2][:, k * 128:(k + 1) * 128], idb[:])
            mm.then_inc(s_pe.h, 1)
            s_pe.n += 1

    dve_xnf = s_dve.n

    P01.close()
    P2 = ExitStack()
    pQ = [P2.enter_context(nc.psum_tensor(f"pQ{i}", [128, 2048], F32)) for i in range(2)]

    QUADS = [1, 2, 0]
    pe_b2 = s_pe.n
    act_b2 = s_act.n
    pool_b2 = s_pool.n

    def mmdone2(g):
        return pe_b2 + g + 1

    def psum2_free(g):
        qi, m = divmod(g, H)
        if qi == 0:
            return (s_act, act_b2 + m + 1)
        if qi == 1:
            return (s_dve, dve_xnf + H + m + 1)
        return (s_act, act_b2 + H + m + 1)

    def stgdone2(g):
        qi, m = divmod(g, H)
        if qi == 0:
            return (s_dve, dve_xnf + m + 1)
        if qi == 1:
            return (s_dve, dve_xnf + H + m + 1)
        return (s_pool, pool_b2 + m + 1)

    @block.gpsimd
    def _(g):
        if phases < 2:
            return
        for qi in range(3):
            if qi > 0:
                g.wait_ge(s_pe.h, mmdone2(qi * H - 1))
            g.dma_start(out=wq_sb[:], in_=wqkv_d[QUADS[qi]],
                        max_dma_last_dim=8192).then_inc(s_wqs.h, 16)
            s_wqs.n += 16
    cst_wq = [16 * (qi + 1) for qi in range(3)]

    @block.tensor
    def _(t):
        if phases < 2:
            return
        for qi in range(3):
            t.wait_ge(s_wqs.h, cst_wq[qi])
            if qi == 0:
                t.wait_ge(s_dve.h, dve_xnf)
            for m in range(H):
                g = qi * H + m
                if g >= 2:
                    fs, fv = psum2_free(g - 2)
                    t.wait_ge(fs.h, fv)
                for k in range(NKC):
                    for tg in range(4):
                        mm = t.matmul(pQ[g % 2][:, tg * 512:(tg + 1) * 512],
                                      wq_v[:, k, m * 128:(m + 1) * 128],
                                      xnf_v[:, k, tg * 512:(tg + 1) * 512],
                                      start=(k == 0), stop=(k == NKC - 1))
                mm.then_inc(s_pe.h, 1)
                s_pe.n += 1

    @block.scalar
    def _(s):
        if phases < 2:
            return
        for m in range(H):
            g = m
            s.wait_ge(s_pe.h, mmdone2(g))
            if m >= 1:
                s.wait_ge(s_dve.h, dve_xnf + m)
            s.activation(e_st[:], pQ[g % 2][:], AF.Exp)
            s.activation(r_st[:], pQ[g % 2][:], AF.Relu)
            s.drain().then_inc(s_act.h, 1)
            s_act.n += 1

    @block.scalar
    def _(s):
        if phases < 2:
            return
        for m in range(H):
            s.wait_ge(s_sp.h, 16 * H)
            if m >= 1:
                fs, fv = stgdone2(H + m - 1)
                s.wait_ge(fs.h, fv)
            s.dma_start(out=kre[:], in_=sp_k[m * 128:(m + 1) * 128, :]) \
                .then_inc(s_rd.h, 16)
            s_rd.n += 16

    @block.scalar
    def _(s):
        if phases < 2:
            return
        for m in range(H):
            g = 2 * H + m
            s.wait_ge(s_pe.h, mmdone2(g))
            if m == 0:
                s.wait_ge(s_dve.h, dve_xnf + 2 * H)
            else:
                s.wait_ge(s_pool.h, pool_b2 + m)
            s.activation(e_st[:], pQ[g % 2][:], AF.Exp)
            s.activation(r_st[:], pQ[g % 2][:], AF.Relu)
            s.drain().then_inc(s_act.h, 1)
            s_act.n += 1

    @block.vector
    def _(v):
        if phases < 2:
            return
        for m in range(H):
            v.wait_ge(s_act.h, act_b2 + m + 1)
            if m >= 2:
                v.wait_ge(s_sp.h, 16 * m)
            v.scalar_tensor_tensor(stg[m % 2][:], e_st[:], 1.0,
                                   r_st[:], ALU.min, ALU.add)
            v.drain().then_inc(s_dve.h, 1)
            s_dve.n += 1
        for m in range(H):
            g = H + m
            v.wait_ge(s_pe.h, mmdone2(g))
            v.wait_ge(s_rd.h, 16 * (m + 1))
            v.wait_ge(s_sp.h, 16 * (H + m))
            v.tensor_tensor(stg[m % 2][:], pQ[g % 2][:], kre[:], ALU.mult)
            v.drain().then_inc(s_dve.h, 1)
            s_dve.n += 1

    @block.gpsimd
    def _(g_):
        if phases < 2:
            return
        for m in range(H):
            g_.wait_ge(s_act.h, act_b2 + H + m + 1)
            g_.wait_ge(s_sp.h, 16 * (2 * H + m))
            g_.tensor_scalar(e_st[:], e_st[:], 1.0, None, ALU.min)
            g_.drain()
            g_.tensor_tensor(stg[m % 2][:], e_st[:], r_st[:], ALU.add)
            g_.drain().then_inc(s_pool.h, 1)
            s_pool.n += 1

    spill_of = {}

    @block.sync
    def _(sync):
        if phases < 2:
            return
        targets = [sp_k, sp_kv, sp_q]
        for qi in range(3):
            for m in range(H):
                g = qi * H + m
                fs, fv = stgdone2(g)
                sync.wait_ge(fs.h, fv)
                sync.dma_start(out=targets[qi][m * 128:(m + 1) * 128, :],
                               in_=stg[m % 2][:]).then_inc(s_sp.h, 16)
                s_sp.n += 16
                spill_of[g] = s_sp.n
    sp_all2 = s_sp.n
    pool_all2 = pool_b2 + H
    act_all2 = act_b2 + 2 * H
    dve_all2 = dve_xnf + 2 * H

    S12.close()
    P2.close()

    S3w = ExitStack()
    AB = S3w.enter_context(nc.sbuf_tensor("AB", [128, NKC * 2048], BF16))
    wo_sb = S3w.enter_context(nc.sbuf_tensor("wo_sb", [128, NKC * 2048], BF16))
    AB_v = AB[:].rearrange("p (k t) -> p k t", k=NKC)
    wo_v = wo_sb[:].rearrange("p (k n) -> p k n", k=NKC)

    S3 = ExitStack()
    kv_sb = S3.enter_context(nc.sbuf_tensor("kv_sb", [128, T], F32))
    k_sb = S3.enter_context(nc.sbuf_tensor("k_sb", [128, T], F32))
    q_sb = S3.enter_context(nc.sbuf_tensor("q_sb", [128, T], F32))
    S_kv = S3.enter_context(nc.sbuf_tensor("S_kv", [128, T], F32))
    S_ks = S3.enter_context(nc.sbuf_tensor("S_ks", [128, T], F32))
    dect = S3.enter_context(nc.sbuf_tensor("dect", [128, T], F32))
    pbf = S3.enter_context(nc.sbuf_tensor("pbf", [128, T], BF16))
    rows3 = S3.enter_context(nc.sbuf_tensor("rows3", [128, T], F32))
    rden_t = S3.enter_context(nc.sbuf_tensor("rden_t", [1, T], F32R))
    P3 = ExitStack()
    pBC = P3.enter_context(nc.psum_tensor("pBC", [128, 2048], F32))

    @block.gpsimd
    def _(g):
        if phases < 3:
            return
        g.wait_ge(s_sp.h, sp_all2)
        g.dma_start(out=wo_sb[:], in_=wout_d[:], max_dma_last_dim=8192) \
            .then_inc(s_wos.h, 16)
    s_wos.n = 16

    dve_b3a = s_dve.n
    rd_b3a = s_rd.n

    @block.scalar
    def _(s):
        if phases < 3:
            return
        for h in range(H):
            if h == 0:
                s.wait_ge(s_sp.h, sp_all2)
                s.wait_ge(s_pool.h, pool_all2)
            else:
                s.wait_ge(s_dve.h, dve_b3a + h)
            s.dma_start(out=kv_sb[:], in_=sp_kv[h * 128:(h + 1) * 128, :]) \
                .then_inc(s_rd.h, 16)
            s.dma_start(out=k_sb[:], in_=sp_k[h * 128:(h + 1) * 128, :]) \
                .then_inc(s_rd.h, 16)
            s_rd.n += 32

    @block.vector
    def _(v):
        if phases < 3:
            return
        for h in range(H):
            if h == 0:
                v.wait_ge(s_sp.h, sp_all2)
                v.wait_ge(s_pool.h, pool_all2)
                v.wait_ge(s_act.h, act_all2)
            v.wait_ge(s_rd.h, rd_b3a + 32 * (h + 1))
            v.memset(dect[:], 1.0)
            v.drain()
            v.tensor_scalar(dect[:], dect[:], dcol[:, h:h + 1], None, ALU.mult)
            v.drain()
            v.tensor_tensor_scan(S_kv[:], dect[:], kv_sb[:], 0.0, ALU.mult, ALU.add)
            v.drain()
            v.tensor_copy(carry_acc[:, h:h + 1], S_kv[:, T - 1:T])
            v.tensor_tensor_scan(S_ks[:], dect[:], k_sb[:], 0.0, ALU.mult, ALU.add)
            v.drain()
            v.tensor_copy(carry_acc[:, H + h:H + h + 1], S_ks[:, T - 1:T])
            v.drain().then_inc(s_dve.h, 1)
            s_dve.n += 1
    dve_3a_scans = s_dve.n

    @block.tensor
    def _(t):
        if phases < 3:
            return
        t.wait_ge(s_sp.h, sp_all2)
        t.wait_ge(s_dve.h, dve_3a_scans)
        t.transpose(pBC[0:32, 0:128], carry_acc[:], idf[:]).then_inc(s_pe.h, 1)
        s_pe.n += 1
    pe_cat = s_pe.n

    @block.vector
    def _(v):
        if phases < 3:
            return
        v.wait_ge(s_pe.h, pe_cat)
        v.tensor_copy(con_st[:], pBC[0:32, 0:128])
        v.drain().then_inc(s_dve.h, 1)
        s_dve.n += 1
    dve_cat = s_dve.n

    @block.sync
    def _(sync):
        if phases < 3:
            return
        sync.wait_ge(s_dve.h, dve_cat)
        sync.dma_start(out=cc_in, in_=con_st[:]).then_inc(s_ccw.h, 16)
        s_ccw.n += 16

    @block.gpsimd
    def _(g):
        if phases < 3:
            return
        g.wait_ge(s_ccw.h, 16)
        g.collective_compute(
            "AllGather", ALU.bypass,
            replica_groups=[list(range(NCORES))],
            ins=[cc_in_t.ap().opt()], outs=[cc_out_t.ap().opt()],
        ).then_inc(s_ccd.h)
        s_ccd.n += 1

    @block.sync
    def _(sync):
        if phases < 3:
            return
        sync.wait_ge(s_ccd.h, 1)
        sync.dma_start(out=gat[:, 0:128], in_=cc_out[0:128, :]).then_inc(s_gat.h, 16)
        sync.dma_start(out=gat[:, 128:256], in_=cc_out[128:256, :]).then_inc(s_gat.h, 16)
        s_gat.n += 32

    @block.tensor
    def _(t):
        if phases < 3:
            return
        t.wait_ge(s_gat.h, 32)
        t.wait_ge(s_dve.h, dve_cat)
        t.matmul(pBC[0:32, 0:128], sel_sb[:, 0:32], gat[:, 0:128],
                 start=True, stop=False)
        t.matmul(pBC[0:32, 0:128], sel_sb[:, 32:64], gat[:, 128:256],
                 start=False, stop=True).then_inc(s_pe.h, 1)
        s_pe.n += 1
    pe_sel = s_pe.n

    @block.vector
    def _(v):
        if phases < 3:
            return
        v.wait_ge(s_pe.h, pe_sel)
        v.tensor_copy(con_st[:], pBC[0:32, 0:128])
        v.drain().then_inc(s_dve.h, 1)
        s_dve.n += 1
    dve_sel = s_dve.n

    @block.tensor
    def _(t):
        if phases < 3:
            return
        t.wait_ge(s_dve.h, dve_sel)
        t.transpose(pBC[0:128, 0:32], con_st[:], idf[0:32, 0:32]).then_inc(s_pe.h, 1)
        s_pe.n += 1
    pe_car = s_pe.n

    @block.vector
    def _(v):
        if phases < 3:
            return
        v.wait_ge(s_pe.h, pe_car)
        v.tensor_copy(carry_cols[:], pBC[0:128, 0:32])
        v.drain().then_inc(s_dve.h, 1)
        s_dve.n += 1
    dve_car = s_dve.n

    dve_b3b = s_dve.n
    pool_b3b = s_pool.n
    act_b3b = s_act.n
    pe_b3b = s_pe.n
    rd_b3b = s_rd.n

    @block.scalar
    def _(s):
        if phases < 4:
            return
        for h in range(H):
            if h >= 1:
                s.wait_ge(s_dve.h, dve_b3b + 3 * (h - 1) + 3)
            else:
                s.wait_ge(s_dve.h, dve_3a_scans)
            s.wait_ge(s_sp.h, spill_of[2 * H + h])
            s.dma_start(out=kv_sb[:], in_=sp_kv[h * 128:(h + 1) * 128, :]) \
                .then_inc(s_rd.h, 16)
            s.dma_start(out=k_sb[:], in_=sp_k[h * 128:(h + 1) * 128, :]) \
                .then_inc(s_rd.h, 16)
            s.dma_start(out=q_sb[:], in_=sp_q[h * 128:(h + 1) * 128, :]) \
                .then_inc(s_rd.h, 16)
            s_rd.n += 48
            s.wait_ge(s_dve.h, dve_b3b + 3 * h + 2)
            s.activation(rows3[32:33, :], rows3[0:1, :], AF.Ln)
            s.drain()
            s.activation(rden_t[:], rows3[32:33, :], AF.Exp, scale=-1.0)
            s.drain().then_inc(s_act.h, 1)
            s_act.n += 1

    @block.vector
    def _(v):
        if phases < 4:
            return
        for h in range(H):
            if h == 0:
                v.wait_ge(s_dve.h, dve_car)
            v.wait_ge(s_rd.h, rd_b3b + 48 * (h + 1))
            v.memset(dect[:], 1.0)
            v.drain()
            v.tensor_scalar(dect[:], dect[:], dcol[:, h:h + 1], None, ALU.mult)
            v.drain()
            v.tensor_tensor_scan(S_kv[:], dect[:], kv_sb[:],
                                 carry_cols[:, h:h + 1], ALU.mult, ALU.add)
            v.tensor_tensor_scan(S_ks[:], dect[:], k_sb[:],
                                 carry_cols[:, H + h:H + h + 1], ALU.mult, ALU.add)
            v.drain().then_inc(s_dve.h, 1)
            v.wait_ge(s_pe.h, pe_b3b + 2 * h + 1)
            v.tensor_scalar(rows3[0:1, :], pBC[0:1, :], EPS_DEN, None, ALU.max)
            v.drain().then_inc(s_dve.h, 1)
            v.wait_ge(s_pe.h, pe_b3b + 2 * h + 2)
            v.tensor_tensor(AB_v[:, h, :], kv_sb[:], pBC[:, :], ALU.mult)
            v.drain().then_inc(s_dve.h, 1)
            s_dve.n += 3

    @block.gpsimd
    def _(g):
        if phases < 4:
            return
        for h in range(H):
            g.wait_ge(s_dve.h, dve_b3b + 3 * h + 1)
            g.tensor_tensor(kv_sb[:], q_sb[:], S_kv[:], ALU.mult)
            g.tensor_tensor(pbf[:], q_sb[:], S_ks[:], ALU.mult)
            g.drain().then_inc(s_pool.h, 1)
            s_pool.n += 1

    @block.tensor
    def _(t):
        if phases < 4:
            return
        rden_r = rden_t[:]
        for h in range(H):
            t.wait_ge(s_pool.h, pool_b3b + h + 1)
            if h >= 1:
                t.wait_ge(s_dve.h, dve_b3b + 3 * (h - 1) + 3)
            for tg in range(4):
                mm = t.matmul(pBC[0:1, tg * 512:(tg + 1) * 512], ones_c[:],
                              pbf[:, tg * 512:(tg + 1) * 512], start=True, stop=True)
            mm.then_inc(s_pe.h, 1)
            s_pe.n += 1
            t.wait_ge(s_act.h, act_b3b + h + 1)
            for tg in range(4):
                mm = t.matmul(pBC[:, tg * 512:(tg + 1) * 512], ones_r[:],
                              rden_r[:, tg * 512:(tg + 1) * 512],
                              start=True, stop=True)
            mm.then_inc(s_pe.h, 1)
            s_pe.n += 1
    dve_3b_done = dve_b3b + 3 * H

    S3.close()
    P3.close()
    S3c = ExitStack()
    pW = [S3c.enter_context(nc.psum_tensor(f"pW{i}", [128, 2048], F32)) for i in range(2)]
    xs3 = S3c.enter_context(nc.sbuf_tensor("xs3", [128, DM], F32))
    x2sb = S3c.enter_context(nc.sbuf_tensor("x2sb", [128, DM], F32))
    hf = S3c.enter_context(nc.sbuf_tensor("hf", [128, DM], F32))
    hstg = S3c.enter_context(nc.sbuf_tensor("hstg", [128, NKC * 512], BF16))
    hstg_v = hstg[:].rearrange("p (k t) -> p k t", k=NKC)
    sp_h_v = sp_h.rearrange("(k p) t -> p k t", p=128)

    pe_b3c = s_pe.n
    act_b3c = s_act.n
    dve_b3c = s_dve.n
    rd_b3c = s_rd.n

    @block.scalar
    def _(s):
        if phases < 5:
            return
        for tc in range(NTC):
            if tc == 0:
                s.wait_ge(s_dve.h, dve_3b_done)
            else:
                s.wait_ge(s_dve.h, dve_b3c + 6 * (tc - 1) + 3)
            s.dma_start(out=xs3[:], in_=x_d[tc * 128:(tc + 1) * 128, :]) \
                .then_inc(s_rd.h, 16)
            s_rd.n += 16
            s.wait_ge(s_pe.h, pe_b3c + 2 * tc + 1)
            s.activation(hf[:], pW[tc % 2][:], AF.Square, accum_out=stat[:, 1:2])
            s.drain().then_inc(s_act.h, 1)
            s.wait_ge(s_dve.h, dve_b3c + 6 * tc + 1)
            s.activation(stat[:, 4:5], stat[:, 3:4], AF.Sqrt, bias=epsl[:])
            s.drain().then_inc(s_act.h, 1)
            s.wait_ge(s_dve.h, dve_b3c + 6 * tc + 3)
            s.activation(hf[:], x2sb[:], AF.Square, accum_out=stat[:, 1:2])
            s.drain().then_inc(s_act.h, 1)
            s.wait_ge(s_dve.h, dve_b3c + 6 * tc + 4)
            s.activation(stat[:, 4:5], stat[:, 3:4], AF.Sqrt, bias=epsl[:])
            s.drain().then_inc(s_act.h, 1)
            s.wait_ge(s_dve.h, dve_b3c + 6 * tc + 5)
            s.activation(hf[:], x2sb[:], AF.Identity,
                         bias=stat[:, 6:7], scale=stat[:, 5:6])
            s.drain().then_inc(s_act.h, 1)
            s_act.n += 5

    @block.vector
    def _(v):
        if phases < 5:
            return
        for tc in range(NTC):
            v.wait_ge(s_pe.h, pe_b3c + 2 * tc + 1)
            v.wait_ge(s_act.h, act_b3c + 5 * tc + 1)
            v.reduce_sum(stat[:, 0:1], pW[tc % 2][:], axis=AX.X)
            v.drain()
            v.tensor_scalar(stat[:, 2:3], stat[:, 0:1], 1.0 / DM, None, ALU.mult)
            v.drain()
            v.tensor_tensor(stat[:, 3:4], stat[:, 2:3], stat[:, 2:3], ALU.mult)
            v.drain()
            v.scalar_tensor_tensor(stat[:, 3:4], stat[:, 1:2], 1.0 / DM,
                                   stat[:, 3:4], ALU.mult, ALU.subtract)
            v.drain().then_inc(s_dve.h, 1)
            v.wait_ge(s_act.h, act_b3c + 5 * tc + 2)
            v.reciprocal(stat[:, 5:6], stat[:, 4:5])
            v.drain()
            v.scalar_tensor_tensor(stat[:, 6:7], stat[:, 2:3], -1.0,
                                   stat[:, 5:6], ALU.mult, ALU.mult)
            v.drain().then_inc(s_dve.h, 1)
            v.wait_ge(s_rd.h, rd_b3c + 16 * (tc + 1))
            if tc >= 1:
                v.wait_ge(s_x2w.h, 16 * tc)
            v.tensor_scalar(xs3[:], xs3[:], stat[:, 6:7], None, ALU.add)
            v.drain()
            v.scalar_tensor_tensor(x2sb[:], pW[tc % 2][:], stat[:, 5:6],
                                   xs3[:], ALU.mult, ALU.add)
            v.drain().then_inc(s_dve.h, 1)
            v.wait_ge(s_act.h, act_b3c + 5 * tc + 3)
            v.reduce_sum(stat[:, 0:1], x2sb[:], axis=AX.X)
            v.drain()
            v.tensor_scalar(stat[:, 2:3], stat[:, 0:1], 1.0 / DM, None, ALU.mult)
            v.drain()
            v.tensor_tensor(stat[:, 3:4], stat[:, 2:3], stat[:, 2:3], ALU.mult)
            v.drain()
            v.scalar_tensor_tensor(stat[:, 3:4], stat[:, 1:2], 1.0 / DM,
                                   stat[:, 3:4], ALU.mult, ALU.subtract)
            v.drain().then_inc(s_dve.h, 1)
            v.wait_ge(s_act.h, act_b3c + 5 * tc + 4)
            v.reciprocal(stat[:, 5:6], stat[:, 4:5])
            v.drain()
            v.scalar_tensor_tensor(stat[:, 6:7], stat[:, 2:3], -1.0,
                                   stat[:, 5:6], ALU.mult, ALU.mult)
            v.drain().then_inc(s_dve.h, 1)
            v.wait_ge(s_pe.h, pe_b3c + 2 * tc + 2)
            if tc % 4 == 0 and tc >= 4:
                v.wait_ge(s_hfl.h, 16 * (tc // 4))
            src = pW[tc % 2][:].rearrange("p (k t) -> p k t", k=NKC)
            v.tensor_copy(hstg_v[:, :, (tc % 4) * 128:(tc % 4) * 128 + 128], src)
            v.drain().then_inc(s_dve.h, 1)
            s_dve.n += 6

    @block.tensor
    def _(t):
        if phases < 5:
            return
        t.wait_ge(s_wos.h, 16)
        t.wait_ge(s_dve.h, dve_3b_done)
        for tc in range(NTC):
            if tc >= 2:
                t.wait_ge(s_dve.h, dve_b3c + 6 * (tc - 2) + 6)
            for hk in range(NKC):
                for tg in range(4):
                    mm = t.matmul(pW[tc % 2][:, tg * 512:(tg + 1) * 512],
                                  AB_v[:, hk, tc * 128:(tc + 1) * 128],
                                  wo_v[:, hk, tg * 512:(tg + 1) * 512],
                                  start=(hk == 0), stop=(hk == NKC - 1))
            mm.then_inc(s_pe.h, 1)
            s_pe.n += 1
            t.wait_ge(s_act.h, act_b3c + 5 * tc + 5)
            t.wait_ge(s_dve.h, dve_b3c + 6 * tc + 3)
            for k in range(NKC):
                mm = t.transpose(pW[tc % 2][:, k * 128:(k + 1) * 128],
                                 hf[:, k * 128:(k + 1) * 128], idf[:])
            mm.then_inc(s_pe.h, 1)
            s_pe.n += 1

    @block.sync
    def _(sync):
        if phases < 5:
            return
        for tc in range(NTC):
            sync.wait_ge(s_dve.h, dve_b3c + 6 * tc + 3)
            sync.dma_start(out=sp_x2[tc * 128:(tc + 1) * 128, :], in_=x2sb[:]) \
                .then_inc(s_x2w.h, 16)
            s_x2w.n += 16
            if tc % 4 == 3:
                sync.wait_ge(s_dve.h, dve_b3c + 6 * tc + 6)
                grp = tc // 4
                sync.dma_start(out=sp_h_v[:, :, grp * 512:(grp + 1) * 512],
                               in_=hstg_v[:, :, :]).then_inc(s_hfl.h, 16)
                s_hfl.n += 16
    dve_3c_done = dve_b3c + 6 * NTC

    S3c.close()
    S3w.close()
    S5 = ExitStack()
    pGU = [S5.enter_context(nc.psum_tensor(f"pGU{i}", [128, 2048], F32)) for i in range(2)]
    hh = S5.enter_context(nc.sbuf_tensor("hh", [128, NKC * 1024], BF16))
    dacc = S5.enter_context(nc.sbuf_tensor("dacc", [128, 8 * 2048], F32))
    gtd = S5.enter_context(nc.sbuf_tensor("gtd", [128, GFF * 1024], BF16))
    wgs = [S5.enter_context(nc.sbuf_tensor(f"wgs{i}", [128, 2048], BF16)) for i in range(2)]
    wus = [S5.enter_context(nc.sbuf_tensor(f"wus{i}", [128, 2048], BF16)) for i in range(2)]
    wdg = S5.enter_context(nc.sbuf_tensor("wdg", [128, GFF * 2048], BF16))
    sil = S5.enter_context(nc.sbuf_tensor("sil", [128, 1024], F32))
    xf = S5.enter_context(nc.sbuf_tensor("xf", [128, DM], F32))
    scr5 = S5.enter_context(nc.sbuf_tensor("scr5", [128, DM], F32))
    outst = S5.enter_context(nc.sbuf_tensor("outst", [128, DM], F32))
    hh_v = hh[:].rearrange("p (k t) -> p k t", k=NKC)
    gtd_v = gtd[:].rearrange("p (m t) -> p m t", m=GFF)
    wdg_v = wdg[:].rearrange("p (m n) -> p m n", m=GFF)

    pe_b5 = s_pe.n
    act_b5 = s_act.n
    dve_b5 = s_dve.n
    cst_b5 = s_cst.n
    HALF_DVE = NGRP * (GFF + 8) + 8 * 3
    HALF_ACT = NGRP * GFF + 8 * 2
    HALF_PE = NGRP * (GFF + 8)

    def dve_gated(half, g, mi):
        return dve_b5 + half * HALF_DVE + g * (GFF + 8) + mi + 1

    def dve_wadd(half, g, w):
        return dve_b5 + half * HALF_DVE + g * (GFF + 8) + GFF + w + 1

    def dve_ln4(half, w, i):
        return dve_b5 + half * HALF_DVE + NGRP * (GFF + 8) + w * 3 + i + 1

    def act_sil(half, g, mi):
        return act_b5 + half * HALF_ACT + g * GFF + mi + 1

    def act_ln4(half, w, i):
        return act_b5 + half * HALF_ACT + NGRP * GFF + w * 2 + i + 1

    def pe_gu(half, g, mi):
        return pe_b5 + half * HALF_PE + g * (GFF + 8) + mi + 1

    def pe_dw(half, g, w):
        return pe_b5 + half * HALF_PE + g * (GFF + 8) + GFF + w + 1

    def cst_w5(half, g, mi):
        return 48 * (half * (NM // 2) + (g * GFF + mi) // 2 + 1)

    @block.sync
    def _(sync):
        if phases < 6:
            return
        for half in range(2):
            sync.wait_ge(s_hfl.h, 64)
            if half == 1:
                sync.wait_ge(s_pe.h, pe_gu(0, NGRP - 1, GFF - 1))
            sync.dma_start(out=hh[:], in_=sp_h_v[:, :, half * 1024:(half + 1) * 1024]) \
                .then_inc(s_hh.h, 16)
            s_hh.n += 16
            for w in range(8):
                tcg = half * 8 + w
                if tcg == 0:
                    sync.wait_ge(s_dve.h, dve_3c_done)
                else:
                    pv = dve_ln4(half, w - 1, 2) if w >= 1 else dve_ln4(0, 7, 2)
                    sync.wait_ge(s_dve.h, pv)
                sync.dma_start(out=xf[:], in_=sp_x2[tcg * 128:(tcg + 1) * 128, :]) \
                    .then_inc(s_gat.h, 16)
                s_gat.n += 16
                sync.wait_ge(s_dve.h, dve_ln4(half, w, 2))
                sync.dma_start(out=out_d[tcg * 128:(tcg + 1) * 128, :], in_=outst[:]) \
                    .then_inc(s_out.h, 16)
                s_out.n += 16
        sync.wait_ge(s_out.h, s_out.n)

    @block.gpsimd
    def _(g_):
        if phases < 6:
            return
        for half in range(2):
            for g in range(NGRP):
                for mi in range(GFF):
                    m = g * GFF + mi
                    sl = mi % 2
                    if half == 0 and g == 0 and mi == 0:
                        g_.wait_ge(s_dve.h, dve_3c_done)
                        g_.wait_ge(s_x2w.h, 16 * NTC)
                        g_.wait_ge(s_hfl.h, 64)
                    gi = g * GFF + mi
                    if half > 0 or gi >= 2:
                        if gi >= 2:
                            pg, pmi = divmod(gi - 2, GFF)
                            g_.wait_ge(s_dve.h, dve_gated(half, pg, pmi))
                        else:
                            g_.wait_ge(s_dve.h, dve_gated(half - 1, NGRP - 1, GFF - 2 + gi))
                    g_.dma_start(out=wgs[sl][:], in_=wg_d[m], max_dma_last_dim=8192) \
                        .then_inc(s_w5[sl].h, 16)
                    g_.dma_start(out=wus[sl][:], in_=wu_d[m], max_dma_last_dim=8192) \
                        .then_inc(s_w5[sl].h, 16)
                    if mi == 0 and (g >= 1 or half >= 1):
                        if g >= 1:
                            g_.wait_ge(s_pe.h, pe_dw(half, g - 1, 7))
                        else:
                            g_.wait_ge(s_pe.h, pe_dw(half - 1, NGRP - 1, 7))
                    g_.dma_start(out=wdg_v[:, mi, :], in_=wd_d[m],
                                 max_dma_last_dim=8192).then_inc(s_w5[sl].h, 16)

    @block.tensor
    def _(t):
        if phases < 6:
            return
        for half in range(2):
            t.wait_ge(s_hh.h, 16 * (half + 1))
            if half == 0:
                t.wait_ge(s_dve.h, dve_3c_done)
            for g in range(NGRP):
                for mi in range(GFF):
                    gi = g * GFF + mi
                    t.wait_ge(s_w5[mi % 2].h, cst_w5(half, g, mi))
                    if gi >= 2:
                        pg, pmi = divmod(gi - 2, GFF)
                        t.wait_ge(s_dve.h, dve_gated(half, pg, pmi))
                        if mi < 2 and g >= 1:
                            t.wait_ge(s_dve.h, dve_wadd(half, g - 1, 7))
                    elif half == 1:
                        t.wait_ge(s_dve.h, dve_wadd(0, NGRP - 1, 7))
                    wgv = wgs[mi % 2][:].rearrange("p (k n) -> p k n", k=NKC)
                    wuv = wus[mi % 2][:].rearrange("p (k n) -> p k n", k=NKC)
                    for k in range(NKC):
                        for tg in range(2):
                            mm = t.matmul(pGU[mi % 2][:, tg * 512:(tg + 1) * 512],
                                          wgv[:, k, :],
                                          hh_v[:, k, tg * 512:(tg + 1) * 512],
                                          start=(k == 0), stop=(k == NKC - 1))
                        for tg in range(2):
                            mm = t.matmul(pGU[mi % 2][:, 1024 + tg * 512:1024 + (tg + 1) * 512],
                                          wuv[:, k, :],
                                          hh_v[:, k, tg * 512:(tg + 1) * 512],
                                          start=(k == 0), stop=(k == NKC - 1))
                    mm.then_inc(s_pe.h, 1)
                    s_pe.n += 1
                for w in range(8):
                    t.wait_ge(s_dve.h, dve_gated(half, g, GFF - 1))
                    if w >= 2:
                        t.wait_ge(s_dve.h, dve_wadd(half, g, w - 2))
                    elif g >= 1:
                        t.wait_ge(s_dve.h, dve_wadd(half, g - 1, 7))
                    elif half == 1:
                        t.wait_ge(s_dve.h, dve_wadd(0, NGRP - 1, 7))
                    for mi in range(GFF):
                        for ng in range(4):
                            mm = t.matmul(pGU[w % 2][:, ng * 512:(ng + 1) * 512],
                                          gtd_v[:, mi, w * 128:(w + 1) * 128],
                                          wdg_v[:, mi, ng * 512:(ng + 1) * 512],
                                          start=(mi == 0), stop=(mi == GFF - 1))
                    mm.then_inc(s_pe.h, 1)
                    s_pe.n += 1

    @block.scalar
    def _(s):
        if phases < 6:
            return
        for half in range(2):
            for g in range(NGRP):
                for mi in range(GFF):
                    s.wait_ge(s_pe.h, pe_gu(half, g, mi))
                    gi = g * GFF + mi
                    if half > 0 or gi >= 1:
                        if gi >= 1:
                            pg, pmi = divmod(gi - 1, GFF)
                            s.wait_ge(s_dve.h, dve_gated(half, pg, pmi))
                        else:
                            s.wait_ge(s_dve.h, dve_gated(half - 1, NGRP - 1, GFF - 1))
                    s.activation(sil[:], pGU[mi % 2][:, 0:1024], AF.Silu)
                    s.drain().then_inc(s_act.h, 1)
                    s_act.n += 1
            for w in range(8):
                s.wait_ge(s_dve.h, dve_wadd(half, NGRP - 1, w))
                s.activation(scr5[:], dacc[:, w * 2048:(w + 1) * 2048],
                             AF.Square, accum_out=stat[:, 1:2])
                s.drain().then_inc(s_act.h, 1)
                s.wait_ge(s_dve.h, dve_ln4(half, w, 0))
                s.activation(stat[:, 4:5], stat[:, 3:4], AF.Sqrt, bias=epsl[:])
                s.drain().then_inc(s_act.h, 1)
                s_act.n += 2

    @block.vector
    def _(v):
        if phases < 6:
            return
        for half in range(2):
            for g in range(NGRP):
                for mi in range(GFF):
                    v.wait_ge(s_act.h, act_sil(half, g, mi))
                    v.tensor_tensor(gtd_v[:, mi, :], sil[:],
                                    pGU[mi % 2][:, 1024:2048], ALU.mult)
                    v.drain().then_inc(s_dve.h, 1)
                    s_dve.n += 1
                for w in range(8):
                    v.wait_ge(s_pe.h, pe_dw(half, g, w))
                    if g == 0:
                        v.tensor_copy(dacc[:, w * 2048:(w + 1) * 2048], pGU[w % 2][:])
                    else:
                        v.tensor_tensor(dacc[:, w * 2048:(w + 1) * 2048],
                                        dacc[:, w * 2048:(w + 1) * 2048],
                                        pGU[w % 2][:], ALU.add)
                    v.drain().then_inc(s_dve.h, 1)
                    s_dve.n += 1
            for w in range(8):
                v.wait_ge(s_act.h, act_ln4(half, w, 0))
                v.reduce_sum(stat[:, 0:1], dacc[:, w * 2048:(w + 1) * 2048], axis=AX.X)
                v.drain()
                v.tensor_scalar(stat[:, 2:3], stat[:, 0:1], 1.0 / DM, None, ALU.mult)
                v.drain()
                v.tensor_tensor(stat[:, 3:4], stat[:, 2:3], stat[:, 2:3], ALU.mult)
                v.drain()
                v.scalar_tensor_tensor(stat[:, 3:4], stat[:, 1:2], 1.0 / DM,
                                       stat[:, 3:4], ALU.mult, ALU.subtract)
                v.drain().then_inc(s_dve.h, 1)
                v.wait_ge(s_act.h, act_ln4(half, w, 1))
                v.reciprocal(stat[:, 5:6], stat[:, 4:5])
                v.drain()
                v.scalar_tensor_tensor(stat[:, 6:7], stat[:, 2:3], -1.0,
                                       stat[:, 5:6], ALU.mult, ALU.mult)
                v.drain().then_inc(s_dve.h, 1)
                v.wait_ge(s_gat.h, 32 + 16 * (half * 8 + w + 1))
                v.wait_ge(s_out.h, 16 * (half * 8 + w))
                v.tensor_scalar(xf[:], xf[:], stat[:, 6:7], None, ALU.add)
                v.drain()
                v.scalar_tensor_tensor(outst[:], dacc[:, w * 2048:(w + 1) * 2048],
                                       stat[:, 5:6], xf[:], ALU.mult, ALU.add)
                v.drain().then_inc(s_dve.h, 1)
                s_dve.n += 3

    if debug:
        @block.sync
        def _(sync):
            if 1 <= phases <= 2:
                sync.wait_ge(s_dve.h, dve_xnf)
                sync.dma_start(out=dbg_xnf, in_=xnf[:]).then_inc(s_out.h, 16)
                s_out.n += 16
            if phases >= 3:
                sync.wait_ge(s_dve.h, dve_car)
                sync.dma_start(out=dbg_carry, in_=carry_cols[:]).then_inc(s_out.h, 16)
                s_out.n += 16
            if 4 <= phases <= 5:
                sync.wait_ge(s_dve.h, dve_3b_done)
                sync.dma_start(out=dbg_ab, in_=AB[:]).then_inc(s_out.h, 16)
                s_out.n += 16
            sync.wait_ge(s_sp.h, s_sp.n)
            sync.wait_ge(s_x2w.h, s_x2w.n)
            sync.wait_ge(s_hfl.h, s_hfl.n)
            sync.wait_ge(s_out.h, s_out.n)

    S5.close()
    ES.close()
    return nc


_CACHE = {}


def _prep_inputs(L0_attn_x, L0_attn_w_qkv, L0_attn_w_out, L0_attn_decay_param,
                 L0_ffn_w_gate, L0_ffn_w_up, L0_ffn_w_down):
    x = np.ascontiguousarray(np.asarray(L0_attn_x, dtype=np.float32).reshape(B * TSEQ, DM))
    w_qkv = np.asarray(L0_attn_w_qkv, dtype=np.float32)
    w_out = np.asarray(L0_attn_w_out, dtype=np.float32)
    decay = np.ascontiguousarray(np.asarray(L0_attn_decay_param, dtype=np.float32).reshape(1, H))
    w_gate = np.asarray(L0_ffn_w_gate, dtype=np.float32)
    w_up = np.asarray(L0_ffn_w_up, dtype=np.float32)
    w_down = np.asarray(L0_ffn_w_down, dtype=np.float32)

    wq_l = np.ascontiguousarray(
        w_qkv.reshape(NKC, 128, 3, H, 128).transpose(2, 1, 0, 3, 4)
        .reshape(3, 128, NKC * 2048))
    wo_l = np.ascontiguousarray(
        w_out.reshape(NKC, 128, DM).transpose(1, 0, 2).reshape(128, NKC * DM))
    wg_l = np.ascontiguousarray(
        w_gate.T.reshape(NKC, 128, NM, 128).transpose(2, 1, 0, 3)
        .reshape(NM, 128, NKC * 128))
    wu_l = np.ascontiguousarray(
        w_up.T.reshape(NKC, 128, NM, 128).transpose(2, 1, 0, 3)
        .reshape(NM, 128, NKC * 128))
    wd_l = np.ascontiguousarray(w_down.T.reshape(NM, 128, DM))

    ident = np.eye(128, dtype=np.float32)
    sels = []
    for c in range(NCORES):
        S = np.zeros((256, 32), dtype=np.float32)
        if c % 2 == 1:
            r = c - 1
            S[32 * r:32 * r + 32, :] = np.eye(32, dtype=np.float32)
        sels.append(S)

    in_maps = []
    for c in range(NCORES):
        in_maps.append({
            "x": np.ascontiguousarray(x[c * T:(c + 1) * T, :]),
            "w_qkv": wq_l, "w_out": wo_l, "w_gT": wg_l, "w_uT": wu_l,
            "w_dT": wd_l, "decay": decay, "sel": sels[c], "ident": ident,
        })
    return in_maps


def kernel(**inputs):
    in_maps = _prep_inputs(**inputs)
    nc = build_program()
    res = run_bass_kernel_spmd(nc, in_maps, core_ids=list(range(NCORES)))
    out = np.concatenate([np.asarray(res.results[c]["out"]) for c in range(NCORES)],
                         axis=0)
    return out.reshape(B, TSEQ, DM).astype(np.float32)


def run_dev(inputs, phases=9, debug=True):
    in_maps = _prep_inputs(**inputs)
    nc = build_program(debug=debug, phases=phases)
    res = run_bass_kernel_spmd(nc, in_maps, core_ids=list(range(NCORES)))
    return res.results


# version 04
# speedup vs baseline: 1.0908x; 1.0908x over previous
import numpy as np
import ml_dtypes
from contextlib import ExitStack

import concourse.bass as bass
import concourse.mybir as mybir
from concourse.bass_utils import run_bass_kernel_spmd

F32 = mybir.dt.float32
F32R = mybir.dt.float32r
BF16 = mybir.dt.bfloat16
AF = mybir.ActivationFunctionType
ALU = mybir.AluOpType
AX = mybir.AxisListType

NCORES = 8
B, TSEQ, DM = 4, 4096, 2048
T = 2048
H = 16
NKC = 16
NTC = 16
HID = 8192
NM = 64
GFF = 8
NGRP = NM // GFF
EPS_LN = 1e-5
EPS_DEN = 1e-6


class Sem:
    def __init__(self, es, nc, name):
        self.h = es.enter_context(nc.semaphore(name))
        self.n = 0


def bcast_col(ap_col, n):
    row = ap_col.ap[0][0]
    return bass.AP(ap_col.tensor, ap_col.offset, [[row, 128], [0, n]])


def build_program(debug=False, phases=9):
    nc = bass.Bass("TRN2", target_bir_lowering=False)

    x_d = nc.dram_tensor("x", [T, DM], F32, kind="ExternalInput").ap()
    wqkv_d = nc.dram_tensor("w_qkv", [3, 2, 128, 16384], F32, kind="ExternalInput").ap()
    wout_d = nc.dram_tensor("w_out", [128, NKC * DM], F32, kind="ExternalInput").ap()
    wg_d = nc.dram_tensor("w_gT", [NM, 128, NKC * 128], F32, kind="ExternalInput").ap()
    wu_d = nc.dram_tensor("w_uT", [NM, 128, NKC * 128], F32, kind="ExternalInput").ap()
    wd_d = nc.dram_tensor("w_dT", [NM, 128, DM], F32, kind="ExternalInput").ap()
    dec_d = nc.dram_tensor("decay", [1, H], F32, kind="ExternalInput").ap()
    sel_d = nc.dram_tensor("sel", [128, NCORES * 128], F32, kind="ExternalInput").ap()
    id_d = nc.dram_tensor("ident", [128, 128], F32, kind="ExternalInput").ap()
    out_d = nc.dram_tensor("out", [T, DM], F32, kind="ExternalOutput").ap()

    spk = "ExternalOutput" if debug else "Internal"
    sp_q = nc.dram_tensor("sp_q", [DM, T], F32, kind=spk).ap()
    sp_k = nc.dram_tensor("sp_k", [DM, T], F32, kind=spk).ap()
    sp_kv = nc.dram_tensor("sp_kv", [DM, T], F32, kind=spk).ap()
    sp_x2 = nc.dram_tensor("sp_x2", [T, DM], F32, kind=spk).ap()
    sp_h = nc.dram_tensor("sp_h", [DM, T], BF16, kind=spk).ap()
    if debug:
        dbg_xnf = nc.dram_tensor("dbg_xnf", [128, NKC * 2048], BF16, kind="ExternalOutput").ap()
        dbg_ab = nc.dram_tensor("dbg_ab", [128, NKC * 2048], BF16, kind="ExternalOutput").ap()
        dbg_carry = nc.dram_tensor("dbg_carry", [128, 32], F32, kind="ExternalOutput").ap()
    cc_in_t = nc.dram_tensor("cc_in", [128, 32], F32)
    cc_out_t = nc.dram_tensor("cc_out", [NCORES * 128, 32], F32)
    cc_in = cc_in_t.ap()
    cc_out = cc_out_t.ap()

    ES = ExitStack()

    idb = ES.enter_context(nc.sbuf_tensor("idb", [128, 128], BF16))
    idf = ES.enter_context(nc.sbuf_tensor("idf", [128, 128], F32))
    dsb = ES.enter_context(nc.sbuf_tensor("dsb", [1, H], F32))
    dcol = ES.enter_context(nc.sbuf_tensor("dcol", [128, H], F32))
    sel_sb = ES.enter_context(nc.sbuf_tensor("sel_sb", [128, NCORES * 128], F32))
    gat = ES.enter_context(nc.sbuf_tensor("gat", [128, NCORES * 32], F32))
    ones_c = ES.enter_context(nc.sbuf_tensor("ones_c", [128, 1], BF16))
    ones_r = ES.enter_context(nc.sbuf_tensor("ones_r", [1, 128], F32R))
    ones_f = ES.enter_context(nc.sbuf_tensor("ones_f", [1, 128], F32))
    carry_acc = ES.enter_context(nc.sbuf_tensor("carry_acc", [128, 32], F32))
    carry_cols = ES.enter_context(nc.sbuf_tensor("carry_cols", [128, 32], F32))
    stat = ES.enter_context(nc.sbuf_tensor("stat", [128, 8], F32))
    bnb = ES.enter_context(nc.sbuf_tensor("bnb", [128, 24], F32))
    epsl = ES.enter_context(nc.sbuf_tensor("epsl", [128, 1], F32))

    sem = lambda name: Sem(ES, nc, name)
    s_ld = sem("s_ld")
    s_idb = sem("s_idb")
    s_wq2 = [sem("s_wq20"), sem("s_wq21")]
    s_wos = sem("s_wos")
    s_x = [sem("s_x0")]
    s_sp = sem("s_sp")
    s_rd = sem("s_rd")
    s_ccw = sem("s_ccw")
    s_ccd = sem("s_ccd")
    s_gat = sem("s_gat")
    s_x2w = sem("s_x2w")
    s_hfl = sem("s_hfl")
    s_hh = sem("s_hh")
    s_out = sem("s_out")
    s_w5 = [sem("s_w50"), sem("s_w51")]
    s_pe = sem("s_pe")
    s_dve = sem("s_dve")
    s_act = sem("s_act")
    s_pool = sem("s_pool")

    block = ES.enter_context(nc.Block())

    def ln_stats(v, src_v, n=2048):
        for i in range(n // 512):
            v.bn_stats(bnb[:, 6 * i:6 * i + 6], src_v[:, i * 512:(i + 1) * 512])
        v.drain()
        v.bn_aggr(stat[:, 2:4], bnb[:, 0:6 * (n // 512)])
        v.drain()

    def ln_finish_dve(v):
        v.reciprocal(stat[:, 5:6], stat[:, 4:5])
        v.drain()
        v.scalar_tensor_tensor(stat[:, 6:7], stat[:, 2:3], -1.0,
                               stat[:, 5:6], ALU.mult, ALU.mult)
        v.drain()

    S12 = ExitStack()
    xnf = S12.enter_context(nc.sbuf_tensor("xnf", [128, NKC * 2048], BF16))
    wq2 = [S12.enter_context(nc.sbuf_tensor(f"wq2_{i}", [128, 16384], BF16)) for i in range(2)]
    stg = [S12.enter_context(nc.sbuf_tensor(f"stg{i}", [128, 2048], F32)) for i in range(2)]
    e_st = S12.enter_context(nc.sbuf_tensor("e_st", [128, 2048], F32))
    r_st = S12.enter_context(nc.sbuf_tensor("r_st", [128, 2048], F32))
    kre = S12.enter_context(nc.sbuf_tensor("kre", [128, 2048], F32))
    S_scr = S12.enter_context(nc.sbuf_tensor("S_scr", [128, 2048], F32))
    S1 = ExitStack()
    xs = S1.enter_context(nc.sbuf_tensor("xs", [128, DM], F32))
    xn_tm = [S1.enter_context(nc.sbuf_tensor(f"xn_tm{i}", [128, DM], BF16)) for i in range(2)]

    P01 = ExitStack()
    pS = P01.enter_context(nc.psum_tensor("pS", [128, 512], F32))
    pT = [P01.enter_context(nc.psum_tensor(f"pT{i}", [128, 2048], BF16)) for i in range(2)]

    xnf_v = xnf[:].rearrange("p (k t) -> p k t", k=NKC)
    wq2_v = [w[:].rearrange("p (k m c) -> p k m c", k=NKC, m=8) for w in wq2]

    @block.sync
    def _(sync):
        sync.dma_start(out=idf[:], in_=id_d[:]).then_inc(s_ld.h, 16)
        sync.dma_start(out=dsb[:], in_=dec_d[:]).then_inc(s_ld.h, 16)
        sync.dma_start(out=sel_sb[:], in_=sel_d[:]).then_inc(s_ld.h, 16)
    s_ld.n = 48

    @block.gpsimd
    def _(g):
        g.dma_start(out=idb[:], in_=id_d[:]).then_inc(s_idb.h, 16)
    s_idb.n = 16

    @block.vector
    def _(v):
        v.memset(ones_c[:], 1.0)
        v.memset(ones_f[:], 1.0)
        v.memset(carry_acc[:], 0.0)
        v.memset(epsl[:], EPS_LN)
        v.drain().then_inc(s_dve.h, 1)
    s_dve.n += 1
    dve_consts = s_dve.n

    @block.scalar
    def _(s):
        s.wait_ge(s_ld.h, 48)
        s.activation(dsb[:], dsb[:], AF.Sigmoid)
        s.wait_ge(s_dve.h, dve_consts)
        s.activation(ones_r[:], ones_f[:], AF.Identity)
        s.drain().then_inc(s_act.h, 1)
    s_act.n += 1
    act_sig = s_act.n

    @block.tensor
    def _(t):
        t.wait_ge(s_act.h, act_sig)
        t.matmul(pS[:, 0:H], ones_f[:], dsb[:], start=True, stop=True).then_inc(s_pe.h, 1)
    s_pe.n += 1
    pe_dcol = s_pe.n

    @block.vector
    def _(v):
        v.wait_ge(s_pe.h, pe_dcol)
        v.tensor_copy(dcol[:], pS[:, 0:H])
        v.drain().then_inc(s_dve.h, 1)
    s_dve.n += 1

    act_b1 = s_act.n
    dve_b1 = s_dve.n
    pe_b1 = s_pe.n

    @block.sync
    def _(sync):
        if phases < 1:
            return
        for tc in range(NTC):
            if tc >= 1:
                sync.wait_ge(s_act.h, act_b1 + 2 * (tc - 1) + 2)
            sync.dma_start(out=xs[:], in_=x_d[tc * 128:(tc + 1) * 128, :]) \
                .then_inc(s_x[0].h, 16)
            s_x[0].n += 16

    @block.scalar
    def _(s):
        if phases < 1:
            return
        for tc in range(NTC):
            s.wait_ge(s_dve.h, dve_b1 + 3 * tc + 1)
            s.activation(stat[:, 4:5], stat[:, 3:4], AF.Sqrt, bias=epsl[:])
            s.drain().then_inc(s_act.h, 1)
            s.wait_ge(s_dve.h, dve_b1 + 3 * tc + 2)
            if tc >= 2:
                s.wait_ge(s_pe.h, pe_b1 + tc - 1)
            s.activation(xn_tm[tc % 2][:], xs[:], AF.Identity,
                         bias=stat[:, 6:7], scale=stat[:, 5:6])
            s.drain().then_inc(s_act.h, 1)
            s_act.n += 2

    @block.vector
    def _(v):
        if phases < 1:
            return
        for tc in range(NTC):
            v.wait_ge(s_x[0].h, 16 * (tc + 1))
            ln_stats(v, xs[:])
            v.drain().then_inc(s_dve.h, 1)
            v.wait_ge(s_act.h, act_b1 + 2 * tc + 1)
            ln_finish_dve(v)
            v.drain().then_inc(s_dve.h, 1)
            v.wait_ge(s_pe.h, pe_b1 + tc + 1)
            src = pT[tc % 2][:].rearrange("p (k t) -> p k t", k=NKC)
            v.tensor_copy(xnf_v[:, :, tc * 128:(tc + 1) * 128], src)
            v.drain().then_inc(s_dve.h, 1)
            s_dve.n += 3

    @block.tensor
    def _(t):
        if phases < 1:
            return
        t.wait_ge(s_idb.h, 16)
        for tc in range(NTC):
            t.wait_ge(s_act.h, act_b1 + 2 * tc + 2)
            if tc >= 2:
                t.wait_ge(s_dve.h, dve_b1 + 3 * (tc - 2) + 3)
            for k in range(NKC):
                mm = t.transpose(pT[tc % 2][:, k * 128:(k + 1) * 128],
                                 xn_tm[tc % 2][:, k * 128:(k + 1) * 128], idb[:])
            mm.then_inc(s_pe.h, 1)
            s_pe.n += 1
    dve_xnf = dve_b1 + 3 * NTC

    P01.close()
    S1.close()
    P2 = ExitStack()
    pQ = [P2.enter_context(nc.psum_tensor(f"pQ{i}", [128, 2048], F32)) for i in range(2)]

    QUADS = [1, 2, 0]
    pe_b2 = s_pe.n
    act_b2 = s_act.n
    pool_b2 = s_pool.n

    def mmdone2(g):
        return pe_b2 + g + 1

    def v_tt(m):
        return dve_xnf + H + 2 * m + 1

    def v_scan(m):
        return dve_xnf + H + 2 * m + 2

    def psum2_free(g):
        qi, m = divmod(g, H)
        if qi == 0:
            return (s_act, act_b2 + m + 1)
        if qi == 1:
            return (s_dve, v_tt(m))
        return (s_act, act_b2 + H + m + 1)

    def stgdone2(g):
        qi, m = divmod(g, H)
        if qi == 0:
            return (s_dve, dve_xnf + m + 1)
        if qi == 1:
            return (s_dve, v_tt(m))
        return (s_pool, pool_b2 + m + 1)

    def stream_idx(g):
        return 2 * (g // H) + (g % H) // 8

    @block.gpsimd
    def _(g_):
        if phases < 2:
            return
        for idx in range(6):
            qi, mh = divmod(idx, 2)
            if idx >= 2:
                g_.wait_ge(s_pe.h, mmdone2(8 * (idx - 2) + 7))
            g_.dma_start(out=wq2[idx % 2][:], in_=wqkv_d[QUADS[qi], mh],
                         max_dma_last_dim=8192).then_inc(s_wq2[idx % 2].h, 16)
            s_wq2[idx % 2].n += 16

    @block.tensor
    def _(t):
        if phases < 2:
            return
        for qi in range(3):
            for m in range(H):
                g = qi * H + m
                idx = stream_idx(g)
                if m % 8 == 0:
                    t.wait_ge(s_wq2[idx % 2].h, 16 * (idx // 2 + 1))
                if g == 0:
                    t.wait_ge(s_dve.h, dve_xnf)
                if g >= 2:
                    fs, fv = psum2_free(g - 2)
                    t.wait_ge(fs.h, fv)
                wv = wq2_v[idx % 2]
                for k in range(NKC):
                    for tg in range(4):
                        mm = t.matmul(pQ[g % 2][:, tg * 512:(tg + 1) * 512],
                                      wv[:, k, m % 8, :],
                                      xnf_v[:, k, tg * 512:(tg + 1) * 512],
                                      start=(k == 0), stop=(k == NKC - 1))
                mm.then_inc(s_pe.h, 1)
                s_pe.n += 1

    @block.scalar
    def _(s):
        if phases < 2:
            return
        for m in range(H):
            s.wait_ge(s_pe.h, mmdone2(m))
            if m >= 1:
                s.wait_ge(s_dve.h, dve_xnf + m)
            s.activation(e_st[:], pQ[m % 2][:], AF.Exp)
            s.activation(r_st[:], pQ[m % 2][:], AF.Relu)
            s.drain().then_inc(s_act.h, 1)
            s_act.n += 1

    @block.scalar
    def _(s):
        if phases < 2:
            return
        for m in range(H):
            s.wait_ge(s_sp.h, 16 * H)
            if m >= 1:
                s.wait_ge(s_dve.h, v_scan(m - 1))
            s.dma_start(out=kre[:], in_=sp_k[m * 128:(m + 1) * 128, :]) \
                .then_inc(s_rd.h, 16)
            s_rd.n += 16

    @block.scalar
    def _(s):
        if phases < 2:
            return
        for m in range(H):
            g = 2 * H + m
            s.wait_ge(s_pe.h, mmdone2(g))
            if m == 0:
                s.wait_ge(s_dve.h, v_scan(H - 1))
            else:
                s.wait_ge(s_pool.h, pool_b2 + m)
            s.activation(e_st[:], pQ[g % 2][:], AF.Exp)
            s.activation(r_st[:], pQ[g % 2][:], AF.Relu)
            s.drain().then_inc(s_act.h, 1)
            s_act.n += 1

    @block.vector
    def _(v):
        if phases < 2:
            return
        for m in range(H):
            v.wait_ge(s_act.h, act_b2 + m + 1)
            if m >= 2:
                v.wait_ge(s_sp.h, 16 * m)
            v.scalar_tensor_tensor(stg[m % 2][:], e_st[:], 1.0,
                                   r_st[:], ALU.min, ALU.add)
            v.drain().then_inc(s_dve.h, 1)
            s_dve.n += 1
        for m in range(H):
            g = H + m
            v.wait_ge(s_pe.h, mmdone2(g))
            v.wait_ge(s_rd.h, 16 * (m + 1))
            v.wait_ge(s_sp.h, 16 * (H + m))
            if m >= 2:
                v.wait_ge(s_dve.h, v_scan(m - 2))
            v.tensor_tensor(stg[m % 2][:], pQ[g % 2][:], kre[:], ALU.mult)
            v.drain().then_inc(s_dve.h, 1)
            dbc = bcast_col(dcol[:, m:m + 1], T)
            v.tensor_tensor_scan(S_scr[:], dbc, stg[m % 2][:], 0.0, ALU.mult, ALU.add)
            v.drain()
            v.tensor_copy(carry_acc[:, m:m + 1], S_scr[:, T - 1:T])
            v.drain()
            v.tensor_tensor_scan(S_scr[:], dbc, kre[:], 0.0, ALU.mult, ALU.add)
            v.drain()
            v.tensor_copy(carry_acc[:, H + m:H + m + 1], S_scr[:, T - 1:T])
            v.drain().then_inc(s_dve.h, 1)
            s_dve.n += 2

    @block.gpsimd
    def _(g_):
        if phases < 2:
            return
        for m in range(H):
            g_.wait_ge(s_act.h, act_b2 + H + m + 1)
            g_.wait_ge(s_sp.h, 16 * (2 * H + m))
            g_.tensor_scalar(e_st[:], e_st[:], 1.0, None, ALU.min)
            g_.drain()
            g_.tensor_tensor(stg[m % 2][:], e_st[:], r_st[:], ALU.add)
            g_.drain().then_inc(s_pool.h, 1)
            s_pool.n += 1
    pool_all2 = s_pool.n

    spill_of = {}

    @block.sync
    def _(sync):
        if phases < 2:
            return
        targets = [sp_k, sp_kv, sp_q]
        for qi in range(3):
            for m in range(H):
                g = qi * H + m
                fs, fv = stgdone2(g)
                sync.wait_ge(fs.h, fv)
                sync.dma_start(out=targets[qi][m * 128:(m + 1) * 128, :],
                               in_=stg[m % 2][:]).then_inc(s_sp.h, 16)
                s_sp.n += 16
                spill_of[g] = s_sp.n
    sp_all2 = s_sp.n

    @block.sync
    def _(sync):
        if phases < 3:
            return
        sync.wait_ge(s_dve.h, v_scan(H - 1))
        sync.dma_start(out=cc_in, in_=carry_acc[:]).then_inc(s_ccw.h, 16)
        s_ccw.n += 16

    @block.gpsimd
    def _(g_):
        if phases < 3:
            return
        g_.wait_ge(s_ccw.h, 16)
        g_.collective_compute(
            "AllGather", ALU.bypass,
            replica_groups=[list(range(NCORES))],
            ins=[cc_in_t.ap().opt()], outs=[cc_out_t.ap().opt()],
        ).then_inc(s_ccd.h)
        s_ccd.n += 1

    @block.sync
    def _(sync):
        if phases < 3:
            return
        sync.wait_ge(s_ccd.h, 1)
        sync.dma_start(out=gat[:].rearrange("p (r j) -> p r j", r=NCORES),
                       in_=cc_out.rearrange("(r p) j -> p r j", p=128)) \
            .then_inc(s_gat.h, 16)
        s_gat.n += 16

    @block.tensor
    def _(t):
        if phases < 3:
            return
        t.wait_ge(s_gat.h, 16)
        for gg in (46, 47):
            fs, fv = psum2_free(gg)
            t.wait_ge(fs.h, fv)
        for r in range(NCORES):
            mm = t.matmul(pQ[0][:, 0:32], sel_sb[:, r * 128:(r + 1) * 128],
                          gat[:, r * 32:(r + 1) * 32],
                          start=(r == 0), stop=(r == NCORES - 1))
        mm.then_inc(s_pe.h, 1)
        s_pe.n += 1
    pe_sel = s_pe.n

    @block.vector
    def _(v):
        if phases < 3:
            return
        v.wait_ge(s_pe.h, pe_sel)
        v.tensor_copy(carry_cols[:], pQ[0][:, 0:32])
        v.drain().then_inc(s_dve.h, 1)
        s_dve.n += 1
    dve_car = s_dve.n

    S12.close()
    P2.close()

    S3w = ExitStack()
    AB = S3w.enter_context(nc.sbuf_tensor("AB", [128, NKC * 2048], BF16))
    wo_sb = S3w.enter_context(nc.sbuf_tensor("wo_sb", [128, NKC * 2048], BF16))
    AB_v = AB[:].rearrange("p (k t) -> p k t", k=NKC)
    wo_v = wo_sb[:].rearrange("p (k n) -> p k n", k=NKC)

    S3 = ExitStack()
    kv_sb = S3.enter_context(nc.sbuf_tensor("kv_sb", [128, T], F32))
    k_sb = S3.enter_context(nc.sbuf_tensor("k_sb", [128, T], F32))
    q_sb = S3.enter_context(nc.sbuf_tensor("q_sb", [128, T], F32))
    S_kv = S3.enter_context(nc.sbuf_tensor("S_kv", [128, T], F32))
    S_ks = S3.enter_context(nc.sbuf_tensor("S_ks", [128, T], F32))
    pbf = S3.enter_context(nc.sbuf_tensor("pbf", [128, T], BF16))
    rows3 = S3.enter_context(nc.sbuf_tensor("rows3", [128, T], F32))
    rden_t = S3.enter_context(nc.sbuf_tensor("rden_t", [1, T], F32R))
    P3 = ExitStack()
    pBC = P3.enter_context(nc.psum_tensor("pBC", [128, 2048], F32))

    @block.gpsimd
    def _(g):
        if phases < 4:
            return
        g.wait_ge(s_sp.h, sp_all2)
        g.dma_start(out=wo_sb[:], in_=wout_d[:], max_dma_last_dim=8192) \
            .then_inc(s_wos.h, 16)
        s_wos.n = 16

    dve_b3b = s_dve.n
    pool_b3b = s_pool.n
    act_b3b = s_act.n
    pe_b3b = s_pe.n
    rd_b3b = s_rd.n

    @block.scalar
    def _(s):
        if phases < 4:
            return
        for h in range(H):
            if h >= 1:
                s.wait_ge(s_dve.h, dve_b3b + 3 * (h - 1) + 3)
            else:
                s.wait_ge(s_dve.h, dve_car)
                s.wait_ge(s_pool.h, pool_all2)
                s.wait_ge(s_sp.h, sp_all2)
            s.wait_ge(s_sp.h, spill_of[2 * H + h])
            s.dma_start(out=kv_sb[:], in_=sp_kv[h * 128:(h + 1) * 128, :]) \
                .then_inc(s_rd.h, 16)
            s.dma_start(out=k_sb[:], in_=sp_k[h * 128:(h + 1) * 128, :]) \
                .then_inc(s_rd.h, 16)
            s.dma_start(out=q_sb[:], in_=sp_q[h * 128:(h + 1) * 128, :]) \
                .then_inc(s_rd.h, 16)
            s_rd.n += 48
            s.wait_ge(s_dve.h, dve_b3b + 3 * h + 2)
            s.activation(rows3[32:33, :], rows3[0:1, :], AF.Ln)
            s.drain()
            s.activation(rden_t[:], rows3[32:33, :], AF.Exp, scale=-1.0)
            s.drain().then_inc(s_act.h, 1)
            s_act.n += 1

    @block.vector
    def _(v):
        if phases < 4:
            return
        for h in range(H):
            v.wait_ge(s_rd.h, rd_b3b + 48 * (h + 1))
            dbc = bcast_col(dcol[:, h:h + 1], T)
            v.tensor_tensor_scan(S_kv[:], dbc, kv_sb[:],
                                 carry_cols[:, h:h + 1], ALU.mult, ALU.add)
            v.tensor_tensor_scan(S_ks[:], dbc, k_sb[:],
                                 carry_cols[:, H + h:H + h + 1], ALU.mult, ALU.add)
            v.drain().then_inc(s_dve.h, 1)
            v.wait_ge(s_pe.h, pe_b3b + 2 * h + 1)
            v.tensor_scalar(rows3[0:1, :], pBC[0:1, :], EPS_DEN, None, ALU.max)
            v.drain().then_inc(s_dve.h, 1)
            v.wait_ge(s_pe.h, pe_b3b + 2 * h + 2)
            v.tensor_tensor(AB_v[:, h, :], kv_sb[:], pBC[:, :], ALU.mult)
            v.drain().then_inc(s_dve.h, 1)
            s_dve.n += 3

    @block.gpsimd
    def _(g):
        if phases < 4:
            return
        for h in range(H):
            g.wait_ge(s_dve.h, dve_b3b + 3 * h + 1)
            g.tensor_tensor(kv_sb[:], q_sb[:], S_kv[:], ALU.mult)
            g.drain()
            g.tensor_tensor(pbf[:], q_sb[:], S_ks[:], ALU.mult)
            g.drain().then_inc(s_pool.h, 1)
            s_pool.n += 1

    @block.tensor
    def _(t):
        if phases < 4:
            return
        for h in range(H):
            t.wait_ge(s_pool.h, pool_b3b + h + 1)
            if h >= 1:
                t.wait_ge(s_dve.h, dve_b3b + 3 * (h - 1) + 3)
            for tg in range(4):
                mm = t.matmul(pBC[0:1, tg * 512:(tg + 1) * 512], ones_c[:],
                              pbf[:, tg * 512:(tg + 1) * 512], start=True, stop=True)
            mm.then_inc(s_pe.h, 1)
            s_pe.n += 1
            t.wait_ge(s_act.h, act_b3b + h + 1)
            for tg in range(4):
                mm = t.matmul(pBC[:, tg * 512:(tg + 1) * 512], ones_r[:],
                              rden_t[:, tg * 512:(tg + 1) * 512],
                              start=True, stop=True)
            mm.then_inc(s_pe.h, 1)
            s_pe.n += 1
    dve_3b_done = dve_b3b + 3 * H

    S3.close()
    P3.close()
    S3c = ExitStack()
    pW = [S3c.enter_context(nc.psum_tensor(f"pW{i}", [128, 2048], F32)) for i in range(2)]
    xs3 = S3c.enter_context(nc.sbuf_tensor("xs3", [128, DM], F32))
    x2sb = S3c.enter_context(nc.sbuf_tensor("x2sb", [128, DM], F32))
    hf = S3c.enter_context(nc.sbuf_tensor("hf", [128, DM], F32))
    hstg = S3c.enter_context(nc.sbuf_tensor("hstg", [128, NKC * 512], BF16))
    hstg_v = hstg[:].rearrange("p (k t) -> p k t", k=NKC)
    sp_h_v = sp_h.rearrange("(k p) t -> p k t", p=128)

    pe_b3c = s_pe.n
    act_b3c = s_act.n
    dve_b3c = s_dve.n
    rd_b3c = s_rd.n

    @block.scalar
    def _(s):
        if phases < 5:
            return
        for tc in range(NTC):
            if tc == 0:
                s.wait_ge(s_dve.h, dve_3b_done)
            else:
                s.wait_ge(s_dve.h, dve_b3c + 6 * (tc - 1) + 3)
            s.dma_start(out=xs3[:], in_=x_d[tc * 128:(tc + 1) * 128, :]) \
                .then_inc(s_rd.h, 16)
            s_rd.n += 16
            s.wait_ge(s_dve.h, dve_b3c + 6 * tc + 1)
            s.activation(stat[:, 4:5], stat[:, 3:4], AF.Sqrt, bias=epsl[:])
            s.drain().then_inc(s_act.h, 1)
            s.wait_ge(s_dve.h, dve_b3c + 6 * tc + 4)
            s.activation(stat[:, 4:5], stat[:, 3:4], AF.Sqrt, bias=epsl[:])
            s.drain().then_inc(s_act.h, 1)
            s.wait_ge(s_dve.h, dve_b3c + 6 * tc + 5)
            s.activation(hf[:], x2sb[:], AF.Identity,
                         bias=stat[:, 6:7], scale=stat[:, 5:6])
            s.drain().then_inc(s_act.h, 1)
            s_act.n += 3

    @block.vector
    def _(v):
        if phases < 5:
            return
        for tc in range(NTC):
            v.wait_ge(s_pe.h, pe_b3c + 2 * tc + 1)
            ln_stats(v, pW[tc % 2][:])
            v.drain().then_inc(s_dve.h, 1)
            v.wait_ge(s_act.h, act_b3c + 3 * tc + 1)
            ln_finish_dve(v)
            v.drain().then_inc(s_dve.h, 1)
            v.wait_ge(s_rd.h, rd_b3c + 16 * (tc + 1))
            if tc >= 1:
                v.wait_ge(s_x2w.h, 16 * tc)
            v.tensor_scalar(xs3[:], xs3[:], stat[:, 6:7], None, ALU.add)
            v.drain()
            v.scalar_tensor_tensor(x2sb[:], pW[tc % 2][:], stat[:, 5:6],
                                   xs3[:], ALU.mult, ALU.add)
            v.drain().then_inc(s_dve.h, 1)
            ln_stats(v, x2sb[:])
            v.drain().then_inc(s_dve.h, 1)
            v.wait_ge(s_act.h, act_b3c + 3 * tc + 2)
            ln_finish_dve(v)
            v.drain().then_inc(s_dve.h, 1)
            v.wait_ge(s_pe.h, pe_b3c + 2 * tc + 2)
            if tc % 4 == 0 and tc >= 4:
                v.wait_ge(s_hfl.h, 16 * (tc // 4))
            src = pW[tc % 2][:].rearrange("p (k t) -> p k t", k=NKC)
            v.tensor_copy(hstg_v[:, :, (tc % 4) * 128:(tc % 4) * 128 + 128], src)
            v.drain().then_inc(s_dve.h, 1)
            s_dve.n += 6

    @block.tensor
    def _(t):
        if phases < 5:
            return
        t.wait_ge(s_wos.h, 16)
        t.wait_ge(s_dve.h, dve_3b_done)
        for tc in range(NTC):
            if tc >= 2:
                t.wait_ge(s_dve.h, dve_b3c + 6 * (tc - 2) + 6)
            for hk in range(NKC):
                for tg in range(4):
                    mm = t.matmul(pW[tc % 2][:, tg * 512:(tg + 1) * 512],
                                  AB_v[:, hk, tc * 128:(tc + 1) * 128],
                                  wo_v[:, hk, tg * 512:(tg + 1) * 512],
                                  start=(hk == 0), stop=(hk == NKC - 1))
            mm.then_inc(s_pe.h, 1)
            s_pe.n += 1
            t.wait_ge(s_act.h, act_b3c + 3 * tc + 3)
            t.wait_ge(s_dve.h, dve_b3c + 6 * tc + 3)
            for k in range(NKC):
                mm = t.transpose(pW[tc % 2][:, k * 128:(k + 1) * 128],
                                 hf[:, k * 128:(k + 1) * 128], idf[:])
            mm.then_inc(s_pe.h, 1)
            s_pe.n += 1

    @block.sync
    def _(sync):
        if phases < 5:
            return
        for tc in range(NTC):
            sync.wait_ge(s_dve.h, dve_b3c + 6 * tc + 3)
            sync.dma_start(out=sp_x2[tc * 128:(tc + 1) * 128, :], in_=x2sb[:]) \
                .then_inc(s_x2w.h, 16)
            s_x2w.n += 16
            if tc % 4 == 3:
                sync.wait_ge(s_dve.h, dve_b3c + 6 * tc + 6)
                grp = tc // 4
                sync.dma_start(out=sp_h_v[:, :, grp * 512:(grp + 1) * 512],
                               in_=hstg_v[:, :, :]).then_inc(s_hfl.h, 16)
                s_hfl.n += 16
    dve_3c_done = dve_b3c + 6 * NTC

    S3c.close()
    S3w.close()
    S5 = ExitStack()
    pGU = [S5.enter_context(nc.psum_tensor(f"pGU{i}", [128, 2048], F32)) for i in range(2)]
    hh = S5.enter_context(nc.sbuf_tensor("hh", [128, NKC * 1024], BF16))
    dacc = S5.enter_context(nc.sbuf_tensor("dacc", [128, 8 * 2048], F32))
    gtd = S5.enter_context(nc.sbuf_tensor("gtd", [128, GFF * 1024], BF16))
    wgs = [S5.enter_context(nc.sbuf_tensor(f"wgs{i}", [128, 2048], BF16)) for i in range(2)]
    wus = [S5.enter_context(nc.sbuf_tensor(f"wus{i}", [128, 2048], BF16)) for i in range(2)]
    wdg = S5.enter_context(nc.sbuf_tensor("wdg", [128, GFF * 2048], BF16))
    sil = S5.enter_context(nc.sbuf_tensor("sil", [128, 1024], F32))
    xf = S5.enter_context(nc.sbuf_tensor("xf", [128, DM], F32))
    outst = S5.enter_context(nc.sbuf_tensor("outst", [128, DM], F32))
    hh_v = hh[:].rearrange("p (k t) -> p k t", k=NKC)
    gtd_v = gtd[:].rearrange("p (m t) -> p m t", m=GFF)
    wdg_v = wdg[:].rearrange("p (m n) -> p m n", m=GFF)

    pe_b5 = s_pe.n
    act_b5 = s_act.n
    dve_b5 = s_dve.n
    HALF_DVE = NGRP * (GFF + 8) + 8 * 3
    HALF_ACT = NGRP * GFF + 8
    HALF_PE = NGRP * (GFF + 8)

    def dve_gated(half, g, mi):
        return dve_b5 + half * HALF_DVE + g * (GFF + 8) + mi + 1

    def dve_wadd(half, g, w):
        return dve_b5 + half * HALF_DVE + g * (GFF + 8) + GFF + w + 1

    def dve_ln4(half, w, i):
        return dve_b5 + half * HALF_DVE + NGRP * (GFF + 8) + w * 3 + i + 1

    def act_sil(half, g, mi):
        return act_b5 + half * HALF_ACT + g * GFF + mi + 1

    def act_ln4s(half, w):
        return act_b5 + half * HALF_ACT + NGRP * GFF + w + 1

    def pe_gu(half, g, mi):
        return pe_b5 + half * HALF_PE + g * (GFF + 8) + mi + 1

    def pe_dw(half, g, w):
        return pe_b5 + half * HALF_PE + g * (GFF + 8) + GFF + w + 1

    def cst_w5(half, g, mi):
        return 48 * (half * (NM // 2) + (g * GFF + mi) // 2 + 1)

    @block.sync
    def _(sync):
        if phases < 6:
            return
        for half in range(2):
            for w in range(8):
                tcg = half * 8 + w
                if tcg == 0:
                    sync.wait_ge(s_dve.h, dve_3c_done)
                    sync.wait_ge(s_hfl.h, 16 * (NTC // 4))
                    sync.wait_ge(s_x2w.h, 16 * NTC)
                else:
                    pv = dve_ln4(half, w - 1, 2) if w >= 1 else dve_ln4(0, 7, 2)
                    sync.wait_ge(s_dve.h, pv)
                sync.dma_start(out=xf[:], in_=sp_x2[tcg * 128:(tcg + 1) * 128, :]) \
                    .then_inc(s_gat.h, 16)
                s_gat.n += 16
                sync.wait_ge(s_dve.h, dve_ln4(half, w, 2))
                sync.dma_start(out=out_d[tcg * 128:(tcg + 1) * 128, :], in_=outst[:]) \
                    .then_inc(s_out.h, 16)
                s_out.n += 16
        sync.wait_ge(s_out.h, s_out.n)

    @block.gpsimd
    def _(g_):
        if phases < 6:
            return
        for half in range(2):
            for g in range(NGRP):
                for mi in range(GFF):
                    m = g * GFF + mi
                    sl = mi % 2
                    gi = g * GFF + mi
                    if half == 0 and gi == 0:
                        g_.wait_ge(s_dve.h, dve_3c_done)
                        g_.wait_ge(s_x2w.h, 16 * NTC)
                        g_.wait_ge(s_hfl.h, 64)
                    if half > 0 or gi >= 2:
                        if gi >= 2:
                            pg, pmi = divmod(gi - 2, GFF)
                            g_.wait_ge(s_dve.h, dve_gated(half, pg, pmi))
                        else:
                            g_.wait_ge(s_dve.h, dve_gated(half - 1, NGRP - 1, GFF - 2 + gi))
                    g_.dma_start(out=wgs[sl][:], in_=wg_d[m], max_dma_last_dim=8192) \
                        .then_inc(s_w5[sl].h, 16)
                    g_.dma_start(out=wus[sl][:], in_=wu_d[m], max_dma_last_dim=8192) \
                        .then_inc(s_w5[sl].h, 16)
                    if mi == 0 and (g >= 1 or half >= 1):
                        if g >= 1:
                            g_.wait_ge(s_pe.h, pe_dw(half, g - 1, 7))
                        else:
                            g_.wait_ge(s_pe.h, pe_dw(half - 1, NGRP - 1, 7))
                    g_.dma_start(out=wdg_v[:, mi, :], in_=wd_d[m],
                                 max_dma_last_dim=8192).then_inc(s_w5[sl].h, 16)

    @block.scalar
    def _(s):
        if phases < 6:
            return
        for half in range(2):
            if half == 0:
                s.wait_ge(s_hfl.h, 64)
            else:
                s.wait_ge(s_pe.h, pe_gu(0, NGRP - 1, GFF - 1))
            s.dma_start(out=hh[:], in_=sp_h_v[:, :, half * 1024:(half + 1) * 1024]) \
                .then_inc(s_hh.h, 16)
            s_hh.n += 16
            for g in range(NGRP):
                for mi in range(GFF):
                    s.wait_ge(s_pe.h, pe_gu(half, g, mi))
                    gi = g * GFF + mi
                    if half > 0 or gi >= 1:
                        if gi >= 1:
                            pg, pmi = divmod(gi - 1, GFF)
                            s.wait_ge(s_dve.h, dve_gated(half, pg, pmi))
                        else:
                            s.wait_ge(s_dve.h, dve_gated(half - 1, NGRP - 1, GFF - 1))
                    s.activation(sil[:], pGU[mi % 2][:, 0:1024], AF.Silu)
                    s.drain().then_inc(s_act.h, 1)
                    s_act.n += 1
            for w in range(8):
                s.wait_ge(s_dve.h, dve_ln4(half, w, 0))
                s.activation(stat[:, 4:5], stat[:, 3:4], AF.Sqrt, bias=epsl[:])
                s.drain().then_inc(s_act.h, 1)
                s_act.n += 1

    @block.tensor
    def _(t):
        if phases < 6:
            return
        for half in range(2):
            t.wait_ge(s_hh.h, 16 * (half + 1))
            if half == 0:
                t.wait_ge(s_dve.h, dve_3c_done)
            for g in range(NGRP):
                for mi in range(GFF):
                    gi = g * GFF + mi
                    t.wait_ge(s_w5[mi % 2].h, cst_w5(half, g, mi))
                    if gi >= 2:
                        pg, pmi = divmod(gi - 2, GFF)
                        t.wait_ge(s_dve.h, dve_gated(half, pg, pmi))
                        if mi < 2 and g >= 1:
                            t.wait_ge(s_dve.h, dve_wadd(half, g - 1, 7))
                    elif half == 1:
                        t.wait_ge(s_dve.h, dve_wadd(0, NGRP - 1, 7))
                    wgv = wgs[mi % 2][:].rearrange("p (k n) -> p k n", k=NKC)
                    wuv = wus[mi % 2][:].rearrange("p (k n) -> p k n", k=NKC)
                    for k in range(NKC):
                        for tg in range(2):
                            mm = t.matmul(pGU[mi % 2][:, tg * 512:(tg + 1) * 512],
                                          wgv[:, k, :],
                                          hh_v[:, k, tg * 512:(tg + 1) * 512],
                                          start=(k == 0), stop=(k == NKC - 1))
                        for tg in range(2):
                            mm = t.matmul(pGU[mi % 2][:, 1024 + tg * 512:1024 + (tg + 1) * 512],
                                          wuv[:, k, :],
                                          hh_v[:, k, tg * 512:(tg + 1) * 512],
                                          start=(k == 0), stop=(k == NKC - 1))
                    mm.then_inc(s_pe.h, 1)
                    s_pe.n += 1
                for w in range(8):
                    t.wait_ge(s_dve.h, dve_gated(half, g, GFF - 1))
                    if w >= 2:
                        t.wait_ge(s_dve.h, dve_wadd(half, g, w - 2))
                    elif g >= 1:
                        t.wait_ge(s_dve.h, dve_wadd(half, g - 1, 7))
                    elif half == 1:
                        t.wait_ge(s_dve.h, dve_wadd(0, NGRP - 1, 7))
                    for mi in range(GFF):
                        for ng in range(4):
                            mm = t.matmul(pGU[w % 2][:, ng * 512:(ng + 1) * 512],
                                          gtd_v[:, mi, w * 128:(w + 1) * 128],
                                          wdg_v[:, mi, ng * 512:(ng + 1) * 512],
                                          start=(mi == 0), stop=(mi == GFF - 1))
                    mm.then_inc(s_pe.h, 1)
                    s_pe.n += 1

    @block.vector
    def _(v):
        if phases < 6:
            return
        for half in range(2):
            for g in range(NGRP):
                for mi in range(GFF):
                    v.wait_ge(s_act.h, act_sil(half, g, mi))
                    v.tensor_tensor(gtd_v[:, mi, :], sil[:],
                                    pGU[mi % 2][:, 1024:2048], ALU.mult)
                    v.drain().then_inc(s_dve.h, 1)
                    s_dve.n += 1
                for w in range(8):
                    v.wait_ge(s_pe.h, pe_dw(half, g, w))
                    if g == 0:
                        v.tensor_copy(dacc[:, w * 2048:(w + 1) * 2048], pGU[w % 2][:])
                    else:
                        v.tensor_tensor(dacc[:, w * 2048:(w + 1) * 2048],
                                        dacc[:, w * 2048:(w + 1) * 2048],
                                        pGU[w % 2][:], ALU.add)
                    v.drain().then_inc(s_dve.h, 1)
                    s_dve.n += 1
            for w in range(8):
                ln_stats(v, dacc[:, w * 2048:(w + 1) * 2048])
                v.drain().then_inc(s_dve.h, 1)
                v.wait_ge(s_act.h, act_ln4s(half, w))
                ln_finish_dve(v)
                v.drain().then_inc(s_dve.h, 1)
                v.wait_ge(s_gat.h, 16 + 16 * (half * 8 + w + 1))
                v.wait_ge(s_out.h, 16 * (half * 8 + w))
                v.tensor_scalar(xf[:], xf[:], stat[:, 6:7], None, ALU.add)
                v.drain()
                v.scalar_tensor_tensor(outst[:], dacc[:, w * 2048:(w + 1) * 2048],
                                       stat[:, 5:6], xf[:], ALU.mult, ALU.add)
                v.drain().then_inc(s_dve.h, 1)
                s_dve.n += 3

    if debug:
        @block.sync
        def _(sync):
            if 1 <= phases <= 2:
                sync.wait_ge(s_dve.h, dve_xnf)
                sync.dma_start(out=dbg_xnf, in_=xnf[:]).then_inc(s_out.h, 16)
                s_out.n += 16
            if phases >= 3:
                sync.wait_ge(s_dve.h, dve_car)
                sync.dma_start(out=dbg_carry, in_=carry_cols[:]).then_inc(s_out.h, 16)
                s_out.n += 16
            if 4 <= phases <= 5:
                sync.wait_ge(s_dve.h, dve_3b_done)
                sync.dma_start(out=dbg_ab, in_=AB[:]).then_inc(s_out.h, 16)
                s_out.n += 16
            sync.wait_ge(s_sp.h, s_sp.n)
            sync.wait_ge(s_x2w.h, s_x2w.n)
            sync.wait_ge(s_hfl.h, s_hfl.n)
            sync.wait_ge(s_out.h, s_out.n)

    S5.close()
    ES.close()
    return nc


def _prep_inputs(L0_attn_x, L0_attn_w_qkv, L0_attn_w_out, L0_attn_decay_param,
                 L0_ffn_w_gate, L0_ffn_w_up, L0_ffn_w_down):
    x = np.ascontiguousarray(np.asarray(L0_attn_x, dtype=np.float32).reshape(B * TSEQ, DM))
    w_qkv = np.asarray(L0_attn_w_qkv, dtype=np.float32)
    w_out = np.asarray(L0_attn_w_out, dtype=np.float32)
    decay = np.ascontiguousarray(np.asarray(L0_attn_decay_param, dtype=np.float32).reshape(1, H))
    w_gate = np.asarray(L0_ffn_w_gate, dtype=np.float32)
    w_up = np.asarray(L0_ffn_w_up, dtype=np.float32)
    w_down = np.asarray(L0_ffn_w_down, dtype=np.float32)

    wq_l = np.ascontiguousarray(
        w_qkv.reshape(NKC, 128, 3, 2, 8, 128).transpose(2, 3, 1, 0, 4, 5)
        .reshape(3, 2, 128, 16384))
    wo_l = np.ascontiguousarray(
        w_out.reshape(NKC, 128, DM).transpose(1, 0, 2).reshape(128, NKC * DM))
    wg_l = np.ascontiguousarray(
        w_gate.T.reshape(NKC, 128, NM, 128).transpose(2, 1, 0, 3)
        .reshape(NM, 128, NKC * 128))
    wu_l = np.ascontiguousarray(
        w_up.T.reshape(NKC, 128, NM, 128).transpose(2, 1, 0, 3)
        .reshape(NM, 128, NKC * 128))
    wd_l = np.ascontiguousarray(w_down.T.reshape(NM, 128, DM))

    ident = np.eye(128, dtype=np.float32)
    in_maps = []
    for c in range(NCORES):
        sel = np.zeros((128, NCORES * 128), dtype=np.float32)
        if c % 2 == 1:
            sel[:, (c - 1) * 128:c * 128] = ident
        in_maps.append({
            "x": np.ascontiguousarray(x[c * T:(c + 1) * T, :]),
            "w_qkv": wq_l, "w_out": wo_l, "w_gT": wg_l, "w_uT": wu_l,
            "w_dT": wd_l, "decay": decay, "sel": sel, "ident": ident,
        })
    return in_maps


def kernel(**inputs):
    in_maps = _prep_inputs(**inputs)
    nc = build_program()
    res = run_bass_kernel_spmd(nc, in_maps, core_ids=list(range(NCORES)))
    out = np.concatenate([np.asarray(res.results[c]["out"]) for c in range(NCORES)],
                         axis=0)
    return out.reshape(B, TSEQ, DM).astype(np.float32)


def run_dev(inputs, phases=9, debug=True):
    in_maps = _prep_inputs(**inputs)
    nc = build_program(debug=debug, phases=phases)
    res = run_bass_kernel_spmd(nc, in_maps, core_ids=list(range(NCORES)))
    return res.results


# version 05
# speedup vs baseline: 1.1421x; 1.0471x over previous
import numpy as np
import ml_dtypes
from contextlib import ExitStack

import concourse.bass as bass
import concourse.mybir as mybir
from concourse.bass_utils import run_bass_kernel_spmd

F32 = mybir.dt.float32
F32R = mybir.dt.float32r
BF16 = mybir.dt.bfloat16
AF = mybir.ActivationFunctionType
ALU = mybir.AluOpType
AX = mybir.AxisListType

NCORES = 8
B, TSEQ, DM = 4, 4096, 2048
T = 2048
H = 16
NKC = 16
NTC = 16
HID = 8192
NM = 64
GFF = 8
NGRP = NM // GFF
EPS_LN = 1e-5
EPS_DEN = 1e-6


class Sem:
    def __init__(self, es, nc, name):
        self.h = es.enter_context(nc.semaphore(name))
        self.n = 0


def bcast_col(ap_col, n):
    row = ap_col.ap[0][0]
    return bass.AP(ap_col.tensor, ap_col.offset, [[row, 128], [0, n]])


def build_program(debug=False, phases=9):
    nc = bass.Bass("TRN2", target_bir_lowering=False)

    x_d = nc.dram_tensor("x", [T, DM], F32, kind="ExternalInput").ap()
    wqkv_d = nc.dram_tensor("w_qkv", [3, 2, 128, 16384], F32, kind="ExternalInput").ap()
    wout_d = nc.dram_tensor("w_out", [128, NKC * DM], F32, kind="ExternalInput").ap()
    wg_d = nc.dram_tensor("w_gT", [NM, 128, NKC * 128], F32, kind="ExternalInput").ap()
    wu_d = nc.dram_tensor("w_uT", [NM, 128, NKC * 128], F32, kind="ExternalInput").ap()
    wd_d = nc.dram_tensor("w_dT", [NM, 128, DM], F32, kind="ExternalInput").ap()
    dec_d = nc.dram_tensor("decay", [1, H], F32, kind="ExternalInput").ap()
    sel_d = nc.dram_tensor("sel", [128, NCORES * 128], F32, kind="ExternalInput").ap()
    id_d = nc.dram_tensor("ident", [128, 128], F32, kind="ExternalInput").ap()
    out_d = nc.dram_tensor("out", [T, DM], F32, kind="ExternalOutput").ap()

    spk = "ExternalOutput" if debug else "Internal"
    sp_q = nc.dram_tensor("sp_q", [DM, T], F32, kind=spk).ap()
    sp_k = nc.dram_tensor("sp_k", [DM, T], F32, kind=spk).ap()
    sp_kv = nc.dram_tensor("sp_kv", [DM, T], F32, kind=spk).ap()
    sp_x2 = nc.dram_tensor("sp_x2", [T, DM], F32, kind=spk).ap()
    sp_h = nc.dram_tensor("sp_h", [DM, T], BF16, kind=spk).ap()
    if debug:
        dbg_xnf = nc.dram_tensor("dbg_xnf", [128, NKC * 2048], BF16, kind="ExternalOutput").ap()
        dbg_ab = nc.dram_tensor("dbg_ab", [128, NKC * 2048], BF16, kind="ExternalOutput").ap()
        dbg_carry = nc.dram_tensor("dbg_carry", [128, 32], F32, kind="ExternalOutput").ap()
    cc_in_t = nc.dram_tensor("cc_in", [128, 32], F32)
    cc_out_t = nc.dram_tensor("cc_out", [NCORES * 128, 32], F32)
    cc_in = cc_in_t.ap()
    cc_out = cc_out_t.ap()

    ES = ExitStack()

    idb = ES.enter_context(nc.sbuf_tensor("idb", [128, 128], BF16))
    idf = ES.enter_context(nc.sbuf_tensor("idf", [128, 128], F32))
    dsb = ES.enter_context(nc.sbuf_tensor("dsb", [1, H], F32))
    dcol = ES.enter_context(nc.sbuf_tensor("dcol", [128, H], F32))
    sel_sb = ES.enter_context(nc.sbuf_tensor("sel_sb", [128, NCORES * 128], F32))
    gat = ES.enter_context(nc.sbuf_tensor("gat", [128, NCORES * 32], F32))
    ones_c = ES.enter_context(nc.sbuf_tensor("ones_c", [128, 1], BF16))
    ones_r = ES.enter_context(nc.sbuf_tensor("ones_r", [1, 128], F32R))
    ones_f = ES.enter_context(nc.sbuf_tensor("ones_f", [1, 128], F32))
    carry_acc = ES.enter_context(nc.sbuf_tensor("carry_acc", [128, 32], F32))
    carry_cols = ES.enter_context(nc.sbuf_tensor("carry_cols", [128, 32], F32))
    stat = ES.enter_context(nc.sbuf_tensor("stat", [128, 8], F32))
    bnb = ES.enter_context(nc.sbuf_tensor("bnb", [128, 24], F32))
    epsl = ES.enter_context(nc.sbuf_tensor("epsl", [128, 1], F32))

    sem = lambda name: Sem(ES, nc, name)
    s_ld = sem("s_ld")
    s_idb = sem("s_idb")
    s_wq2 = [sem("s_wq20"), sem("s_wq21")]
    s_wos = sem("s_wos")
    s_x = [sem("s_x0")]
    s_sp = sem("s_sp")
    s_rd = sem("s_rd")
    s_ccw = sem("s_ccw")
    s_ccd = sem("s_ccd")
    s_gat = sem("s_gat")
    s_x2w = sem("s_x2w")
    s_hfl = sem("s_hfl")
    s_hh = sem("s_hh")
    s_out = sem("s_out")
    s_w5 = [sem("s_w50"), sem("s_w51")]
    s_pe = sem("s_pe")
    s_dve = sem("s_dve")
    s_act = sem("s_act")
    s_pool = sem("s_pool")

    block = ES.enter_context(nc.Block())

    def ln_stats(v, src_v, n=2048):
        for i in range(n // 512):
            v.bn_stats(bnb[:, 6 * i:6 * i + 6], src_v[:, i * 512:(i + 1) * 512])
        v.drain()
        v.bn_aggr(stat[:, 2:4], bnb[:, 0:6 * (n // 512)])
        v.drain()

    def ln_finish_dve(v):
        v.reciprocal(stat[:, 5:6], stat[:, 4:5])
        v.drain()
        v.scalar_tensor_tensor(stat[:, 6:7], stat[:, 2:3], -1.0,
                               stat[:, 5:6], ALU.mult, ALU.mult)
        v.drain()

    S12 = ExitStack()
    xnf = S12.enter_context(nc.sbuf_tensor("xnf", [128, NKC * 2048], BF16))
    wq2 = [S12.enter_context(nc.sbuf_tensor(f"wq2_{i}", [128, 16384], BF16)) for i in range(2)]
    stg = [S12.enter_context(nc.sbuf_tensor(f"stg{i}", [128, 2048], F32)) for i in range(2)]
    e_st = S12.enter_context(nc.sbuf_tensor("e_st", [128, 2048], F32))
    r_st = S12.enter_context(nc.sbuf_tensor("r_st", [128, 2048], F32))
    kre = S12.enter_context(nc.sbuf_tensor("kre", [128, 2048], F32))
    S_scr = S12.enter_context(nc.sbuf_tensor("S_scr", [128, 2048], F32))
    S1 = ExitStack()
    xs = S1.enter_context(nc.sbuf_tensor("xs", [128, DM], F32))
    xn_tm = [S1.enter_context(nc.sbuf_tensor(f"xn_tm{i}", [128, DM], BF16)) for i in range(2)]

    P01 = ExitStack()
    pS = P01.enter_context(nc.psum_tensor("pS", [128, 512], F32))
    pT = [P01.enter_context(nc.psum_tensor(f"pT{i}", [128, 2048], BF16)) for i in range(2)]

    xnf_v = xnf[:].rearrange("p (k t) -> p k t", k=NKC)
    wq2_v = [w[:].rearrange("p (k m c) -> p k m c", k=NKC, m=8) for w in wq2]

    @block.sync
    def _(sync):
        sync.dma_start(out=idf[:], in_=id_d[:]).then_inc(s_ld.h, 16)
        sync.dma_start(out=dsb[:], in_=dec_d[:]).then_inc(s_ld.h, 16)
        sync.dma_start(out=sel_sb[:], in_=sel_d[:]).then_inc(s_ld.h, 16)
    s_ld.n = 48

    @block.gpsimd
    def _(g):
        g.dma_start(out=idb[:], in_=id_d[:]).then_inc(s_idb.h, 16)
    s_idb.n = 16

    @block.vector
    def _(v):
        v.memset(ones_c[:], 1.0)
        v.memset(ones_f[:], 1.0)
        v.memset(carry_acc[:], 0.0)
        v.memset(epsl[:], EPS_LN)
        v.drain().then_inc(s_dve.h, 1)
    s_dve.n += 1
    dve_consts = s_dve.n

    @block.scalar
    def _(s):
        s.wait_ge(s_ld.h, 48)
        s.activation(dsb[:], dsb[:], AF.Sigmoid)
        s.wait_ge(s_dve.h, dve_consts)
        s.activation(ones_r[:], ones_f[:], AF.Identity)
        s.drain().then_inc(s_act.h, 1)
    s_act.n += 1
    act_sig = s_act.n

    @block.tensor
    def _(t):
        t.wait_ge(s_act.h, act_sig)
        t.matmul(pS[:, 0:H], ones_f[:], dsb[:], start=True, stop=True).then_inc(s_pe.h, 1)
    s_pe.n += 1
    pe_dcol = s_pe.n

    @block.vector
    def _(v):
        v.wait_ge(s_pe.h, pe_dcol)
        v.tensor_copy(dcol[:], pS[:, 0:H])
        v.drain().then_inc(s_dve.h, 1)
    s_dve.n += 1

    act_b1 = s_act.n
    dve_b1 = s_dve.n
    pe_b1 = s_pe.n

    @block.sync
    def _(sync):
        if phases < 1:
            return
        for tc in range(NTC):
            if tc >= 1:
                sync.wait_ge(s_act.h, act_b1 + 2 * (tc - 1) + 2)
            sync.dma_start(out=xs[:], in_=x_d[tc * 128:(tc + 1) * 128, :]) \
                .then_inc(s_x[0].h, 16)
            s_x[0].n += 16

    @block.scalar
    def _(s):
        if phases < 1:
            return
        for tc in range(NTC):
            s.wait_ge(s_dve.h, dve_b1 + 3 * tc + 1)
            s.activation(stat[:, 4:5], stat[:, 3:4], AF.Sqrt, bias=epsl[:])
            s.drain().then_inc(s_act.h, 1)
            s.wait_ge(s_dve.h, dve_b1 + 3 * tc + 2)
            if tc >= 2:
                s.wait_ge(s_pe.h, pe_b1 + tc - 1)
            s.activation(xn_tm[tc % 2][:], xs[:], AF.Identity,
                         bias=stat[:, 6:7], scale=stat[:, 5:6])
            s.drain().then_inc(s_act.h, 1)
            s_act.n += 2

    @block.vector
    def _(v):
        if phases < 1:
            return
        for tc in range(NTC):
            v.wait_ge(s_x[0].h, 16 * (tc + 1))
            ln_stats(v, xs[:])
            v.drain().then_inc(s_dve.h, 1)
            v.wait_ge(s_act.h, act_b1 + 2 * tc + 1)
            ln_finish_dve(v)
            v.drain().then_inc(s_dve.h, 1)
            v.wait_ge(s_pe.h, pe_b1 + tc + 1)
            src = pT[tc % 2][:].rearrange("p (k t) -> p k t", k=NKC)
            v.tensor_copy(xnf_v[:, :, tc * 128:(tc + 1) * 128], src)
            v.drain().then_inc(s_dve.h, 1)
            s_dve.n += 3

    @block.tensor
    def _(t):
        if phases < 1:
            return
        t.wait_ge(s_idb.h, 16)
        for tc in range(NTC):
            t.wait_ge(s_act.h, act_b1 + 2 * tc + 2)
            if tc >= 2:
                t.wait_ge(s_dve.h, dve_b1 + 3 * (tc - 2) + 3)
            for k in range(NKC):
                mm = t.transpose(pT[tc % 2][:, k * 128:(k + 1) * 128],
                                 xn_tm[tc % 2][:, k * 128:(k + 1) * 128], idb[:])
            mm.then_inc(s_pe.h, 1)
            s_pe.n += 1
    dve_xnf = dve_b1 + 3 * NTC

    P01.close()
    S1.close()
    P2 = ExitStack()
    pQ = [P2.enter_context(nc.psum_tensor(f"pQ{i}", [128, 2048], F32)) for i in range(2)]

    QUADS = [1, 2, 0]
    pe_b2 = s_pe.n
    act_b2 = s_act.n
    pool_b2 = s_pool.n

    def mmdone2(g):
        return pe_b2 + g + 1

    def v_tt(m):
        return dve_xnf + H + 2 * m + 1

    def v_scan(m):
        return dve_xnf + H + 2 * m + 2

    def psum2_free(g):
        qi, m = divmod(g, H)
        if qi == 0:
            return (s_act, act_b2 + m + 1)
        if qi == 1:
            return (s_dve, v_tt(m))
        return (s_act, act_b2 + H + m + 1)

    def stgdone2(g):
        qi, m = divmod(g, H)
        if qi == 0:
            return (s_dve, dve_xnf + m + 1)
        if qi == 1:
            return (s_dve, v_tt(m))
        return (s_pool, pool_b2 + m + 1)

    def stream_idx(g):
        return 2 * (g // H) + (g % H) // 8

    @block.gpsimd
    def _(g_):
        if phases < 2:
            return
        for idx in range(6):
            qi, mh = divmod(idx, 2)
            if idx >= 2:
                g_.wait_ge(s_pe.h, mmdone2(8 * (idx - 2) + 7))
            g_.dma_start(out=wq2[idx % 2][:], in_=wqkv_d[QUADS[qi], mh],
                         max_dma_last_dim=8192).then_inc(s_wq2[idx % 2].h, 16)
            s_wq2[idx % 2].n += 16

    @block.tensor
    def _(t):
        if phases < 2:
            return
        for qi in range(3):
            for m in range(H):
                g = qi * H + m
                idx = stream_idx(g)
                if m % 8 == 0:
                    t.wait_ge(s_wq2[idx % 2].h, 16 * (idx // 2 + 1))
                if g == 0:
                    t.wait_ge(s_dve.h, dve_xnf)
                if g >= 2:
                    fs, fv = psum2_free(g - 2)
                    t.wait_ge(fs.h, fv)
                wv = wq2_v[idx % 2]
                for k in range(NKC):
                    for tg in range(4):
                        mm = t.matmul(pQ[g % 2][:, tg * 512:(tg + 1) * 512],
                                      wv[:, k, m % 8, :],
                                      xnf_v[:, k, tg * 512:(tg + 1) * 512],
                                      start=(k == 0), stop=(k == NKC - 1))
                mm.then_inc(s_pe.h, 1)
                s_pe.n += 1

    @block.scalar
    def _(s):
        if phases < 2:
            return
        for m in range(H):
            s.wait_ge(s_pe.h, mmdone2(m))
            if m >= 1:
                s.wait_ge(s_dve.h, dve_xnf + m)
            s.activation(e_st[:], pQ[m % 2][:], AF.Exp)
            s.activation(r_st[:], pQ[m % 2][:], AF.Relu)
            s.drain().then_inc(s_act.h, 1)
            s_act.n += 1

    @block.scalar
    def _(s):
        if phases < 2:
            return
        for m in range(H):
            s.wait_ge(s_sp.h, 16 * H)
            if m >= 1:
                s.wait_ge(s_dve.h, v_scan(m - 1))
            s.dma_start(out=kre[:], in_=sp_k[m * 128:(m + 1) * 128, :]) \
                .then_inc(s_rd.h, 16)
            s_rd.n += 16

    @block.scalar
    def _(s):
        if phases < 2:
            return
        for m in range(H):
            g = 2 * H + m
            s.wait_ge(s_pe.h, mmdone2(g))
            if m == 0:
                s.wait_ge(s_dve.h, v_scan(H - 1))
            else:
                s.wait_ge(s_pool.h, pool_b2 + m)
            s.activation(e_st[:], pQ[g % 2][:], AF.Exp)
            s.activation(r_st[:], pQ[g % 2][:], AF.Relu)
            s.drain().then_inc(s_act.h, 1)
            s_act.n += 1

    @block.vector
    def _(v):
        if phases < 2:
            return
        for m in range(H):
            v.wait_ge(s_act.h, act_b2 + m + 1)
            if m >= 2:
                v.wait_ge(s_sp.h, 16 * m)
            v.scalar_tensor_tensor(stg[m % 2][:], e_st[:], 1.0,
                                   r_st[:], ALU.min, ALU.add)
            v.drain().then_inc(s_dve.h, 1)
            s_dve.n += 1
        for m in range(H):
            g = H + m
            v.wait_ge(s_pe.h, mmdone2(g))
            v.wait_ge(s_rd.h, 16 * (m + 1))
            v.wait_ge(s_sp.h, 16 * (H + m))
            if m >= 2:
                v.wait_ge(s_dve.h, v_scan(m - 2))
            v.tensor_tensor(stg[m % 2][:], pQ[g % 2][:], kre[:], ALU.mult)
            v.drain().then_inc(s_dve.h, 1)
            dbc = bcast_col(dcol[:, m:m + 1], T)
            v.tensor_tensor_scan(S_scr[:], dbc, stg[m % 2][:], 0.0, ALU.mult, ALU.add)
            v.drain()
            v.tensor_copy(carry_acc[:, m:m + 1], S_scr[:, T - 1:T])
            v.drain()
            v.tensor_tensor_scan(S_scr[:], dbc, kre[:], 0.0, ALU.mult, ALU.add)
            v.drain()
            v.tensor_copy(carry_acc[:, H + m:H + m + 1], S_scr[:, T - 1:T])
            v.drain().then_inc(s_dve.h, 1)
            s_dve.n += 2

    @block.gpsimd
    def _(g_):
        if phases < 2:
            return
        for m in range(H):
            g_.wait_ge(s_act.h, act_b2 + H + m + 1)
            g_.wait_ge(s_sp.h, 16 * (2 * H + m))
            g_.tensor_scalar(e_st[:], e_st[:], 1.0, None, ALU.min)
            g_.drain()
            g_.tensor_tensor(stg[m % 2][:], e_st[:], r_st[:], ALU.add)
            g_.drain().then_inc(s_pool.h, 1)
            s_pool.n += 1
    pool_all2 = s_pool.n

    spill_of = {}

    @block.sync
    def _(sync):
        if phases < 2:
            return
        targets = [sp_k, sp_kv, sp_q]
        for qi in range(3):
            for m in range(H):
                g = qi * H + m
                fs, fv = stgdone2(g)
                sync.wait_ge(fs.h, fv)
                sync.dma_start(out=targets[qi][m * 128:(m + 1) * 128, :],
                               in_=stg[m % 2][:]).then_inc(s_sp.h, 16)
                s_sp.n += 16
                spill_of[g] = s_sp.n
    sp_all2 = s_sp.n

    @block.sync
    def _(sync):
        if phases < 3:
            return
        sync.wait_ge(s_dve.h, v_scan(H - 1))
        sync.dma_start(out=cc_in, in_=carry_acc[:]).then_inc(s_ccw.h, 16)
        s_ccw.n += 16

    @block.gpsimd
    def _(g_):
        if phases < 3:
            return
        g_.wait_ge(s_ccw.h, 16)
        g_.collective_compute(
            "AllGather", ALU.bypass,
            replica_groups=[list(range(NCORES))],
            ins=[cc_in_t.ap().opt()], outs=[cc_out_t.ap().opt()],
        ).then_inc(s_ccd.h)
        s_ccd.n += 1

    @block.sync
    def _(sync):
        if phases < 3:
            return
        sync.wait_ge(s_ccd.h, 1)
        sync.dma_start(out=gat[:].rearrange("p (r j) -> p r j", r=NCORES),
                       in_=cc_out.rearrange("(r p) j -> p r j", p=128)) \
            .then_inc(s_gat.h, 16)
        s_gat.n += 16

    @block.tensor
    def _(t):
        if phases < 3:
            return
        t.wait_ge(s_gat.h, 16)
        for gg in (46, 47):
            fs, fv = psum2_free(gg)
            t.wait_ge(fs.h, fv)
        for r in range(NCORES):
            mm = t.matmul(pQ[0][:, 0:32], sel_sb[:, r * 128:(r + 1) * 128],
                          gat[:, r * 32:(r + 1) * 32],
                          start=(r == 0), stop=(r == NCORES - 1))
        mm.then_inc(s_pe.h, 1)
        s_pe.n += 1
    pe_sel = s_pe.n

    @block.vector
    def _(v):
        if phases < 3:
            return
        v.wait_ge(s_pe.h, pe_sel)
        v.tensor_copy(carry_cols[:], pQ[0][:, 0:32])
        v.drain().then_inc(s_dve.h, 1)
        s_dve.n += 1
    dve_car = s_dve.n

    S12.close()
    P2.close()

    S3w = ExitStack()
    AB = S3w.enter_context(nc.sbuf_tensor("AB", [128, NKC * 2048], BF16))
    wo_sb = S3w.enter_context(nc.sbuf_tensor("wo_sb", [128, NKC * 2048], BF16))
    AB_v = AB[:].rearrange("p (k t) -> p k t", k=NKC)
    wo_v = wo_sb[:].rearrange("p (k n) -> p k n", k=NKC)

    S3 = ExitStack()
    kv_sb = S3.enter_context(nc.sbuf_tensor("kv_sb", [128, T], F32))
    k_sb = S3.enter_context(nc.sbuf_tensor("k_sb", [128, T], F32))
    q_sb = S3.enter_context(nc.sbuf_tensor("q_sb", [128, T], F32))
    S_kv = S3.enter_context(nc.sbuf_tensor("S_kv", [128, T], F32))
    S_ks = S3.enter_context(nc.sbuf_tensor("S_ks", [128, T], F32))
    pbf = S3.enter_context(nc.sbuf_tensor("pbf", [128, T], BF16))
    rows3 = S3.enter_context(nc.sbuf_tensor("rows3", [128, T], F32))
    rden_t = S3.enter_context(nc.sbuf_tensor("rden_t", [1, T], F32R))
    P3 = ExitStack()
    pBC = P3.enter_context(nc.psum_tensor("pBC", [128, 2048], F32))

    @block.gpsimd
    def _(g):
        if phases < 4:
            return
        g.wait_ge(s_sp.h, sp_all2)
        g.dma_start(out=wo_sb[:], in_=wout_d[:], max_dma_last_dim=8192) \
            .then_inc(s_wos.h, 16)
        s_wos.n = 16

    dve_b3b = s_dve.n
    pool_b3b = s_pool.n
    act_b3b = s_act.n
    pe_b3b = s_pe.n
    rd_b3b = s_rd.n

    @block.scalar
    def _(s):
        if phases < 4:
            return
        for h in range(H):
            if h >= 1:
                s.wait_ge(s_dve.h, dve_b3b + 3 * (h - 1) + 3)
            else:
                s.wait_ge(s_dve.h, dve_car)
                s.wait_ge(s_pool.h, pool_all2)
                s.wait_ge(s_sp.h, sp_all2)
            s.wait_ge(s_sp.h, spill_of[2 * H + h])
            s.dma_start(out=kv_sb[:], in_=sp_kv[h * 128:(h + 1) * 128, :]) \
                .then_inc(s_rd.h, 16)
            s.dma_start(out=k_sb[:], in_=sp_k[h * 128:(h + 1) * 128, :]) \
                .then_inc(s_rd.h, 16)
            s.dma_start(out=q_sb[:], in_=sp_q[h * 128:(h + 1) * 128, :]) \
                .then_inc(s_rd.h, 16)
            s_rd.n += 48
            s.wait_ge(s_dve.h, dve_b3b + 3 * h + 2)
            s.activation(rows3[32:33, :], rows3[0:1, :], AF.Ln)
            s.drain()
            s.activation(rden_t[:], rows3[32:33, :], AF.Exp, scale=-1.0)
            s.drain().then_inc(s_act.h, 1)
            s_act.n += 1

    @block.vector
    def _(v):
        if phases < 4:
            return
        for h in range(H):
            v.wait_ge(s_rd.h, rd_b3b + 48 * (h + 1))
            dbc = bcast_col(dcol[:, h:h + 1], T)
            v.tensor_tensor_scan(S_kv[:], dbc, kv_sb[:],
                                 carry_cols[:, h:h + 1], ALU.mult, ALU.add)
            v.tensor_tensor_scan(S_ks[:], dbc, k_sb[:],
                                 carry_cols[:, H + h:H + h + 1], ALU.mult, ALU.add)
            v.drain().then_inc(s_dve.h, 1)
            v.wait_ge(s_pe.h, pe_b3b + 2 * h + 1)
            v.tensor_scalar(rows3[0:1, :], pBC[0:1, :], EPS_DEN, None, ALU.max)
            v.drain().then_inc(s_dve.h, 1)
            v.wait_ge(s_pe.h, pe_b3b + 2 * h + 2)
            v.tensor_tensor(AB_v[:, h, :], kv_sb[:], pBC[:, :], ALU.mult)
            v.drain().then_inc(s_dve.h, 1)
            s_dve.n += 3

    @block.gpsimd
    def _(g):
        if phases < 4:
            return
        for h in range(H):
            g.wait_ge(s_dve.h, dve_b3b + 3 * h + 1)
            g.tensor_tensor(kv_sb[:], q_sb[:], S_kv[:], ALU.mult)
            g.drain()
            g.tensor_tensor(pbf[:], q_sb[:], S_ks[:], ALU.mult)
            g.drain().then_inc(s_pool.h, 1)
            s_pool.n += 1

    @block.tensor
    def _(t):
        if phases < 4:
            return
        for h in range(H):
            t.wait_ge(s_pool.h, pool_b3b + h + 1)
            if h >= 1:
                t.wait_ge(s_dve.h, dve_b3b + 3 * (h - 1) + 3)
            for tg in range(4):
                mm = t.matmul(pBC[0:1, tg * 512:(tg + 1) * 512], ones_c[:],
                              pbf[:, tg * 512:(tg + 1) * 512], start=True, stop=True)
            mm.then_inc(s_pe.h, 1)
            s_pe.n += 1
            t.wait_ge(s_act.h, act_b3b + h + 1)
            for tg in range(4):
                mm = t.matmul(pBC[:, tg * 512:(tg + 1) * 512], ones_r[:],
                              rden_t[:, tg * 512:(tg + 1) * 512],
                              start=True, stop=True)
            mm.then_inc(s_pe.h, 1)
            s_pe.n += 1
    dve_3b_done = dve_b3b + 3 * H

    S3.close()
    P3.close()
    S3c = ExitStack()
    pW = S3c.enter_context(nc.psum_tensor("pW", [128, 2048], F32))
    pTh = [S3c.enter_context(nc.psum_tensor(f"pTh{i}", [128, 2048], BF16)) for i in range(2)]
    xs3 = S3c.enter_context(nc.sbuf_tensor("xs3", [128, DM], F32))
    x2sb = S3c.enter_context(nc.sbuf_tensor("x2sb", [128, DM], F32))
    attn_sb = [S3c.enter_context(nc.sbuf_tensor(f"attn_sb{i}", [128, DM], F32)) for i in range(2)]
    hf = [S3c.enter_context(nc.sbuf_tensor(f"hf{i}", [128, DM], BF16)) for i in range(2)]
    hstg = S3c.enter_context(nc.sbuf_tensor("hstg", [128, NKC * 512], BF16))
    hstg_v = hstg[:].rearrange("p (k t) -> p k t", k=NKC)
    sp_h_v = sp_h.rearrange("(k p) t -> p k t", p=128)

    pe_b3c = s_pe.n
    act_b3c = s_act.n
    dve_b3c = s_dve.n
    rd_b3c = s_rd.n

    def dbase3(tc):
        return dve_b3c + (6 * tc if tc < 2 else 12 + 7 * (tc - 2))

    def hevac_val(te):
        if te <= NTC - 3:
            return dbase3(te + 2) + 7
        return dve_b3c + 12 + 7 * (NTC - 2) + (te - (NTC - 2)) + 1

    def pe_mm3(tc):
        return pe_b3c + (tc + 1 if tc < 2 else 2 * tc - 1)

    def pe_tr3(te):
        if te <= NTC - 3:
            return pe_b3c + 2 * te + 4
        return pe_b3c + NTC + 1 + te

    @block.scalar
    def _(s):
        if phases < 5:
            return
        for tc in range(NTC):
            if tc == 0:
                s.wait_ge(s_dve.h, dve_3b_done)
            else:
                s.wait_ge(s_dve.h, dbase3(tc - 1) + 4)
            s.dma_start(out=xs3[:], in_=x_d[tc * 128:(tc + 1) * 128, :]) \
                .then_inc(s_rd.h, 16)
            s_rd.n += 16
            s.wait_ge(s_dve.h, dbase3(tc) + 2)
            s.activation(stat[:, 4:5], stat[:, 3:4], AF.Sqrt, bias=epsl[:])
            s.drain().then_inc(s_act.h, 1)
            s.wait_ge(s_dve.h, dbase3(tc) + 5)
            s.activation(stat[:, 4:5], stat[:, 3:4], AF.Sqrt, bias=epsl[:])
            s.drain().then_inc(s_act.h, 1)
            s.wait_ge(s_dve.h, dbase3(tc) + 6)
            if tc >= 2:
                s.wait_ge(s_pe.h, pe_tr3(tc - 2))
            s.activation(hf[tc % 2][:], x2sb[:], AF.Identity,
                         bias=stat[:, 6:7], scale=stat[:, 5:6])
            s.drain().then_inc(s_act.h, 1)
            s_act.n += 3

    @block.vector
    def _(v):
        if phases < 5:
            return
        for tc in range(NTC):
            v.wait_ge(s_pe.h, pe_mm3(tc))
            v.tensor_copy(attn_sb[tc % 2][:], pW[:])
            v.drain().then_inc(s_dve.h, 1)
            ln_stats(v, attn_sb[tc % 2][:])
            v.drain().then_inc(s_dve.h, 1)
            v.wait_ge(s_act.h, act_b3c + 3 * tc + 1)
            ln_finish_dve(v)
            v.drain().then_inc(s_dve.h, 1)
            v.wait_ge(s_rd.h, rd_b3c + 16 * (tc + 1))
            if tc >= 1:
                v.wait_ge(s_x2w.h, 16 * tc)
            v.tensor_scalar(xs3[:], xs3[:], stat[:, 6:7], None, ALU.add)
            v.drain()
            v.scalar_tensor_tensor(x2sb[:], attn_sb[tc % 2][:], stat[:, 5:6],
                                   xs3[:], ALU.mult, ALU.add)
            v.drain().then_inc(s_dve.h, 1)
            ln_stats(v, x2sb[:])
            v.drain().then_inc(s_dve.h, 1)
            v.wait_ge(s_act.h, act_b3c + 3 * tc + 2)
            ln_finish_dve(v)
            v.drain().then_inc(s_dve.h, 1)
            s_dve.n += 6
            if tc >= 2:
                te = tc - 2
                v.wait_ge(s_pe.h, pe_tr3(te))
                if te % 4 == 0 and te >= 4:
                    v.wait_ge(s_hfl.h, 16 * (te // 4))
                sv = pTh[te % 2][:].rearrange("p (k t) -> p k t", k=NKC)
                v.tensor_copy(hstg_v[:, :, (te % 4) * 128:(te % 4) * 128 + 128], sv)
                v.drain().then_inc(s_dve.h, 1)
                s_dve.n += 1
        for te in (NTC - 2, NTC - 1):
            v.wait_ge(s_pe.h, pe_tr3(te))
            if te % 4 == 0 and te >= 4:
                v.wait_ge(s_hfl.h, 16 * (te // 4))
            sv = pTh[te % 2][:].rearrange("p (k t) -> p k t", k=NKC)
            v.tensor_copy(hstg_v[:, :, (te % 4) * 128:(te % 4) * 128 + 128], sv)
            v.drain().then_inc(s_dve.h, 1)
            s_dve.n += 1

    @block.tensor
    def _(t):
        if phases < 5:
            return
        t.wait_ge(s_wos.h, 16)
        t.wait_ge(s_dve.h, dve_3b_done)
        for tc in range(NTC):
            if tc >= 1:
                t.wait_ge(s_dve.h, dbase3(tc - 1) + 1)
            for hk in range(NKC):
                for tg in range(4):
                    mm = t.matmul(pW[:, tg * 512:(tg + 1) * 512],
                                  AB_v[:, hk, tc * 128:(tc + 1) * 128],
                                  wo_v[:, hk, tg * 512:(tg + 1) * 512],
                                  start=(hk == 0), stop=(hk == NKC - 1))
            mm.then_inc(s_pe.h, 1)
            s_pe.n += 1
            if tc >= 2:
                te = tc - 2
                t.wait_ge(s_act.h, act_b3c + 3 * te + 3)
                if te >= 2:
                    t.wait_ge(s_dve.h, hevac_val(te - 2))
                for k in range(NKC):
                    mm = t.transpose(pTh[te % 2][:, k * 128:(k + 1) * 128],
                                     hf[te % 2][:, k * 128:(k + 1) * 128], idb[:])
                mm.then_inc(s_pe.h, 1)
                s_pe.n += 1
        for te in (NTC - 2, NTC - 1):
            t.wait_ge(s_act.h, act_b3c + 3 * te + 3)
            t.wait_ge(s_dve.h, hevac_val(te - 2))
            for k in range(NKC):
                mm = t.transpose(pTh[te % 2][:, k * 128:(k + 1) * 128],
                                 hf[te % 2][:, k * 128:(k + 1) * 128], idb[:])
            mm.then_inc(s_pe.h, 1)
            s_pe.n += 1

    @block.sync
    def _(sync):
        if phases < 5:
            return
        for tc in range(NTC):
            sync.wait_ge(s_dve.h, dbase3(tc) + 4)
            sync.dma_start(out=sp_x2[tc * 128:(tc + 1) * 128, :], in_=x2sb[:]) \
                .then_inc(s_x2w.h, 16)
            s_x2w.n += 16

    @block.gpsimd
    def _(g_):
        if phases < 5:
            return
        for grp in range(NTC // 4):
            g_.wait_ge(s_dve.h, hevac_val(4 * grp + 3))
            g_.dma_start(out=sp_h_v[:, :, grp * 512:(grp + 1) * 512],
                         in_=hstg_v[:, :, :]).then_inc(s_hfl.h, 16)
            s_hfl.n += 16
    dve_3c_done = dve_b3c + 12 + 7 * (NTC - 2) + 2

    S3c.close()
    S3w.close()
    S5 = ExitStack()
    pGU = [S5.enter_context(nc.psum_tensor(f"pGU{i}", [128, 2048], F32)) for i in range(2)]
    hh = S5.enter_context(nc.sbuf_tensor("hh", [128, NKC * 1024], BF16))
    dacc = S5.enter_context(nc.sbuf_tensor("dacc", [128, 8 * 2048], F32))
    gtd = S5.enter_context(nc.sbuf_tensor("gtd", [128, GFF * 1024], BF16))
    wgs = [S5.enter_context(nc.sbuf_tensor(f"wgs{i}", [128, 2048], BF16)) for i in range(2)]
    wus = [S5.enter_context(nc.sbuf_tensor(f"wus{i}", [128, 2048], BF16)) for i in range(2)]
    wdg = S5.enter_context(nc.sbuf_tensor("wdg", [128, GFF * 2048], BF16))
    sil = S5.enter_context(nc.sbuf_tensor("sil", [128, 1024], F32))
    xf = S5.enter_context(nc.sbuf_tensor("xf", [128, DM], F32))
    outst = S5.enter_context(nc.sbuf_tensor("outst", [128, DM], F32))
    hh_v = hh[:].rearrange("p (k t) -> p k t", k=NKC)
    gtd_v = gtd[:].rearrange("p (m t) -> p m t", m=GFF)
    wdg_v = wdg[:].rearrange("p (m n) -> p m n", m=GFF)

    pe_b5 = s_pe.n
    act_b5 = s_act.n
    dve_b5 = s_dve.n
    HALF_PE = NGRP * (GFF + 8)

    def dve_gated(half, g, mi):
        if half == 0:
            return dve_b5 + g * 16 + mi + 1
        if g == 0:
            return dve_b5 + 128 + 4 * mi + 1
        return dve_b5 + 168 + 16 * (g - 1) + mi + 1

    def dve_wadd(half, g, w):
        if half == 0:
            return dve_b5 + g * 16 + 8 + w + 1
        if g == 0:
            return dve_b5 + 161 + w
        return dve_b5 + 168 + 16 * (g - 1) + 8 + w + 1

    def dve_ln4(half, w, i):
        if half == 0:
            return dve_b5 + 128 + 4 * w + 2 + i
        return dve_b5 + 280 + w * 3 + i + 1

    def act_sil(half, g, mi):
        if half == 0:
            return act_b5 + g * 8 + mi + 1
        if g == 0:
            return act_b5 + 64 + 2 * mi + 1
        return act_b5 + 80 + 8 * (g - 1) + mi + 1

    def act_ln4s(half, w):
        if half == 0:
            return act_b5 + 64 + 2 * w + 2
        return act_b5 + 136 + w + 1

    def pe_gu(half, g, mi):
        return pe_b5 + half * HALF_PE + g * (GFF + 8) + mi + 1

    def pe_dw(half, g, w):
        return pe_b5 + half * HALF_PE + g * (GFF + 8) + GFF + w + 1

    def cst_w5(half, g, mi):
        return 48 * (half * (NM // 2) + (g * GFF + mi) // 2 + 1)

    @block.sync
    def _(sync):
        if phases < 6:
            return
        for half in range(2):
            for w in range(8):
                tcg = half * 8 + w
                if tcg == 0:
                    sync.wait_ge(s_dve.h, dve_3c_done)
                    sync.wait_ge(s_hfl.h, 16 * (NTC // 4))
                    sync.wait_ge(s_x2w.h, 16 * NTC)
                else:
                    pv = dve_ln4(half, w - 1, 2) if w >= 1 else dve_ln4(0, 7, 2)
                    sync.wait_ge(s_dve.h, pv)
                sync.dma_start(out=xf[:], in_=sp_x2[tcg * 128:(tcg + 1) * 128, :]) \
                    .then_inc(s_gat.h, 16)
                s_gat.n += 16
                sync.wait_ge(s_dve.h, dve_ln4(half, w, 2))
                sync.dma_start(out=out_d[tcg * 128:(tcg + 1) * 128, :], in_=outst[:]) \
                    .then_inc(s_out.h, 16)
                s_out.n += 16
        sync.wait_ge(s_out.h, s_out.n)

    @block.gpsimd
    def _(g_):
        if phases < 6:
            return
        for half in range(2):
            for g in range(NGRP):
                for mi in range(GFF):
                    m = g * GFF + mi
                    sl = mi % 2
                    gi = g * GFF + mi
                    if half == 0 and gi == 0:
                        g_.wait_ge(s_dve.h, dve_3c_done)
                        g_.wait_ge(s_x2w.h, 16 * NTC)
                        g_.wait_ge(s_hfl.h, 64)
                    if half > 0 or gi >= 2:
                        if gi >= 2:
                            pg, pmi = divmod(gi - 2, GFF)
                            g_.wait_ge(s_dve.h, dve_gated(half, pg, pmi))
                        else:
                            g_.wait_ge(s_dve.h, dve_gated(half - 1, NGRP - 1, GFF - 2 + gi))
                    g_.dma_start(out=wgs[sl][:], in_=wg_d[m], max_dma_last_dim=8192) \
                        .then_inc(s_w5[sl].h, 16)
                    g_.dma_start(out=wus[sl][:], in_=wu_d[m], max_dma_last_dim=8192) \
                        .then_inc(s_w5[sl].h, 16)
                    if mi == 0 and (g >= 1 or half >= 1):
                        if g >= 1:
                            g_.wait_ge(s_pe.h, pe_dw(half, g - 1, 7))
                        else:
                            g_.wait_ge(s_pe.h, pe_dw(half - 1, NGRP - 1, 7))
                    g_.dma_start(out=wdg_v[:, mi, :], in_=wd_d[m],
                                 max_dma_last_dim=8192).then_inc(s_w5[sl].h, 16)

    def emit_silu(s, half, g, mi):
        s.wait_ge(s_pe.h, pe_gu(half, g, mi))
        gi = g * GFF + mi
        if half > 0 or gi >= 1:
            if gi >= 1:
                pg, pmi = divmod(gi - 1, GFF)
                s.wait_ge(s_dve.h, dve_gated(half, pg, pmi))
            else:
                s.wait_ge(s_dve.h, dve_gated(half - 1, NGRP - 1, GFF - 1))
        s.activation(sil[:], pGU[mi % 2][:, 0:1024], AF.Silu)
        s.drain().then_inc(s_act.h, 1)
        s_act.n += 1

    def emit_sqrt4(s, half, w):
        s.wait_ge(s_dve.h, dve_ln4(half, w, 0))
        s.activation(stat[:, 4:5], stat[:, 3:4], AF.Sqrt, bias=epsl[:])
        s.drain().then_inc(s_act.h, 1)
        s_act.n += 1

    @block.scalar
    def _(s):
        if phases < 6:
            return
        s.wait_ge(s_hfl.h, 64)
        s.dma_start(out=hh[:], in_=sp_h_v[:, :, 0:1024]).then_inc(s_hh.h, 16)
        s_hh.n += 16
        for g in range(NGRP):
            for mi in range(GFF):
                emit_silu(s, 0, g, mi)
        s.wait_ge(s_pe.h, pe_gu(0, NGRP - 1, GFF - 1))
        s.dma_start(out=hh[:], in_=sp_h_v[:, :, 1024:2048]).then_inc(s_hh.h, 16)
        s_hh.n += 16
        for mi in range(GFF):
            emit_silu(s, 1, 0, mi)
            emit_sqrt4(s, 0, mi)
        for g in range(1, NGRP):
            for mi in range(GFF):
                emit_silu(s, 1, g, mi)
        for w in range(8):
            emit_sqrt4(s, 1, w)

    @block.tensor
    def _(t):
        if phases < 6:
            return
        for half in range(2):
            t.wait_ge(s_hh.h, 16 * (half + 1))
            if half == 0:
                t.wait_ge(s_dve.h, dve_3c_done)
            for g in range(NGRP):
                for mi in range(GFF):
                    gi = g * GFF + mi
                    t.wait_ge(s_w5[mi % 2].h, cst_w5(half, g, mi))
                    if gi >= 2:
                        pg, pmi = divmod(gi - 2, GFF)
                        t.wait_ge(s_dve.h, dve_gated(half, pg, pmi))
                        if mi < 2 and g >= 1:
                            t.wait_ge(s_dve.h, dve_wadd(half, g - 1, 7))
                    elif half == 1:
                        t.wait_ge(s_dve.h, dve_wadd(0, NGRP - 1, 7))
                    wgv = wgs[mi % 2][:].rearrange("p (k n) -> p k n", k=NKC)
                    wuv = wus[mi % 2][:].rearrange("p (k n) -> p k n", k=NKC)
                    for k in range(NKC):
                        for tg in range(2):
                            mm = t.matmul(pGU[mi % 2][:, tg * 512:(tg + 1) * 512],
                                          wgv[:, k, :],
                                          hh_v[:, k, tg * 512:(tg + 1) * 512],
                                          start=(k == 0), stop=(k == NKC - 1))
                        for tg in range(2):
                            mm = t.matmul(pGU[mi % 2][:, 1024 + tg * 512:1024 + (tg + 1) * 512],
                                          wuv[:, k, :],
                                          hh_v[:, k, tg * 512:(tg + 1) * 512],
                                          start=(k == 0), stop=(k == NKC - 1))
                    mm.then_inc(s_pe.h, 1)
                    s_pe.n += 1
                for w in range(8):
                    t.wait_ge(s_dve.h, dve_gated(half, g, GFF - 1))
                    if w >= 2:
                        t.wait_ge(s_dve.h, dve_wadd(half, g, w - 2))
                    elif g >= 1:
                        t.wait_ge(s_dve.h, dve_wadd(half, g - 1, 7))
                    elif half == 1:
                        t.wait_ge(s_dve.h, dve_wadd(0, NGRP - 1, 7))
                    for mi in range(GFF):
                        for ng in range(4):
                            mm = t.matmul(pGU[w % 2][:, ng * 512:(ng + 1) * 512],
                                          gtd_v[:, mi, w * 128:(w + 1) * 128],
                                          wdg_v[:, mi, ng * 512:(ng + 1) * 512],
                                          start=(mi == 0), stop=(mi == GFF - 1))
                    mm.then_inc(s_pe.h, 1)
                    s_pe.n += 1

    def emit_gated(v, half, g, mi):
        v.wait_ge(s_act.h, act_sil(half, g, mi))
        v.tensor_tensor(gtd_v[:, mi, :], sil[:],
                        pGU[mi % 2][:, 1024:2048], ALU.mult)
        v.drain().then_inc(s_dve.h, 1)
        s_dve.n += 1

    def emit_wadd(v, half, g, w):
        v.wait_ge(s_pe.h, pe_dw(half, g, w))
        if g == 0:
            v.tensor_copy(dacc[:, w * 2048:(w + 1) * 2048], pGU[w % 2][:])
        else:
            v.tensor_tensor(dacc[:, w * 2048:(w + 1) * 2048],
                            dacc[:, w * 2048:(w + 1) * 2048],
                            pGU[w % 2][:], ALU.add)
        v.drain().then_inc(s_dve.h, 1)
        s_dve.n += 1

    def emit_ln4(v, half, w):
        ln_stats(v, dacc[:, w * 2048:(w + 1) * 2048])
        v.drain().then_inc(s_dve.h, 1)
        v.wait_ge(s_act.h, act_ln4s(half, w))
        ln_finish_dve(v)
        v.drain().then_inc(s_dve.h, 1)
        v.wait_ge(s_gat.h, 16 + 16 * (half * 8 + w + 1))
        v.wait_ge(s_out.h, 16 * (half * 8 + w))
        v.tensor_scalar(xf[:], xf[:], stat[:, 6:7], None, ALU.add)
        v.drain()
        v.scalar_tensor_tensor(outst[:], dacc[:, w * 2048:(w + 1) * 2048],
                               stat[:, 5:6], xf[:], ALU.mult, ALU.add)
        v.drain().then_inc(s_dve.h, 1)
        s_dve.n += 3

    @block.vector
    def _(v):
        if phases < 6:
            return
        for g in range(NGRP):
            for mi in range(GFF):
                emit_gated(v, 0, g, mi)
            for w in range(8):
                emit_wadd(v, 0, g, w)
        for mi in range(GFF):
            emit_gated(v, 1, 0, mi)
            emit_ln4(v, 0, mi)
        for w in range(8):
            emit_wadd(v, 1, 0, w)
        for g in range(1, NGRP):
            for mi in range(GFF):
                emit_gated(v, 1, g, mi)
            for w in range(8):
                emit_wadd(v, 1, g, w)
        for w in range(8):
            emit_ln4(v, 1, w)

    if debug:
        @block.sync
        def _(sync):
            if 1 <= phases <= 2:
                sync.wait_ge(s_dve.h, dve_xnf)
                sync.dma_start(out=dbg_xnf, in_=xnf[:]).then_inc(s_out.h, 16)
                s_out.n += 16
            if phases >= 3:
                sync.wait_ge(s_dve.h, dve_car)
                sync.dma_start(out=dbg_carry, in_=carry_cols[:]).then_inc(s_out.h, 16)
                s_out.n += 16
            if 4 <= phases <= 5:
                sync.wait_ge(s_dve.h, dve_3b_done)
                sync.dma_start(out=dbg_ab, in_=AB[:]).then_inc(s_out.h, 16)
                s_out.n += 16
            sync.wait_ge(s_sp.h, s_sp.n)
            sync.wait_ge(s_x2w.h, s_x2w.n)
            sync.wait_ge(s_hfl.h, s_hfl.n)
            sync.wait_ge(s_out.h, s_out.n)

    S5.close()
    ES.close()
    return nc


def _prep_inputs(L0_attn_x, L0_attn_w_qkv, L0_attn_w_out, L0_attn_decay_param,
                 L0_ffn_w_gate, L0_ffn_w_up, L0_ffn_w_down):
    x = np.ascontiguousarray(np.asarray(L0_attn_x, dtype=np.float32).reshape(B * TSEQ, DM))
    w_qkv = np.asarray(L0_attn_w_qkv, dtype=np.float32)
    w_out = np.asarray(L0_attn_w_out, dtype=np.float32)
    decay = np.ascontiguousarray(np.asarray(L0_attn_decay_param, dtype=np.float32).reshape(1, H))
    w_gate = np.asarray(L0_ffn_w_gate, dtype=np.float32)
    w_up = np.asarray(L0_ffn_w_up, dtype=np.float32)
    w_down = np.asarray(L0_ffn_w_down, dtype=np.float32)

    wq_l = np.ascontiguousarray(
        w_qkv.reshape(NKC, 128, 3, 2, 8, 128).transpose(2, 3, 1, 0, 4, 5)
        .reshape(3, 2, 128, 16384))
    wo_l = np.ascontiguousarray(
        w_out.reshape(NKC, 128, DM).transpose(1, 0, 2).reshape(128, NKC * DM))
    wg_l = np.ascontiguousarray(
        w_gate.T.reshape(NKC, 128, NM, 128).transpose(2, 1, 0, 3)
        .reshape(NM, 128, NKC * 128))
    wu_l = np.ascontiguousarray(
        w_up.T.reshape(NKC, 128, NM, 128).transpose(2, 1, 0, 3)
        .reshape(NM, 128, NKC * 128))
    wd_l = np.ascontiguousarray(w_down.T.reshape(NM, 128, DM))

    ident = np.eye(128, dtype=np.float32)
    in_maps = []
    for c in range(NCORES):
        sel = np.zeros((128, NCORES * 128), dtype=np.float32)
        if c % 2 == 1:
            sel[:, (c - 1) * 128:c * 128] = ident
        in_maps.append({
            "x": np.ascontiguousarray(x[c * T:(c + 1) * T, :]),
            "w_qkv": wq_l, "w_out": wo_l, "w_gT": wg_l, "w_uT": wu_l,
            "w_dT": wd_l, "decay": decay, "sel": sel, "ident": ident,
        })
    return in_maps


def kernel(**inputs):
    in_maps = _prep_inputs(**inputs)
    nc = build_program()
    res = run_bass_kernel_spmd(nc, in_maps, core_ids=list(range(NCORES)))
    out = np.concatenate([np.asarray(res.results[c]["out"]) for c in range(NCORES)],
                         axis=0)
    return out.reshape(B, TSEQ, DM).astype(np.float32)


def run_dev(inputs, phases=9, debug=True):
    in_maps = _prep_inputs(**inputs)
    nc = build_program(debug=debug, phases=phases)
    res = run_bass_kernel_spmd(nc, in_maps, core_ids=list(range(NCORES)))
    return res.results


# version 06
# speedup vs baseline: 1.1458x; 1.0032x over previous
import numpy as np
import ml_dtypes
from contextlib import ExitStack

import concourse.bass as bass
import concourse.mybir as mybir
from concourse.bass_utils import run_bass_kernel_spmd

F32 = mybir.dt.float32
F32R = mybir.dt.float32r
BF16 = mybir.dt.bfloat16
AF = mybir.ActivationFunctionType
ALU = mybir.AluOpType
AX = mybir.AxisListType

NCORES = 8
B, TSEQ, DM = 4, 4096, 2048
T = 2048
H = 16
NKC = 16
NTC = 16
HID = 8192
NM = 64
GFF = 8
NGRP = NM // GFF
EPS_LN = 1e-5
EPS_DEN = 1e-6


class Sem:
    def __init__(self, es, nc, name):
        self.h = es.enter_context(nc.semaphore(name))
        self.n = 0


def bcast_col(ap_col, n):
    row = ap_col.ap[0][0]
    return bass.AP(ap_col.tensor, ap_col.offset, [[row, 128], [0, n]])


def build_program(debug=False, phases=9):
    nc = bass.Bass("TRN2", target_bir_lowering=False)

    x_d = nc.dram_tensor("x", [T, DM], F32, kind="ExternalInput").ap()
    wqkv_d = nc.dram_tensor("w_qkv", [3, 2, 128, 16384], F32, kind="ExternalInput").ap()
    wout_d = nc.dram_tensor("w_out", [128, NKC * DM], F32, kind="ExternalInput").ap()
    wg_d = nc.dram_tensor("w_gT", [NM, 128, NKC * 128], F32, kind="ExternalInput").ap()
    wu_d = nc.dram_tensor("w_uT", [NM, 128, NKC * 128], F32, kind="ExternalInput").ap()
    wd_d = nc.dram_tensor("w_dT", [NM, 128, DM], F32, kind="ExternalInput").ap()
    dec_d = nc.dram_tensor("decay", [1, H], F32, kind="ExternalInput").ap()
    sel_d = nc.dram_tensor("sel", [128, NCORES * 128], F32, kind="ExternalInput").ap()
    id_d = nc.dram_tensor("ident", [128, 128], F32, kind="ExternalInput").ap()
    out_d = nc.dram_tensor("out", [T, DM], F32, kind="ExternalOutput").ap()

    spk = "ExternalOutput" if debug else "Internal"
    sp_q = nc.dram_tensor("sp_q", [DM, T], F32, kind=spk).ap()
    sp_k = nc.dram_tensor("sp_k", [DM, T], F32, kind=spk).ap()
    sp_kv = nc.dram_tensor("sp_kv", [DM, T], F32, kind=spk).ap()
    sp_x2 = nc.dram_tensor("sp_x2", [T, DM], F32, kind=spk).ap()
    sp_h = nc.dram_tensor("sp_h", [DM, T], BF16, kind=spk).ap()
    if debug:
        dbg_xnf = nc.dram_tensor("dbg_xnf", [128, NKC * 2048], BF16, kind="ExternalOutput").ap()
        dbg_ab = nc.dram_tensor("dbg_ab", [128, NKC * 2048], BF16, kind="ExternalOutput").ap()
        dbg_carry = nc.dram_tensor("dbg_carry", [128, 32], F32, kind="ExternalOutput").ap()
    cc_in_t = nc.dram_tensor("cc_in", [128, 32], F32)
    cc_out_t = nc.dram_tensor("cc_out", [NCORES * 128, 32], F32)
    cc_in = cc_in_t.ap()
    cc_out = cc_out_t.ap()

    ES = ExitStack()

    idb = ES.enter_context(nc.sbuf_tensor("idb", [128, 128], BF16))
    idf = ES.enter_context(nc.sbuf_tensor("idf", [128, 128], F32))
    dsb = ES.enter_context(nc.sbuf_tensor("dsb", [1, H], F32))
    dcol = ES.enter_context(nc.sbuf_tensor("dcol", [128, H], F32))
    sel_sb = ES.enter_context(nc.sbuf_tensor("sel_sb", [128, NCORES * 128], F32))
    gat = ES.enter_context(nc.sbuf_tensor("gat", [128, NCORES * 32], F32))
    ones_c = ES.enter_context(nc.sbuf_tensor("ones_c", [128, 1], BF16))
    ones_r = ES.enter_context(nc.sbuf_tensor("ones_r", [1, 128], F32R))
    ones_f = ES.enter_context(nc.sbuf_tensor("ones_f", [1, 128], F32))
    carry_acc = ES.enter_context(nc.sbuf_tensor("carry_acc", [128, 32], F32))
    carry_cols = ES.enter_context(nc.sbuf_tensor("carry_cols", [128, 32], F32))
    stat = ES.enter_context(nc.sbuf_tensor("stat", [128, 8], F32))
    bnb = ES.enter_context(nc.sbuf_tensor("bnb", [128, 24], F32))
    epsl = ES.enter_context(nc.sbuf_tensor("epsl", [128, 1], F32))

    sem = lambda name: Sem(ES, nc, name)
    s_ld = sem("s_ld")
    s_idb = sem("s_idb")
    s_wq2 = [sem("s_wq20"), sem("s_wq21")]
    s_wos = sem("s_wos")
    s_x = [sem("s_x0")]
    s_sp = sem("s_sp")
    s_rd = sem("s_rd")
    s_ccw = sem("s_ccw")
    s_ccd = sem("s_ccd")
    s_gat = sem("s_gat")
    s_x2w = sem("s_x2w")
    s_hfl = sem("s_hfl")
    s_hh = sem("s_hh")
    s_out = sem("s_out")
    s_w5 = [sem("s_w50"), sem("s_w51")]
    s_pe = sem("s_pe")
    s_dve = sem("s_dve")
    s_act = sem("s_act")
    s_pool = sem("s_pool")

    block = ES.enter_context(nc.Block())

    def ln_stats(v, src_v, n=2048):
        for i in range(n // 512):
            v.bn_stats(bnb[:, 6 * i:6 * i + 6], src_v[:, i * 512:(i + 1) * 512])
        v.drain()
        v.bn_aggr(stat[:, 2:4], bnb[:, 0:6 * (n // 512)])
        v.drain()

    def ln_finish_dve(v):
        v.reciprocal(stat[:, 5:6], stat[:, 4:5])
        v.drain()
        v.scalar_tensor_tensor(stat[:, 6:7], stat[:, 2:3], -1.0,
                               stat[:, 5:6], ALU.mult, ALU.mult)
        v.drain()

    S12 = ExitStack()
    xnf = S12.enter_context(nc.sbuf_tensor("xnf", [128, NKC * 2048], BF16))
    wq2 = [S12.enter_context(nc.sbuf_tensor(f"wq2_{i}", [128, 16384], BF16)) for i in range(2)]
    stg = [S12.enter_context(nc.sbuf_tensor(f"stg{i}", [128, 2048], F32)) for i in range(2)]
    e_st = S12.enter_context(nc.sbuf_tensor("e_st", [128, 2048], F32))
    r_st = S12.enter_context(nc.sbuf_tensor("r_st", [128, 2048], F32))
    kre = S12.enter_context(nc.sbuf_tensor("kre", [128, 2048], F32))
    S_scr = S12.enter_context(nc.sbuf_tensor("S_scr", [128, 2048], F32))
    S1 = ExitStack()
    xs = S1.enter_context(nc.sbuf_tensor("xs", [128, DM], F32))
    xn_tm = [S1.enter_context(nc.sbuf_tensor(f"xn_tm{i}", [128, DM], BF16)) for i in range(2)]

    P01 = ExitStack()
    pS = P01.enter_context(nc.psum_tensor("pS", [128, 512], F32))
    pT = [P01.enter_context(nc.psum_tensor(f"pT{i}", [128, 2048], BF16)) for i in range(2)]

    xnf_v = xnf[:].rearrange("p (k t) -> p k t", k=NKC)
    wq2_v = [w[:].rearrange("p (k m c) -> p k m c", k=NKC, m=8) for w in wq2]

    @block.sync
    def _(sync):
        sync.dma_start(out=idf[:], in_=id_d[:]).then_inc(s_ld.h, 16)
        sync.dma_start(out=dsb[:], in_=dec_d[:]).then_inc(s_ld.h, 16)
        sync.dma_start(out=sel_sb[:], in_=sel_d[:]).then_inc(s_ld.h, 16)
    s_ld.n = 48

    @block.gpsimd
    def _(g):
        g.dma_start(out=idb[:], in_=id_d[:]).then_inc(s_idb.h, 16)
    s_idb.n = 16

    @block.vector
    def _(v):
        v.memset(ones_c[:], 1.0)
        v.memset(ones_f[:], 1.0)
        v.memset(carry_acc[:], 0.0)
        v.memset(epsl[:], EPS_LN)
        v.drain().then_inc(s_dve.h, 1)
    s_dve.n += 1
    dve_consts = s_dve.n

    @block.scalar
    def _(s):
        s.wait_ge(s_ld.h, 48)
        s.activation(dsb[:], dsb[:], AF.Sigmoid)
        s.wait_ge(s_dve.h, dve_consts)
        s.activation(ones_r[:], ones_f[:], AF.Identity)
        s.drain().then_inc(s_act.h, 1)
    s_act.n += 1
    act_sig = s_act.n

    @block.tensor
    def _(t):
        t.wait_ge(s_act.h, act_sig)
        t.matmul(pS[:, 0:H], ones_f[:], dsb[:], start=True, stop=True).then_inc(s_pe.h, 1)
    s_pe.n += 1
    pe_dcol = s_pe.n

    @block.vector
    def _(v):
        v.wait_ge(s_pe.h, pe_dcol)
        v.tensor_copy(dcol[:], pS[:, 0:H])
        v.drain().then_inc(s_dve.h, 1)
    s_dve.n += 1

    act_b1 = s_act.n
    dve_b1 = s_dve.n
    pe_b1 = s_pe.n

    @block.sync
    def _(sync):
        if phases < 1:
            return
        for tc in range(NTC):
            if tc >= 1:
                sync.wait_ge(s_act.h, act_b1 + 2 * (tc - 1) + 2)
            sync.dma_start(out=xs[:], in_=x_d[tc * 128:(tc + 1) * 128, :]) \
                .then_inc(s_x[0].h, 16)
            s_x[0].n += 16

    @block.scalar
    def _(s):
        if phases < 1:
            return
        for tc in range(NTC):
            s.wait_ge(s_dve.h, dve_b1 + 3 * tc + 1)
            s.activation(stat[:, 4:5], stat[:, 3:4], AF.Sqrt, bias=epsl[:])
            s.drain().then_inc(s_act.h, 1)
            s.wait_ge(s_dve.h, dve_b1 + 3 * tc + 2)
            if tc >= 2:
                s.wait_ge(s_pe.h, pe_b1 + tc - 1)
            s.activation(xn_tm[tc % 2][:], xs[:], AF.Identity,
                         bias=stat[:, 6:7], scale=stat[:, 5:6])
            s.drain().then_inc(s_act.h, 1)
            s_act.n += 2

    @block.vector
    def _(v):
        if phases < 1:
            return
        for tc in range(NTC):
            v.wait_ge(s_x[0].h, 16 * (tc + 1))
            ln_stats(v, xs[:])
            v.drain().then_inc(s_dve.h, 1)
            v.wait_ge(s_act.h, act_b1 + 2 * tc + 1)
            ln_finish_dve(v)
            v.drain().then_inc(s_dve.h, 1)
            v.wait_ge(s_pe.h, pe_b1 + tc + 1)
            src = pT[tc % 2][:].rearrange("p (k t) -> p k t", k=NKC)
            v.tensor_copy(xnf_v[:, :, tc * 128:(tc + 1) * 128], src)
            v.drain().then_inc(s_dve.h, 1)
            s_dve.n += 3

    @block.tensor
    def _(t):
        if phases < 1:
            return
        t.wait_ge(s_idb.h, 16)
        for tc in range(NTC):
            t.wait_ge(s_act.h, act_b1 + 2 * tc + 2)
            if tc >= 2:
                t.wait_ge(s_dve.h, dve_b1 + 3 * (tc - 2) + 3)
            for k in range(NKC):
                mm = t.transpose(pT[tc % 2][:, k * 128:(k + 1) * 128],
                                 xn_tm[tc % 2][:, k * 128:(k + 1) * 128], idb[:])
            mm.then_inc(s_pe.h, 1)
            s_pe.n += 1
    dve_xnf = dve_b1 + 3 * NTC

    P01.close()
    S1.close()
    P2 = ExitStack()
    pQ = [P2.enter_context(nc.psum_tensor(f"pQ{i}", [128, 2048], F32)) for i in range(2)]

    QUADS = [1, 2, 0]
    pe_b2 = s_pe.n
    act_b2 = s_act.n
    pool_b2 = s_pool.n

    def mmdone2(g):
        return pe_b2 + g + 1

    def v_tt(m):
        return dve_xnf + H + 2 * m + 1

    def v_scan(m):
        return dve_xnf + H + 2 * m + 2

    def psum2_free(g):
        qi, m = divmod(g, H)
        if qi == 0:
            return (s_act, act_b2 + m + 1)
        if qi == 1:
            return (s_dve, v_tt(m))
        return (s_act, act_b2 + H + m + 1)

    def stgdone2(g):
        qi, m = divmod(g, H)
        if qi == 0:
            return (s_dve, dve_xnf + m + 1)
        if qi == 1:
            return (s_dve, v_tt(m))
        return (s_pool, pool_b2 + m + 1)

    def stream_idx(g):
        return 2 * (g // H) + (g % H) // 8

    @block.gpsimd
    def _(g_):
        if phases < 2:
            return
        for idx in range(6):
            qi, mh = divmod(idx, 2)
            if idx >= 2:
                g_.wait_ge(s_pe.h, mmdone2(8 * (idx - 2) + 7))
            g_.dma_start(out=wq2[idx % 2][:], in_=wqkv_d[QUADS[qi], mh],
                         max_dma_last_dim=8192).then_inc(s_wq2[idx % 2].h, 16)
            s_wq2[idx % 2].n += 16

    @block.tensor
    def _(t):
        if phases < 2:
            return
        for qi in range(3):
            for m in range(H):
                g = qi * H + m
                idx = stream_idx(g)
                if m % 8 == 0:
                    t.wait_ge(s_wq2[idx % 2].h, 16 * (idx // 2 + 1))
                if g == 0:
                    t.wait_ge(s_dve.h, dve_xnf)
                if g >= 2:
                    fs, fv = psum2_free(g - 2)
                    t.wait_ge(fs.h, fv)
                wv = wq2_v[idx % 2]
                for k in range(NKC):
                    for tg in range(4):
                        mm = t.matmul(pQ[g % 2][:, tg * 512:(tg + 1) * 512],
                                      wv[:, k, m % 8, :],
                                      xnf_v[:, k, tg * 512:(tg + 1) * 512],
                                      start=(k == 0), stop=(k == NKC - 1))
                mm.then_inc(s_pe.h, 1)
                s_pe.n += 1

    @block.scalar
    def _(s):
        if phases < 2:
            return
        for m in range(H):
            s.wait_ge(s_pe.h, mmdone2(m))
            if m >= 1:
                s.wait_ge(s_dve.h, dve_xnf + m)
            s.activation(e_st[:], pQ[m % 2][:], AF.Exp)
            s.activation(r_st[:], pQ[m % 2][:], AF.Relu)
            s.drain().then_inc(s_act.h, 1)
            s_act.n += 1

    @block.scalar
    def _(s):
        if phases < 2:
            return
        for m in range(H):
            s.wait_ge(s_sp.h, 16 * H)
            if m >= 1:
                s.wait_ge(s_dve.h, v_scan(m - 1))
            s.dma_start(out=kre[:], in_=sp_k[m * 128:(m + 1) * 128, :]) \
                .then_inc(s_rd.h, 16)
            s_rd.n += 16

    @block.scalar
    def _(s):
        if phases < 2:
            return
        for m in range(H):
            g = 2 * H + m
            s.wait_ge(s_pe.h, mmdone2(g))
            if m == 0:
                s.wait_ge(s_dve.h, v_scan(H - 1))
            else:
                s.wait_ge(s_pool.h, pool_b2 + m)
            s.activation(e_st[:], pQ[g % 2][:], AF.Exp)
            s.activation(r_st[:], pQ[g % 2][:], AF.Relu)
            s.drain().then_inc(s_act.h, 1)
            s_act.n += 1

    @block.vector
    def _(v):
        if phases < 2:
            return
        for m in range(H):
            v.wait_ge(s_act.h, act_b2 + m + 1)
            if m >= 2:
                v.wait_ge(s_sp.h, 16 * m)
            v.scalar_tensor_tensor(stg[m % 2][:], e_st[:], 1.0,
                                   r_st[:], ALU.min, ALU.add)
            v.drain().then_inc(s_dve.h, 1)
            s_dve.n += 1
        for m in range(H):
            g = H + m
            v.wait_ge(s_pe.h, mmdone2(g))
            v.wait_ge(s_rd.h, 16 * (m + 1))
            v.wait_ge(s_sp.h, 16 * (H + m))
            if m >= 2:
                v.wait_ge(s_dve.h, v_scan(m - 2))
            v.tensor_tensor(stg[m % 2][:], pQ[g % 2][:], kre[:], ALU.mult)
            v.drain().then_inc(s_dve.h, 1)
            dbc = bcast_col(dcol[:, m:m + 1], T)
            v.tensor_tensor_scan(S_scr[:], dbc, stg[m % 2][:], 0.0, ALU.mult, ALU.add)
            v.drain()
            v.tensor_copy(carry_acc[:, m:m + 1], S_scr[:, T - 1:T])
            v.drain()
            v.tensor_tensor_scan(S_scr[:], dbc, kre[:], 0.0, ALU.mult, ALU.add)
            v.drain()
            v.tensor_copy(carry_acc[:, H + m:H + m + 1], S_scr[:, T - 1:T])
            v.drain().then_inc(s_dve.h, 1)
            s_dve.n += 2

    @block.gpsimd
    def _(g_):
        if phases < 2:
            return
        for m in range(H):
            g_.wait_ge(s_act.h, act_b2 + H + m + 1)
            g_.wait_ge(s_sp.h, 16 * (2 * H + m))
            g_.tensor_scalar(e_st[:], e_st[:], 1.0, None, ALU.min)
            g_.drain()
            g_.tensor_tensor(stg[m % 2][:], e_st[:], r_st[:], ALU.add)
            g_.drain().then_inc(s_pool.h, 1)
            s_pool.n += 1
    pool_all2 = s_pool.n

    spill_of = {}

    @block.sync
    def _(sync):
        if phases < 2:
            return
        targets = [sp_k, sp_kv, sp_q]
        for qi in range(3):
            for m in range(H):
                g = qi * H + m
                fs, fv = stgdone2(g)
                sync.wait_ge(fs.h, fv)
                sync.dma_start(out=targets[qi][m * 128:(m + 1) * 128, :],
                               in_=stg[m % 2][:]).then_inc(s_sp.h, 16)
                s_sp.n += 16
                spill_of[g] = s_sp.n
    sp_all2 = s_sp.n

    @block.sync
    def _(sync):
        if phases < 3:
            return
        sync.wait_ge(s_dve.h, v_scan(H - 1))
        sync.dma_start(out=cc_in, in_=carry_acc[:]).then_inc(s_ccw.h, 16)
        s_ccw.n += 16

    @block.gpsimd
    def _(g_):
        if phases < 3:
            return
        g_.wait_ge(s_ccw.h, 16)
        g_.collective_compute(
            "AllGather", ALU.bypass,
            replica_groups=[list(range(NCORES))],
            ins=[cc_in_t.ap().opt()], outs=[cc_out_t.ap().opt()],
        ).then_inc(s_ccd.h)
        s_ccd.n += 1

    @block.sync
    def _(sync):
        if phases < 3:
            return
        sync.wait_ge(s_ccd.h, 1)
        sync.dma_start(out=gat[:].rearrange("p (r j) -> p r j", r=NCORES),
                       in_=cc_out.rearrange("(r p) j -> p r j", p=128)) \
            .then_inc(s_gat.h, 16)
        s_gat.n += 16

    @block.tensor
    def _(t):
        if phases < 3:
            return
        t.wait_ge(s_gat.h, 16)
        for gg in (46, 47):
            fs, fv = psum2_free(gg)
            t.wait_ge(fs.h, fv)
        for r in range(NCORES):
            mm = t.matmul(pQ[0][:, 0:32], sel_sb[:, r * 128:(r + 1) * 128],
                          gat[:, r * 32:(r + 1) * 32],
                          start=(r == 0), stop=(r == NCORES - 1))
        mm.then_inc(s_pe.h, 1)
        s_pe.n += 1
    pe_sel = s_pe.n

    @block.vector
    def _(v):
        if phases < 3:
            return
        v.wait_ge(s_pe.h, pe_sel)
        v.tensor_copy(carry_cols[:], pQ[0][:, 0:32])
        v.drain().then_inc(s_dve.h, 1)
        s_dve.n += 1
    dve_car = s_dve.n

    S12.close()
    P2.close()

    S3w = ExitStack()
    AB = S3w.enter_context(nc.sbuf_tensor("AB", [128, NKC * 2048], BF16))
    wo_sb = S3w.enter_context(nc.sbuf_tensor("wo_sb", [128, NKC * 2048], BF16))
    AB_v = AB[:].rearrange("p (k t) -> p k t", k=NKC)
    wo_v = wo_sb[:].rearrange("p (k n) -> p k n", k=NKC)

    S3 = ExitStack()
    kv_sb = S3.enter_context(nc.sbuf_tensor("kv_sb", [128, T], F32))
    k_sb = S3.enter_context(nc.sbuf_tensor("k_sb", [128, T], F32))
    q_sb = S3.enter_context(nc.sbuf_tensor("q_sb", [128, T], F32))
    S_kv = S3.enter_context(nc.sbuf_tensor("S_kv", [128, T], F32))
    S_ks = S3.enter_context(nc.sbuf_tensor("S_ks", [128, T], F32))
    pbf = S3.enter_context(nc.sbuf_tensor("pbf", [128, T], BF16))
    rows3 = S3.enter_context(nc.sbuf_tensor("rows3", [128, T], F32))
    rden_t = S3.enter_context(nc.sbuf_tensor("rden_t", [1, T], F32R))
    P3 = ExitStack()
    pBC = [P3.enter_context(nc.psum_tensor(f"pBC{i}", [128, 2048], F32)) for i in range(2)]

    @block.gpsimd
    def _(g):
        if phases < 4:
            return
        g.wait_ge(s_sp.h, sp_all2)
        g.dma_start(out=wo_sb[:], in_=wout_d[:], max_dma_last_dim=8192) \
            .then_inc(s_wos.h, 16)
        s_wos.n = 16

    dve_b3b = s_dve.n
    pool_b3b = s_pool.n
    act_b3b = s_act.n
    pe_b3b = s_pe.n
    rd_b3b = s_rd.n

    def scankv3(h):
        return dve_b3b + (1 if h == 0 else 4 * h - 1)

    def scanks3(h):
        return dve_b3b + (2 if h == 0 else 4 * h + 2)

    def clipv3b(h):
        return dve_b3b + (4 * (h + 1) if h <= H - 2 else 4 * H - 1)

    def outprev3b(h):
        return dve_b3b + (4 * (h + 1) + 1 if h <= H - 2 else 4 * H)

    @block.scalar
    def _(s):
        if phases < 4:
            return
        for h in range(H):
            if h >= 1:
                s.wait_ge(s_pool.h, pool_b3b + 2 * (h - 1) + 2)
            else:
                s.wait_ge(s_dve.h, dve_car)
                s.wait_ge(s_pool.h, pool_all2)
                s.wait_ge(s_sp.h, sp_all2)
            s.wait_ge(s_sp.h, spill_of[2 * H + h])
            s.dma_start(out=kv_sb[:], in_=sp_kv[h * 128:(h + 1) * 128, :]) \
                .then_inc(s_rd.h, 16)
            s_rd.n += 16
            if h >= 1:
                s.wait_ge(s_dve.h, clipv3b(h - 1))
                s.activation(rows3[32:33, :], rows3[0:1, :], AF.Ln)
                s.drain()
                s.wait_ge(s_pe.h, pe_b3b + 2 * (h - 1))
                s.activation(rden_t[:], rows3[32:33, :], AF.Exp, scale=-1.0)
                s.drain().then_inc(s_act.h, 1)
                s_act.n += 1
        s.wait_ge(s_dve.h, clipv3b(H - 1))
        s.activation(rows3[32:33, :], rows3[0:1, :], AF.Ln)
        s.drain()
        s.wait_ge(s_pe.h, pe_b3b + 2 * (H - 1))
        s.activation(rden_t[:], rows3[32:33, :], AF.Exp, scale=-1.0)
        s.drain().then_inc(s_act.h, 1)
        s_act.n += 1

    @block.sync
    def _(sync):
        if phases < 4:
            return
        for h in range(H):
            if h >= 1:
                sync.wait_ge(s_pool.h, pool_b3b + 2 * (h - 1) + 2)
            else:
                sync.wait_ge(s_dve.h, dve_car)
                sync.wait_ge(s_pool.h, pool_all2)
                sync.wait_ge(s_sp.h, sp_all2)
            sync.wait_ge(s_sp.h, spill_of[2 * H + h])
            sync.dma_start(out=k_sb[:], in_=sp_k[h * 128:(h + 1) * 128, :]) \
                .then_inc(s_rd.h, 16)
            sync.dma_start(out=q_sb[:], in_=sp_q[h * 128:(h + 1) * 128, :]) \
                .then_inc(s_rd.h, 16)
            s_rd.n += 32

    @block.vector
    def _(v):
        if phases < 4:
            return
        for h in range(H):
            v.wait_ge(s_rd.h, rd_b3b + 48 * (h + 1))
            if h >= 1:
                v.wait_ge(s_pool.h, pool_b3b + 2 * (h - 1) + 2)
            dbc = bcast_col(dcol[:, h:h + 1], T)
            v.tensor_tensor_scan(S_kv[:], dbc, kv_sb[:],
                                 carry_cols[:, h:h + 1], ALU.mult, ALU.add)
            v.drain().then_inc(s_dve.h, 1)
            s_dve.n += 1
            if h >= 1:
                v.wait_ge(s_pe.h, pe_b3b + 2 * (h - 1) + 1)
                if h >= 2:
                    v.wait_ge(s_act.h, act_b3b + h - 1)
                v.tensor_scalar(rows3[0:1, :], pBC[(h - 1) % 2][0:1, :],
                                EPS_DEN, None, ALU.max)
                v.drain().then_inc(s_dve.h, 1)
                v.wait_ge(s_pe.h, pe_b3b + 2 * (h - 1) + 2)
                v.tensor_tensor(AB_v[:, h - 1, :], S_ks[:], pBC[(h - 1) % 2][:, :],
                                ALU.mult)
                v.drain().then_inc(s_dve.h, 1)
                s_dve.n += 2
            v.tensor_tensor_scan(S_ks[:], dbc, k_sb[:],
                                 carry_cols[:, H + h:H + h + 1], ALU.mult, ALU.add)
            v.drain().then_inc(s_dve.h, 1)
            s_dve.n += 1
        v.wait_ge(s_pe.h, pe_b3b + 2 * (H - 1) + 1)
        v.wait_ge(s_act.h, act_b3b + H - 1)
        v.tensor_scalar(rows3[0:1, :], pBC[(H - 1) % 2][0:1, :],
                        EPS_DEN, None, ALU.max)
        v.drain().then_inc(s_dve.h, 1)
        v.wait_ge(s_pe.h, pe_b3b + 2 * (H - 1) + 2)
        v.wait_ge(s_pool.h, pool_b3b + 2 * H)
        v.tensor_tensor(AB_v[:, H - 1, :], S_ks[:], pBC[(H - 1) % 2][:, :],
                        ALU.mult)
        v.drain().then_inc(s_dve.h, 1)
        s_dve.n += 2

    @block.gpsimd
    def _(g):
        if phases < 4:
            return
        for h in range(H):
            g.wait_ge(s_dve.h, scanks3(h))
            if h >= 1:
                g.wait_ge(s_pe.h, pe_b3b + 2 * (h - 1) + 1)
            g.tensor_tensor(pbf[:], q_sb[:], S_ks[:], ALU.mult)
            g.drain().then_inc(s_pool.h, 1)
            g.tensor_tensor(S_ks[:], q_sb[:], S_kv[:], ALU.mult)
            g.drain().then_inc(s_pool.h, 1)
            s_pool.n += 2

    @block.tensor
    def _(t):
        if phases < 4:
            return
        for h in range(H):
            t.wait_ge(s_pool.h, pool_b3b + 2 * h + 1)
            if h >= 2:
                t.wait_ge(s_dve.h, outprev3b(h - 2))
            for tg in range(4):
                mm = t.matmul(pBC[h % 2][0:1, tg * 512:(tg + 1) * 512], ones_c[:],
                              pbf[:, tg * 512:(tg + 1) * 512], start=True, stop=True)
            mm.then_inc(s_pe.h, 1)
            s_pe.n += 1
            t.wait_ge(s_act.h, act_b3b + h + 1)
            for tg in range(4):
                mm = t.matmul(pBC[h % 2][:, tg * 512:(tg + 1) * 512], ones_r[:],
                              rden_t[:, tg * 512:(tg + 1) * 512],
                              start=True, stop=True)
            mm.then_inc(s_pe.h, 1)
            s_pe.n += 1
    dve_3b_done = dve_b3b + 4 * H

    S3.close()
    P3.close()
    S3c = ExitStack()
    pW = S3c.enter_context(nc.psum_tensor("pW", [128, 2048], F32))
    pTh = [S3c.enter_context(nc.psum_tensor(f"pTh{i}", [128, 2048], BF16)) for i in range(2)]
    xs3 = S3c.enter_context(nc.sbuf_tensor("xs3", [128, DM], F32))
    x2sb = S3c.enter_context(nc.sbuf_tensor("x2sb", [128, DM], F32))
    attn_sb = [S3c.enter_context(nc.sbuf_tensor(f"attn_sb{i}", [128, DM], F32)) for i in range(2)]
    hf = [S3c.enter_context(nc.sbuf_tensor(f"hf{i}", [128, DM], BF16)) for i in range(2)]
    hstg = S3c.enter_context(nc.sbuf_tensor("hstg", [128, NKC * 512], BF16))
    hstg_v = hstg[:].rearrange("p (k t) -> p k t", k=NKC)
    sp_h_v = sp_h.rearrange("(k p) t -> p k t", p=128)

    pe_b3c = s_pe.n
    act_b3c = s_act.n
    dve_b3c = s_dve.n
    rd_b3c = s_rd.n

    def dbase3(tc):
        return dve_b3c + (6 * tc if tc < 2 else 12 + 7 * (tc - 2))

    def hevac_val(te):
        if te <= NTC - 3:
            return dbase3(te + 2) + 7
        return dve_b3c + 12 + 7 * (NTC - 2) + (te - (NTC - 2)) + 1

    def pe_mm3(tc):
        return pe_b3c + (tc + 1 if tc < 2 else 2 * tc - 1)

    def pe_tr3(te):
        if te <= NTC - 3:
            return pe_b3c + 2 * te + 4
        return pe_b3c + NTC + 1 + te

    @block.scalar
    def _(s):
        if phases < 5:
            return
        for tc in range(NTC):
            if tc == 0:
                s.wait_ge(s_dve.h, dve_3b_done)
            else:
                s.wait_ge(s_dve.h, dbase3(tc - 1) + 4)
            s.dma_start(out=xs3[:], in_=x_d[tc * 128:(tc + 1) * 128, :]) \
                .then_inc(s_rd.h, 16)
            s_rd.n += 16
            s.wait_ge(s_dve.h, dbase3(tc) + 2)
            s.activation(stat[:, 4:5], stat[:, 3:4], AF.Sqrt, bias=epsl[:])
            s.drain().then_inc(s_act.h, 1)
            s.wait_ge(s_dve.h, dbase3(tc) + 5)
            s.activation(stat[:, 4:5], stat[:, 3:4], AF.Sqrt, bias=epsl[:])
            s.drain().then_inc(s_act.h, 1)
            s.wait_ge(s_dve.h, dbase3(tc) + 6)
            if tc >= 2:
                s.wait_ge(s_pe.h, pe_tr3(tc - 2))
            s.activation(hf[tc % 2][:], x2sb[:], AF.Identity,
                         bias=stat[:, 6:7], scale=stat[:, 5:6])
            s.drain().then_inc(s_act.h, 1)
            s_act.n += 3

    @block.vector
    def _(v):
        if phases < 5:
            return
        for tc in range(NTC):
            v.wait_ge(s_pe.h, pe_mm3(tc))
            v.tensor_copy(attn_sb[tc % 2][:], pW[:])
            v.drain().then_inc(s_dve.h, 1)
            ln_stats(v, attn_sb[tc % 2][:])
            v.drain().then_inc(s_dve.h, 1)
            v.wait_ge(s_act.h, act_b3c + 3 * tc + 1)
            ln_finish_dve(v)
            v.drain().then_inc(s_dve.h, 1)
            v.wait_ge(s_rd.h, rd_b3c + 16 * (tc + 1))
            if tc >= 1:
                v.wait_ge(s_x2w.h, 16 * tc)
            v.tensor_scalar(xs3[:], xs3[:], stat[:, 6:7], None, ALU.add)
            v.drain()
            v.scalar_tensor_tensor(x2sb[:], attn_sb[tc % 2][:], stat[:, 5:6],
                                   xs3[:], ALU.mult, ALU.add)
            v.drain().then_inc(s_dve.h, 1)
            ln_stats(v, x2sb[:])
            v.drain().then_inc(s_dve.h, 1)
            v.wait_ge(s_act.h, act_b3c + 3 * tc + 2)
            ln_finish_dve(v)
            v.drain().then_inc(s_dve.h, 1)
            s_dve.n += 6
            if tc >= 2:
                te = tc - 2
                v.wait_ge(s_pe.h, pe_tr3(te))
                if te % 4 == 0 and te >= 4:
                    v.wait_ge(s_hfl.h, 16 * (te // 4))
                sv = pTh[te % 2][:].rearrange("p (k t) -> p k t", k=NKC)
                v.tensor_copy(hstg_v[:, :, (te % 4) * 128:(te % 4) * 128 + 128], sv)
                v.drain().then_inc(s_dve.h, 1)
                s_dve.n += 1
        for te in (NTC - 2, NTC - 1):
            v.wait_ge(s_pe.h, pe_tr3(te))
            if te % 4 == 0 and te >= 4:
                v.wait_ge(s_hfl.h, 16 * (te // 4))
            sv = pTh[te % 2][:].rearrange("p (k t) -> p k t", k=NKC)
            v.tensor_copy(hstg_v[:, :, (te % 4) * 128:(te % 4) * 128 + 128], sv)
            v.drain().then_inc(s_dve.h, 1)
            s_dve.n += 1

    @block.tensor
    def _(t):
        if phases < 5:
            return
        t.wait_ge(s_wos.h, 16)
        t.wait_ge(s_dve.h, dve_3b_done)
        for tc in range(NTC):
            if tc >= 1:
                t.wait_ge(s_dve.h, dbase3(tc - 1) + 1)
            for hk in range(NKC):
                for tg in range(4):
                    mm = t.matmul(pW[:, tg * 512:(tg + 1) * 512],
                                  AB_v[:, hk, tc * 128:(tc + 1) * 128],
                                  wo_v[:, hk, tg * 512:(tg + 1) * 512],
                                  start=(hk == 0), stop=(hk == NKC - 1))
            mm.then_inc(s_pe.h, 1)
            s_pe.n += 1
            if tc >= 2:
                te = tc - 2
                t.wait_ge(s_act.h, act_b3c + 3 * te + 3)
                if te >= 2:
                    t.wait_ge(s_dve.h, hevac_val(te - 2))
                for k in range(NKC):
                    mm = t.transpose(pTh[te % 2][:, k * 128:(k + 1) * 128],
                                     hf[te % 2][:, k * 128:(k + 1) * 128], idb[:])
                mm.then_inc(s_pe.h, 1)
                s_pe.n += 1
        for te in (NTC - 2, NTC - 1):
            t.wait_ge(s_act.h, act_b3c + 3 * te + 3)
            t.wait_ge(s_dve.h, hevac_val(te - 2))
            for k in range(NKC):
                mm = t.transpose(pTh[te % 2][:, k * 128:(k + 1) * 128],
                                 hf[te % 2][:, k * 128:(k + 1) * 128], idb[:])
            mm.then_inc(s_pe.h, 1)
            s_pe.n += 1

    @block.sync
    def _(sync):
        if phases < 5:
            return
        for tc in range(NTC):
            sync.wait_ge(s_dve.h, dbase3(tc) + 4)
            sync.dma_start(out=sp_x2[tc * 128:(tc + 1) * 128, :], in_=x2sb[:]) \
                .then_inc(s_x2w.h, 16)
            s_x2w.n += 16

    @block.gpsimd
    def _(g_):
        if phases < 5:
            return
        for grp in range(NTC // 4):
            g_.wait_ge(s_dve.h, hevac_val(4 * grp + 3))
            g_.dma_start(out=sp_h_v[:, :, grp * 512:(grp + 1) * 512],
                         in_=hstg_v[:, :, :]).then_inc(s_hfl.h, 16)
            s_hfl.n += 16
    dve_3c_done = dve_b3c + 12 + 7 * (NTC - 2) + 2

    S3c.close()
    S3w.close()
    S5 = ExitStack()
    pGU = [S5.enter_context(nc.psum_tensor(f"pGU{i}", [128, 2048], F32)) for i in range(2)]
    hh = S5.enter_context(nc.sbuf_tensor("hh", [128, NKC * 1024], BF16))
    dacc = S5.enter_context(nc.sbuf_tensor("dacc", [128, 8 * 2048], F32))
    gtd = S5.enter_context(nc.sbuf_tensor("gtd", [128, GFF * 1024], BF16))
    wgs = [S5.enter_context(nc.sbuf_tensor(f"wgs{i}", [128, 2048], BF16)) for i in range(2)]
    wus = [S5.enter_context(nc.sbuf_tensor(f"wus{i}", [128, 2048], BF16)) for i in range(2)]
    wdg = S5.enter_context(nc.sbuf_tensor("wdg", [128, GFF * 2048], BF16))
    sil = S5.enter_context(nc.sbuf_tensor("sil", [128, 1024], F32))
    xf = S5.enter_context(nc.sbuf_tensor("xf", [128, DM], F32))
    outst = S5.enter_context(nc.sbuf_tensor("outst", [128, DM], F32))
    hh_v = hh[:].rearrange("p (k t) -> p k t", k=NKC)
    gtd_v = gtd[:].rearrange("p (m t) -> p m t", m=GFF)
    wdg_v = wdg[:].rearrange("p (m n) -> p m n", m=GFF)

    pe_b5 = s_pe.n
    act_b5 = s_act.n
    dve_b5 = s_dve.n
    HALF_PE = NGRP * (GFF + 8)

    def dve_gated(half, g, mi):
        if half == 0:
            return dve_b5 + g * 16 + mi + 1
        if g == 0:
            return dve_b5 + 128 + 4 * mi + 1
        return dve_b5 + 168 + 16 * (g - 1) + mi + 1

    def dve_wadd(half, g, w):
        if half == 0:
            return dve_b5 + g * 16 + 8 + w + 1
        if g == 0:
            return dve_b5 + 161 + w
        return dve_b5 + 168 + 16 * (g - 1) + 8 + w + 1

    def dve_ln4(half, w, i):
        if half == 0:
            return dve_b5 + 128 + 4 * w + 2 + i
        return dve_b5 + 280 + w * 3 + i + 1

    def act_sil(half, g, mi):
        if half == 0:
            return act_b5 + g * 8 + mi + 1
        if g == 0:
            return act_b5 + 64 + 2 * mi + 1
        return act_b5 + 80 + 8 * (g - 1) + mi + 1

    def act_ln4s(half, w):
        if half == 0:
            return act_b5 + 64 + 2 * w + 2
        return act_b5 + 136 + w + 1

    def pe_gu(half, g, mi):
        return pe_b5 + half * HALF_PE + g * (GFF + 8) + mi + 1

    def pe_dw(half, g, w):
        return pe_b5 + half * HALF_PE + g * (GFF + 8) + GFF + w + 1

    def cst_w5(half, g, mi):
        return 48 * (half * (NM // 2) + (g * GFF + mi) // 2 + 1)

    @block.sync
    def _(sync):
        if phases < 6:
            return
        for half in range(2):
            for w in range(8):
                tcg = half * 8 + w
                if tcg == 0:
                    sync.wait_ge(s_dve.h, dve_3c_done)
                    sync.wait_ge(s_hfl.h, 16 * (NTC // 4))
                    sync.wait_ge(s_x2w.h, 16 * NTC)
                else:
                    pv = dve_ln4(half, w - 1, 2) if w >= 1 else dve_ln4(0, 7, 2)
                    sync.wait_ge(s_dve.h, pv)
                sync.dma_start(out=xf[:], in_=sp_x2[tcg * 128:(tcg + 1) * 128, :]) \
                    .then_inc(s_gat.h, 16)
                s_gat.n += 16
                sync.wait_ge(s_dve.h, dve_ln4(half, w, 2))
                sync.dma_start(out=out_d[tcg * 128:(tcg + 1) * 128, :], in_=outst[:]) \
                    .then_inc(s_out.h, 16)
                s_out.n += 16
        sync.wait_ge(s_out.h, s_out.n)

    @block.gpsimd
    def _(g_):
        if phases < 6:
            return
        for half in range(2):
            for g in range(NGRP):
                for mi in range(GFF):
                    m = g * GFF + mi
                    sl = mi % 2
                    gi = g * GFF + mi
                    if half == 0 and gi == 0:
                        g_.wait_ge(s_dve.h, dve_3c_done)
                        g_.wait_ge(s_x2w.h, 16 * NTC)
                        g_.wait_ge(s_hfl.h, 64)
                    if half > 0 or gi >= 2:
                        if gi >= 2:
                            pg, pmi = divmod(gi - 2, GFF)
                            g_.wait_ge(s_dve.h, dve_gated(half, pg, pmi))
                        else:
                            g_.wait_ge(s_dve.h, dve_gated(half - 1, NGRP - 1, GFF - 2 + gi))
                    g_.dma_start(out=wgs[sl][:], in_=wg_d[m], max_dma_last_dim=8192) \
                        .then_inc(s_w5[sl].h, 16)
                    g_.dma_start(out=wus[sl][:], in_=wu_d[m], max_dma_last_dim=8192) \
                        .then_inc(s_w5[sl].h, 16)
                    if mi == 0 and (g >= 1 or half >= 1):
                        if g >= 1:
                            g_.wait_ge(s_pe.h, pe_dw(half, g - 1, 7))
                        else:
                            g_.wait_ge(s_pe.h, pe_dw(half - 1, NGRP - 1, 7))
                    g_.dma_start(out=wdg_v[:, mi, :], in_=wd_d[m],
                                 max_dma_last_dim=8192).then_inc(s_w5[sl].h, 16)

    def emit_silu(s, half, g, mi):
        s.wait_ge(s_pe.h, pe_gu(half, g, mi))
        gi = g * GFF + mi
        if half > 0 or gi >= 1:
            if gi >= 1:
                pg, pmi = divmod(gi - 1, GFF)
                s.wait_ge(s_dve.h, dve_gated(half, pg, pmi))
            else:
                s.wait_ge(s_dve.h, dve_gated(half - 1, NGRP - 1, GFF - 1))
        s.activation(sil[:], pGU[mi % 2][:, 0:1024], AF.Silu)
        s.drain().then_inc(s_act.h, 1)
        s_act.n += 1

    def emit_sqrt4(s, half, w):
        s.wait_ge(s_dve.h, dve_ln4(half, w, 0))
        s.activation(stat[:, 4:5], stat[:, 3:4], AF.Sqrt, bias=epsl[:])
        s.drain().then_inc(s_act.h, 1)
        s_act.n += 1

    @block.scalar
    def _(s):
        if phases < 6:
            return
        s.wait_ge(s_hfl.h, 64)
        s.dma_start(out=hh[:], in_=sp_h_v[:, :, 0:1024]).then_inc(s_hh.h, 16)
        s_hh.n += 16
        for g in range(NGRP):
            for mi in range(GFF):
                emit_silu(s, 0, g, mi)
        s.wait_ge(s_pe.h, pe_gu(0, NGRP - 1, GFF - 1))
        s.dma_start(out=hh[:], in_=sp_h_v[:, :, 1024:2048]).then_inc(s_hh.h, 16)
        s_hh.n += 16
        for mi in range(GFF):
            emit_silu(s, 1, 0, mi)
            emit_sqrt4(s, 0, mi)
        for g in range(1, NGRP):
            for mi in range(GFF):
                emit_silu(s, 1, g, mi)
        for w in range(8):
            emit_sqrt4(s, 1, w)

    @block.tensor
    def _(t):
        if phases < 6:
            return
        for half in range(2):
            t.wait_ge(s_hh.h, 16 * (half + 1))
            if half == 0:
                t.wait_ge(s_dve.h, dve_3c_done)
            for g in range(NGRP):
                for mi in range(GFF):
                    gi = g * GFF + mi
                    t.wait_ge(s_w5[mi % 2].h, cst_w5(half, g, mi))
                    if gi >= 2:
                        pg, pmi = divmod(gi - 2, GFF)
                        t.wait_ge(s_dve.h, dve_gated(half, pg, pmi))
                        if mi < 2 and g >= 1:
                            t.wait_ge(s_dve.h, dve_wadd(half, g - 1, 7))
                    elif half == 1:
                        t.wait_ge(s_dve.h, dve_wadd(0, NGRP - 1, 7))
                    wgv = wgs[mi % 2][:].rearrange("p (k n) -> p k n", k=NKC)
                    wuv = wus[mi % 2][:].rearrange("p (k n) -> p k n", k=NKC)
                    for k in range(NKC):
                        for tg in range(2):
                            mm = t.matmul(pGU[mi % 2][:, tg * 512:(tg + 1) * 512],
                                          wgv[:, k, :],
                                          hh_v[:, k, tg * 512:(tg + 1) * 512],
                                          start=(k == 0), stop=(k == NKC - 1))
                        for tg in range(2):
                            mm = t.matmul(pGU[mi % 2][:, 1024 + tg * 512:1024 + (tg + 1) * 512],
                                          wuv[:, k, :],
                                          hh_v[:, k, tg * 512:(tg + 1) * 512],
                                          start=(k == 0), stop=(k == NKC - 1))
                    mm.then_inc(s_pe.h, 1)
                    s_pe.n += 1
                for w in range(8):
                    t.wait_ge(s_dve.h, dve_gated(half, g, GFF - 1))
                    if w >= 2:
                        t.wait_ge(s_dve.h, dve_wadd(half, g, w - 2))
                    elif g >= 1:
                        t.wait_ge(s_dve.h, dve_wadd(half, g - 1, 7))
                    elif half == 1:
                        t.wait_ge(s_dve.h, dve_wadd(0, NGRP - 1, 7))
                    for mi in range(GFF):
                        for ng in range(4):
                            mm = t.matmul(pGU[w % 2][:, ng * 512:(ng + 1) * 512],
                                          gtd_v[:, mi, w * 128:(w + 1) * 128],
                                          wdg_v[:, mi, ng * 512:(ng + 1) * 512],
                                          start=(mi == 0), stop=(mi == GFF - 1))
                    mm.then_inc(s_pe.h, 1)
                    s_pe.n += 1

    def emit_gated(v, half, g, mi):
        v.wait_ge(s_act.h, act_sil(half, g, mi))
        v.tensor_tensor(gtd_v[:, mi, :], sil[:],
                        pGU[mi % 2][:, 1024:2048], ALU.mult)
        v.drain().then_inc(s_dve.h, 1)
        s_dve.n += 1

    def emit_wadd(v, half, g, w):
        v.wait_ge(s_pe.h, pe_dw(half, g, w))
        if g == 0:
            v.tensor_copy(dacc[:, w * 2048:(w + 1) * 2048], pGU[w % 2][:])
        else:
            v.tensor_tensor(dacc[:, w * 2048:(w + 1) * 2048],
                            dacc[:, w * 2048:(w + 1) * 2048],
                            pGU[w % 2][:], ALU.add)
        v.drain().then_inc(s_dve.h, 1)
        s_dve.n += 1

    def emit_ln4(v, half, w):
        ln_stats(v, dacc[:, w * 2048:(w + 1) * 2048])
        v.drain().then_inc(s_dve.h, 1)
        v.wait_ge(s_act.h, act_ln4s(half, w))
        ln_finish_dve(v)
        v.drain().then_inc(s_dve.h, 1)
        v.wait_ge(s_gat.h, 16 + 16 * (half * 8 + w + 1))
        v.wait_ge(s_out.h, 16 * (half * 8 + w))
        v.tensor_scalar(xf[:], xf[:], stat[:, 6:7], None, ALU.add)
        v.drain()
        v.scalar_tensor_tensor(outst[:], dacc[:, w * 2048:(w + 1) * 2048],
                               stat[:, 5:6], xf[:], ALU.mult, ALU.add)
        v.drain().then_inc(s_dve.h, 1)
        s_dve.n += 3

    @block.vector
    def _(v):
        if phases < 6:
            return
        for g in range(NGRP):
            for mi in range(GFF):
                emit_gated(v, 0, g, mi)
            for w in range(8):
                emit_wadd(v, 0, g, w)
        for mi in range(GFF):
            emit_gated(v, 1, 0, mi)
            emit_ln4(v, 0, mi)
        for w in range(8):
            emit_wadd(v, 1, 0, w)
        for g in range(1, NGRP):
            for mi in range(GFF):
                emit_gated(v, 1, g, mi)
            for w in range(8):
                emit_wadd(v, 1, g, w)
        for w in range(8):
            emit_ln4(v, 1, w)

    if debug:
        @block.sync
        def _(sync):
            if 1 <= phases <= 2:
                sync.wait_ge(s_dve.h, dve_xnf)
                sync.dma_start(out=dbg_xnf, in_=xnf[:]).then_inc(s_out.h, 16)
                s_out.n += 16
            if phases >= 3:
                sync.wait_ge(s_dve.h, dve_car)
                sync.dma_start(out=dbg_carry, in_=carry_cols[:]).then_inc(s_out.h, 16)
                s_out.n += 16
            if 4 <= phases <= 5:
                sync.wait_ge(s_dve.h, dve_3b_done)
                sync.dma_start(out=dbg_ab, in_=AB[:]).then_inc(s_out.h, 16)
                s_out.n += 16
            sync.wait_ge(s_sp.h, s_sp.n)
            sync.wait_ge(s_x2w.h, s_x2w.n)
            sync.wait_ge(s_hfl.h, s_hfl.n)
            sync.wait_ge(s_out.h, s_out.n)

    S5.close()
    ES.close()
    return nc


def _prep_inputs(L0_attn_x, L0_attn_w_qkv, L0_attn_w_out, L0_attn_decay_param,
                 L0_ffn_w_gate, L0_ffn_w_up, L0_ffn_w_down):
    x = np.ascontiguousarray(np.asarray(L0_attn_x, dtype=np.float32).reshape(B * TSEQ, DM))
    w_qkv = np.asarray(L0_attn_w_qkv, dtype=np.float32)
    w_out = np.asarray(L0_attn_w_out, dtype=np.float32)
    decay = np.ascontiguousarray(np.asarray(L0_attn_decay_param, dtype=np.float32).reshape(1, H))
    w_gate = np.asarray(L0_ffn_w_gate, dtype=np.float32)
    w_up = np.asarray(L0_ffn_w_up, dtype=np.float32)
    w_down = np.asarray(L0_ffn_w_down, dtype=np.float32)

    wq_l = np.ascontiguousarray(
        w_qkv.reshape(NKC, 128, 3, 2, 8, 128).transpose(2, 3, 1, 0, 4, 5)
        .reshape(3, 2, 128, 16384))
    wo_l = np.ascontiguousarray(
        w_out.reshape(NKC, 128, DM).transpose(1, 0, 2).reshape(128, NKC * DM))
    wg_l = np.ascontiguousarray(
        w_gate.T.reshape(NKC, 128, NM, 128).transpose(2, 1, 0, 3)
        .reshape(NM, 128, NKC * 128))
    wu_l = np.ascontiguousarray(
        w_up.T.reshape(NKC, 128, NM, 128).transpose(2, 1, 0, 3)
        .reshape(NM, 128, NKC * 128))
    wd_l = np.ascontiguousarray(w_down.T.reshape(NM, 128, DM))

    ident = np.eye(128, dtype=np.float32)
    in_maps = []
    for c in range(NCORES):
        sel = np.zeros((128, NCORES * 128), dtype=np.float32)
        if c % 2 == 1:
            sel[:, (c - 1) * 128:c * 128] = ident
        in_maps.append({
            "x": np.ascontiguousarray(x[c * T:(c + 1) * T, :]),
            "w_qkv": wq_l, "w_out": wo_l, "w_gT": wg_l, "w_uT": wu_l,
            "w_dT": wd_l, "decay": decay, "sel": sel, "ident": ident,
        })
    return in_maps


def kernel(**inputs):
    in_maps = _prep_inputs(**inputs)
    nc = build_program()
    res = run_bass_kernel_spmd(nc, in_maps, core_ids=list(range(NCORES)))
    out = np.concatenate([np.asarray(res.results[c]["out"]) for c in range(NCORES)],
                         axis=0)
    return out.reshape(B, TSEQ, DM).astype(np.float32)


def run_dev(inputs, phases=9, debug=True):
    in_maps = _prep_inputs(**inputs)
    nc = build_program(debug=debug, phases=phases)
    res = run_bass_kernel_spmd(nc, in_maps, core_ids=list(range(NCORES)))
    return res.results


# version 08
# speedup vs baseline: 1.2614x; 1.1009x over previous
import numpy as np
import ml_dtypes
from contextlib import ExitStack

import concourse.bass as bass
import concourse.mybir as mybir
from concourse.bass_utils import run_bass_kernel_spmd

F32 = mybir.dt.float32
F32R = mybir.dt.float32r
BF16 = mybir.dt.bfloat16
AF = mybir.ActivationFunctionType
ALU = mybir.AluOpType
AX = mybir.AxisListType

NCORES = 8
B, TSEQ, DM = 4, 4096, 2048
T = 2048
H = 16
NKC = 16
NTC = 16
HID = 8192
NM = 64
GFF = 8
NGRP = NM // GFF
EPS_LN = 1e-5
EPS_DEN = 1e-6


class Sem:
    def __init__(self, es, nc, name):
        self.h = es.enter_context(nc.semaphore(name))
        self.n = 0


def bcast_col(ap_col, n):
    row = ap_col.ap[0][0]
    return bass.AP(ap_col.tensor, ap_col.offset, [[row, 128], [0, n]])


def build_program(debug=False, phases=9):
    nc = bass.Bass("TRN2", target_bir_lowering=False)

    x_d = nc.dram_tensor("x", [T, DM], F32, kind="ExternalInput").ap()
    wqkv_d = nc.dram_tensor("w_qkv", [3, 2, 128, 16384], F32, kind="ExternalInput").ap()
    wout_d = nc.dram_tensor("w_out", [128, NKC * DM], F32, kind="ExternalInput").ap()
    wg_d = nc.dram_tensor("w_gT", [NM, 128, NKC * 128], F32, kind="ExternalInput").ap()
    wu_d = nc.dram_tensor("w_uT", [NM, 128, NKC * 128], F32, kind="ExternalInput").ap()
    wd_d = nc.dram_tensor("w_dT", [NM, 128, DM], F32, kind="ExternalInput").ap()
    dec_d = nc.dram_tensor("decay", [1, H], F32, kind="ExternalInput").ap()
    sel_d = nc.dram_tensor("sel", [128, NCORES * 128], F32, kind="ExternalInput").ap()
    id_d = nc.dram_tensor("ident", [128, 128], F32, kind="ExternalInput").ap()
    out_d = nc.dram_tensor("out", [T, DM], F32, kind="ExternalOutput").ap()

    spk = "ExternalOutput" if debug else "Internal"
    sp_q = nc.dram_tensor("sp_q", [DM, T], F32, kind=spk).ap()
    sp_k = nc.dram_tensor("sp_k", [DM, T], F32, kind=spk).ap()
    sp_kv = nc.dram_tensor("sp_kv", [DM, T], F32, kind=spk).ap()
    sp_x2 = nc.dram_tensor("sp_x2", [T, DM], F32, kind=spk).ap()
    sp_h = nc.dram_tensor("sp_h", [DM, T], BF16, kind=spk).ap()
    if debug:
        dbg_xnf = nc.dram_tensor("dbg_xnf", [128, NKC * 2048], BF16, kind="ExternalOutput").ap()
        dbg_ab = nc.dram_tensor("dbg_ab", [128, NKC * 2048], BF16, kind="ExternalOutput").ap()
        dbg_carry = nc.dram_tensor("dbg_carry", [128, 32], F32, kind="ExternalOutput").ap()
    cc_in_t = nc.dram_tensor("cc_in", [128, 32], F32)
    cc_out_t = nc.dram_tensor("cc_out", [NCORES * 128, 32], F32)
    cc_in = cc_in_t.ap()
    cc_out = cc_out_t.ap()

    ES = ExitStack()

    idb = ES.enter_context(nc.sbuf_tensor("idb", [128, 128], BF16))
    idf = ES.enter_context(nc.sbuf_tensor("idf", [128, 128], F32))
    dsb = ES.enter_context(nc.sbuf_tensor("dsb", [1, H], F32))
    dcol = ES.enter_context(nc.sbuf_tensor("dcol", [128, H], F32))
    sel_sb = ES.enter_context(nc.sbuf_tensor("sel_sb", [128, NCORES * 128], F32))
    gat = ES.enter_context(nc.sbuf_tensor("gat", [128, NCORES * 32], F32))
    ones_c = ES.enter_context(nc.sbuf_tensor("ones_c", [128, 1], BF16))
    ones_r = ES.enter_context(nc.sbuf_tensor("ones_r", [1, 128], F32R))
    ones_f = ES.enter_context(nc.sbuf_tensor("ones_f", [1, 128], F32))
    carry_acc = ES.enter_context(nc.sbuf_tensor("carry_acc", [128, 32], F32))
    carry_cols = ES.enter_context(nc.sbuf_tensor("carry_cols", [128, 32], F32))
    stat = ES.enter_context(nc.sbuf_tensor("stat", [128, 8], F32))
    bnb = ES.enter_context(nc.sbuf_tensor("bnb", [128, 24], F32))
    epsl = ES.enter_context(nc.sbuf_tensor("epsl", [128, 1], F32))

    sem = lambda name: Sem(ES, nc, name)
    s_ld = sem("s_ld")
    s_idb = sem("s_idb")
    s_wq2 = [sem("s_wq20"), sem("s_wq21")]
    s_wos = sem("s_wos")
    s_x = [sem("s_x0")]
    s_sp = sem("s_sp")
    s_rd = sem("s_rd")
    s_rdk = sem("s_rdk")
    s_rdq = sem("s_rdq")
    s_ccw = sem("s_ccw")
    s_ccd = sem("s_ccd")
    s_gat = sem("s_gat")
    s_x2w = sem("s_x2w")
    s_hfl = sem("s_hfl")
    s_hh = sem("s_hh")
    s_out = sem("s_out")
    s_w5 = [sem("s_w50"), sem("s_w51")]
    s_pe = sem("s_pe")
    s_dve = sem("s_dve")
    s_act = sem("s_act")
    s_pool = sem("s_pool")

    block = ES.enter_context(nc.Block())

    def ln_stats(v, src_v, n=2048):
        for i in range(n // 512):
            v.bn_stats(bnb[:, 6 * i:6 * i + 6], src_v[:, i * 512:(i + 1) * 512])
        v.drain()
        v.bn_aggr(stat[:, 2:4], bnb[:, 0:6 * (n // 512)])
        v.drain()

    def ln_finish_dve(v):
        v.reciprocal(stat[:, 5:6], stat[:, 4:5])
        v.drain()
        v.scalar_tensor_tensor(stat[:, 6:7], stat[:, 2:3], -1.0,
                               stat[:, 5:6], ALU.mult, ALU.mult)
        v.drain()

    S12 = ExitStack()
    xnf = S12.enter_context(nc.sbuf_tensor("xnf", [128, NKC * 2048], BF16))
    wq2 = [S12.enter_context(nc.sbuf_tensor(f"wq2_{i}", [128, 16384], BF16)) for i in range(2)]
    stg = [S12.enter_context(nc.sbuf_tensor(f"stg{i}", [128, 2048], F32)) for i in range(2)]
    e_st = S12.enter_context(nc.sbuf_tensor("e_st", [128, 2048], F32))
    r_st = S12.enter_context(nc.sbuf_tensor("r_st", [128, 2048], F32))
    kre = S12.enter_context(nc.sbuf_tensor("kre", [128, 2048], F32))
    S_scr = S12.enter_context(nc.sbuf_tensor("S_scr", [128, 2048], F32))
    S1 = ExitStack()
    xs = S1.enter_context(nc.sbuf_tensor("xs", [128, DM], F32))
    xn_tm = [S1.enter_context(nc.sbuf_tensor(f"xn_tm{i}", [128, DM], BF16)) for i in range(2)]

    P01 = ExitStack()
    pS = P01.enter_context(nc.psum_tensor("pS", [128, 512], F32))
    pT = [P01.enter_context(nc.psum_tensor(f"pT{i}", [128, 2048], BF16)) for i in range(2)]

    xnf_v = xnf[:].rearrange("p (k t) -> p k t", k=NKC)
    wq2_v = [w[:].rearrange("p (k m c) -> p k m c", k=NKC, m=8) for w in wq2]

    @block.sync
    def _(sync):
        sync.dma_start(out=idf[:], in_=id_d[:]).then_inc(s_ld.h, 16)
        sync.dma_start(out=dsb[:], in_=dec_d[:]).then_inc(s_ld.h, 16)
        sync.dma_start(out=sel_sb[:], in_=sel_d[:]).then_inc(s_ld.h, 16)
    s_ld.n = 48

    @block.gpsimd
    def _(g):
        g.dma_start(out=idb[:], in_=id_d[:]).then_inc(s_idb.h, 16)
    s_idb.n = 16

    @block.vector
    def _(v):
        v.memset(ones_c[:], 1.0)
        v.memset(ones_f[:], 1.0)
        v.memset(carry_acc[:], 0.0)
        v.memset(epsl[:], EPS_LN)
        v.drain().then_inc(s_dve.h, 1)
    s_dve.n += 1
    dve_consts = s_dve.n

    @block.scalar
    def _(s):
        s.wait_ge(s_ld.h, 48)
        s.activation(dsb[:], dsb[:], AF.Sigmoid)
        s.wait_ge(s_dve.h, dve_consts)
        s.activation(ones_r[:], ones_f[:], AF.Identity)
        s.drain().then_inc(s_act.h, 1)
    s_act.n += 1
    act_sig = s_act.n

    @block.tensor
    def _(t):
        t.wait_ge(s_act.h, act_sig)
        t.matmul(pS[:, 0:H], ones_f[:], dsb[:], start=True, stop=True).then_inc(s_pe.h, 1)
    s_pe.n += 1
    pe_dcol = s_pe.n

    @block.vector
    def _(v):
        v.wait_ge(s_pe.h, pe_dcol)
        v.tensor_copy(dcol[:], pS[:, 0:H])
        v.drain().then_inc(s_dve.h, 1)
    s_dve.n += 1

    act_b1 = s_act.n
    dve_b1 = s_dve.n
    pe_b1 = s_pe.n

    @block.sync
    def _(sync):
        if phases < 1:
            return
        for tc in range(NTC):
            if tc >= 1:
                sync.wait_ge(s_act.h, act_b1 + 2 * (tc - 1) + 2)
            sync.dma_start(out=xs[:], in_=x_d[tc * 128:(tc + 1) * 128, :]) \
                .then_inc(s_x[0].h, 16)
            s_x[0].n += 16

    @block.scalar
    def _(s):
        if phases < 1:
            return
        for tc in range(NTC):
            s.wait_ge(s_dve.h, dve_b1 + 3 * tc + 1)
            s.activation(stat[:, 4:5], stat[:, 3:4], AF.Sqrt, bias=epsl[:])
            s.drain().then_inc(s_act.h, 1)
            s.wait_ge(s_dve.h, dve_b1 + 3 * tc + 2)
            if tc >= 2:
                s.wait_ge(s_pe.h, pe_b1 + tc - 1)
            s.activation(xn_tm[tc % 2][:], xs[:], AF.Identity,
                         bias=stat[:, 6:7], scale=stat[:, 5:6])
            s.drain().then_inc(s_act.h, 1)
            s_act.n += 2

    @block.vector
    def _(v):
        if phases < 1:
            return
        for tc in range(NTC):
            v.wait_ge(s_x[0].h, 16 * (tc + 1))
            ln_stats(v, xs[:])
            v.drain().then_inc(s_dve.h, 1)
            v.wait_ge(s_act.h, act_b1 + 2 * tc + 1)
            ln_finish_dve(v)
            v.drain().then_inc(s_dve.h, 1)
            v.wait_ge(s_pe.h, pe_b1 + tc + 1)
            src = pT[tc % 2][:].rearrange("p (k t) -> p k t", k=NKC)
            v.tensor_copy(xnf_v[:, :, tc * 128:(tc + 1) * 128], src)
            v.drain().then_inc(s_dve.h, 1)
            s_dve.n += 3

    @block.tensor
    def _(t):
        if phases < 1:
            return
        t.wait_ge(s_idb.h, 16)
        for tc in range(NTC):
            t.wait_ge(s_act.h, act_b1 + 2 * tc + 2)
            if tc >= 2:
                t.wait_ge(s_dve.h, dve_b1 + 3 * (tc - 2) + 3)
            for k in range(NKC):
                mm = t.transpose(pT[tc % 2][:, k * 128:(k + 1) * 128],
                                 xn_tm[tc % 2][:, k * 128:(k + 1) * 128], idb[:])
            mm.then_inc(s_pe.h, 1)
            s_pe.n += 1
    dve_xnf = dve_b1 + 3 * NTC

    P01.close()
    S1.close()
    P2 = ExitStack()
    pQ = [P2.enter_context(nc.psum_tensor(f"pQ{i}", [128, 2048], F32)) for i in range(2)]

    QUADS = [1, 2, 0]
    pe_b2 = s_pe.n
    act_b2 = s_act.n
    pool_b2 = s_pool.n

    def mmdone2(g):
        return pe_b2 + g + 1

    def v_tt(m):
        return dve_xnf + H + 2 * m + 1

    def v_scan(m):
        return dve_xnf + H + 2 * m + 2

    def psum2_free(g):
        qi, m = divmod(g, H)
        if qi == 0:
            return (s_act, act_b2 + m + 1)
        if qi == 1:
            return (s_dve, v_tt(m))
        return (s_act, act_b2 + H + m + 1)

    def stgdone2(g):
        qi, m = divmod(g, H)
        if qi == 0:
            return (s_dve, dve_xnf + m + 1)
        if qi == 1:
            return (s_dve, v_tt(m))
        return (s_dve, dve_xnf + 3 * H + m + 1)

    def stream_idx(g):
        return 2 * (g // H) + (g % H) // 8

    @block.gpsimd
    def _(g_):
        if phases < 2:
            return
        for idx in range(6):
            qi, mh = divmod(idx, 2)
            if idx >= 2:
                g_.wait_ge(s_pe.h, mmdone2(8 * (idx - 2) + 7))
            g_.dma_start(out=wq2[idx % 2][:], in_=wqkv_d[QUADS[qi], mh],
                         max_dma_last_dim=8192).then_inc(s_wq2[idx % 2].h, 16)
            s_wq2[idx % 2].n += 16

    @block.tensor
    def _(t):
        if phases < 2:
            return
        for qi in range(3):
            for m in range(H):
                g = qi * H + m
                idx = stream_idx(g)
                if m % 8 == 0:
                    t.wait_ge(s_wq2[idx % 2].h, 16 * (idx // 2 + 1))
                if g == 0:
                    t.wait_ge(s_dve.h, dve_xnf)
                if g >= 2:
                    fs, fv = psum2_free(g - 2)
                    t.wait_ge(fs.h, fv)
                wv = wq2_v[idx % 2]
                for k in range(NKC):
                    for tg in range(4):
                        mm = t.matmul(pQ[g % 2][:, tg * 512:(tg + 1) * 512],
                                      wv[:, k, m % 8, :],
                                      xnf_v[:, k, tg * 512:(tg + 1) * 512],
                                      start=(k == 0), stop=(k == NKC - 1))
                mm.then_inc(s_pe.h, 1)
                s_pe.n += 1

    @block.scalar
    def _(s):
        if phases < 2:
            return
        for m in range(H):
            s.wait_ge(s_pe.h, mmdone2(m))
            if m >= 1:
                s.wait_ge(s_dve.h, dve_xnf + m)
            s.activation(e_st[:], pQ[m % 2][:], AF.Exp)
            s.activation(r_st[:], pQ[m % 2][:], AF.Relu)
            s.drain().then_inc(s_act.h, 1)
            s_act.n += 1

    @block.scalar
    def _(s):
        if phases < 2:
            return
        for m in range(H):
            s.wait_ge(s_sp.h, 16 * H)
            if m >= 1:
                s.wait_ge(s_dve.h, v_scan(m - 1))
            s.dma_start(out=kre[:], in_=sp_k[m * 128:(m + 1) * 128, :]) \
                .then_inc(s_rd.h, 16)
            s_rd.n += 16

    @block.scalar
    def _(s):
        if phases < 2:
            return
        for m in range(H):
            g = 2 * H + m
            s.wait_ge(s_pe.h, mmdone2(g))
            if m == 0:
                s.wait_ge(s_dve.h, dve_xnf + H)
            else:
                s.wait_ge(s_dve.h, dve_xnf + 3 * H + m)
            s.activation(e_st[:], pQ[g % 2][:], AF.Exp)
            s.activation(r_st[:], pQ[g % 2][:], AF.Relu)
            s.drain().then_inc(s_act.h, 1)
            s_act.n += 1

    @block.vector
    def _(v):
        if phases < 2:
            return
        for m in range(H):
            v.wait_ge(s_act.h, act_b2 + m + 1)
            if m >= 2:
                v.wait_ge(s_sp.h, 16 * m)
            v.scalar_tensor_tensor(stg[m % 2][:], e_st[:], 1.0,
                                   r_st[:], ALU.min, ALU.add)
            v.drain().then_inc(s_dve.h, 1)
            s_dve.n += 1
        for m in range(H):
            g = H + m
            v.wait_ge(s_pe.h, mmdone2(g))
            v.wait_ge(s_rd.h, 16 * (m + 1))
            v.wait_ge(s_sp.h, 16 * (H + m))
            if m >= 2:
                v.wait_ge(s_dve.h, v_scan(m - 2))
            v.tensor_tensor(stg[m % 2][:], pQ[g % 2][:], kre[:], ALU.mult)
            v.drain().then_inc(s_dve.h, 1)
            dbc = bcast_col(dcol[:, m:m + 1], T)
            v.tensor_tensor_scan(S_scr[:], dbc, stg[m % 2][:], 0.0, ALU.mult, ALU.add)
            v.drain()
            v.tensor_copy(carry_acc[:, m:m + 1], S_scr[:, T - 1:T])
            v.drain()
            v.tensor_tensor_scan(S_scr[:], dbc, kre[:], 0.0, ALU.mult, ALU.add)
            v.drain()
            v.tensor_copy(carry_acc[:, H + m:H + m + 1], S_scr[:, T - 1:T])
            v.drain().then_inc(s_dve.h, 1)
            s_dve.n += 2
        for m in range(H):
            v.wait_ge(s_act.h, act_b2 + H + m + 1)
            v.wait_ge(s_sp.h, 16 * (2 * H + m))
            v.scalar_tensor_tensor(stg[m % 2][:], e_st[:], 1.0,
                                   r_st[:], ALU.min, ALU.add)
            v.drain().then_inc(s_dve.h, 1)
            s_dve.n += 1

    pool_all2 = s_pool.n

    spill_of = {}

    @block.sync
    def _(sync):
        if phases < 2:
            return
        targets = [sp_k, sp_kv, sp_q]
        for qi in range(3):
            for m in range(H):
                g = qi * H + m
                fs, fv = stgdone2(g)
                sync.wait_ge(fs.h, fv)
                sync.dma_start(out=targets[qi][m * 128:(m + 1) * 128, :],
                               in_=stg[m % 2][:]).then_inc(s_sp.h, 16)
                s_sp.n += 16
                spill_of[g] = s_sp.n
    sp_all2 = s_sp.n

    @block.sync
    def _(sync):
        if phases < 3:
            return
        sync.wait_ge(s_dve.h, v_scan(H - 1))
        sync.dma_start(out=cc_in, in_=carry_acc[:]).then_inc(s_ccw.h, 16)
        s_ccw.n += 16

    @block.gpsimd
    def _(g_):
        if phases < 3:
            return
        g_.wait_ge(s_ccw.h, 16)
        g_.collective_compute(
            "AllGather", ALU.bypass,
            replica_groups=[list(range(NCORES))],
            ins=[cc_in_t.ap().opt()], outs=[cc_out_t.ap().opt()],
        ).then_inc(s_ccd.h)
        s_ccd.n += 1

    @block.sync
    def _(sync):
        if phases < 3:
            return
        sync.wait_ge(s_ccd.h, 1)
        sync.dma_start(out=gat[:].rearrange("p (r j) -> p r j", r=NCORES),
                       in_=cc_out.rearrange("(r p) j -> p r j", p=128)) \
            .then_inc(s_gat.h, 16)
        s_gat.n += 16

    @block.tensor
    def _(t):
        if phases < 3:
            return
        t.wait_ge(s_gat.h, 16)
        for gg in (46, 47):
            fs, fv = psum2_free(gg)
            t.wait_ge(fs.h, fv)
        for r in range(NCORES):
            mm = t.matmul(pQ[0][:, 0:32], sel_sb[:, r * 128:(r + 1) * 128],
                          gat[:, r * 32:(r + 1) * 32],
                          start=(r == 0), stop=(r == NCORES - 1))
        mm.then_inc(s_pe.h, 1)
        s_pe.n += 1
    pe_sel = s_pe.n

    @block.vector
    def _(v):
        if phases < 3:
            return
        v.wait_ge(s_pe.h, pe_sel)
        v.tensor_copy(carry_cols[:], pQ[0][:, 0:32])
        v.drain().then_inc(s_dve.h, 1)
        s_dve.n += 1
    dve_car = s_dve.n

    S12.close()
    P2.close()

    S3w = ExitStack()
    AB = S3w.enter_context(nc.sbuf_tensor("AB", [128, NKC * 2048], BF16))
    wo_sb = S3w.enter_context(nc.sbuf_tensor("wo_sb", [128, NKC * 2048], BF16))
    AB_v = AB[:].rearrange("p (k t) -> p k t", k=NKC)
    wo_v = wo_sb[:].rearrange("p (k n) -> p k n", k=NKC)

    S3 = ExitStack()
    kv_sb = S3.enter_context(nc.sbuf_tensor("kv_sb", [128, T], F32))
    k_sb = S3.enter_context(nc.sbuf_tensor("k_sb", [128, T], F32))
    q_sb = S3.enter_context(nc.sbuf_tensor("q_sb", [128, T], F32))
    S_kv = [S3.enter_context(nc.sbuf_tensor(f"S_kv{i}", [128, T], F32)) for i in range(2)]
    S_ks = S3.enter_context(nc.sbuf_tensor("S_ks", [128, T], F32))
    pbf = S3.enter_context(nc.sbuf_tensor("pbf", [128, T], BF16))
    rows3 = S3.enter_context(nc.sbuf_tensor("rows3", [128, T], F32))
    rden_t = S3.enter_context(nc.sbuf_tensor("rden_t", [1, T], F32R))
    P3 = ExitStack()
    pBC = [P3.enter_context(nc.psum_tensor(f"pBC{i}", [128, 2048], F32)) for i in range(2)]

    @block.gpsimd
    def _(g):
        if phases < 4:
            return
        g.wait_ge(s_sp.h, sp_all2)
        g.dma_start(out=wo_sb[:], in_=wout_d[:], max_dma_last_dim=8192) \
            .then_inc(s_wos.h, 16)
        s_wos.n = 16

    dve_b3b = s_dve.n
    pool_b3b = s_pool.n
    act_b3b = s_act.n
    pe_b3b = s_pe.n
    rd_b3b = s_rd.n
    rdk_b3b = s_rdk.n
    rdq_b3b = s_rdq.n

    def scankv3(h):
        return dve_b3b + (1 if h == 0 else 4 * h - 1)

    def scanks3(h):
        return dve_b3b + (2 if h == 0 else 4 * h + 2)

    def clipv3b(h):
        return dve_b3b + (4 * (h + 1) if h <= H - 2 else 4 * H - 1)

    def outprev3b(h):
        return dve_b3b + (4 * (h + 1) + 1 if h <= H - 2 else 4 * H)

    @block.scalar
    def _(s):
        if phases < 4:
            return
        for h in range(H):
            if h >= 1:
                s.wait_ge(s_dve.h, scankv3(h - 1))
            else:
                s.wait_ge(s_dve.h, dve_car)
                s.wait_ge(s_pool.h, pool_all2)
                s.wait_ge(s_sp.h, sp_all2)
            s.wait_ge(s_sp.h, spill_of[2 * H + h])
            s.dma_start(out=kv_sb[:], in_=sp_kv[h * 128:(h + 1) * 128, :]) \
                .then_inc(s_rdk.h, 16)
            s_rdk.n += 16
            if h >= 1:
                s.wait_ge(s_dve.h, clipv3b(h - 1))
                s.activation(rows3[32:33, :], rows3[0:1, :], AF.Ln)
                s.drain()
                s.wait_ge(s_pe.h, pe_b3b + 2 * (h - 1))
                s.activation(rden_t[:], rows3[32:33, :], AF.Exp, scale=-1.0)
                s.drain().then_inc(s_act.h, 1)
                s_act.n += 1
        s.wait_ge(s_dve.h, clipv3b(H - 1))
        s.activation(rows3[32:33, :], rows3[0:1, :], AF.Ln)
        s.drain()
        s.wait_ge(s_pe.h, pe_b3b + 2 * (H - 1))
        s.activation(rden_t[:], rows3[32:33, :], AF.Exp, scale=-1.0)
        s.drain().then_inc(s_act.h, 1)
        s_act.n += 1

    @block.sync
    def _(sync):
        if phases < 4:
            return
        for h in range(H):
            if h >= 1:
                sync.wait_ge(s_dve.h, scanks3(h - 1))
            else:
                sync.wait_ge(s_dve.h, dve_car)
                sync.wait_ge(s_pool.h, pool_all2)
                sync.wait_ge(s_sp.h, sp_all2)
            sync.wait_ge(s_sp.h, spill_of[2 * H + h])
            sync.dma_start(out=k_sb[:], in_=sp_k[h * 128:(h + 1) * 128, :]) \
                .then_inc(s_rd.h, 16)
            if h >= 1:
                sync.wait_ge(s_pool.h, pool_b3b + 2 * (h - 1) + 2)
            sync.dma_start(out=q_sb[:], in_=sp_q[h * 128:(h + 1) * 128, :]) \
                .then_inc(s_rdq.h, 16)
            s_rd.n += 16
            s_rdq.n += 16

    @block.vector
    def _(v):
        if phases < 4:
            return
        for h in range(H):
            v.wait_ge(s_rdk.h, rdk_b3b + 16 * (h + 1))
            if h >= 2:
                v.wait_ge(s_pool.h, pool_b3b + 2 * (h - 2) + 2)
            dbc = bcast_col(dcol[:, h:h + 1], T)
            v.tensor_tensor_scan(S_kv[h % 2][:], dbc, kv_sb[:],
                                 carry_cols[:, h:h + 1], ALU.mult, ALU.add)
            v.drain().then_inc(s_dve.h, 1)
            s_dve.n += 1
            if h >= 1:
                v.wait_ge(s_pe.h, pe_b3b + 2 * (h - 1) + 1)
                if h >= 2:
                    v.wait_ge(s_act.h, act_b3b + h - 1)
                v.tensor_scalar(rows3[0:1, :], pBC[(h - 1) % 2][0:1, :],
                                EPS_DEN, None, ALU.max)
                v.drain().then_inc(s_dve.h, 1)
                v.wait_ge(s_pe.h, pe_b3b + 2 * (h - 1) + 2)
                v.wait_ge(s_pool.h, pool_b3b + 2 * (h - 1) + 2)
                v.tensor_tensor(AB_v[:, h - 1, :], S_ks[:], pBC[(h - 1) % 2][:, :],
                                ALU.mult)
                v.drain().then_inc(s_dve.h, 1)
                s_dve.n += 2
            v.wait_ge(s_rd.h, rd_b3b + 16 * (h + 1))
            v.tensor_tensor_scan(S_ks[:], dbc, k_sb[:],
                                 carry_cols[:, H + h:H + h + 1], ALU.mult, ALU.add)
            v.drain().then_inc(s_dve.h, 1)
            s_dve.n += 1
        v.wait_ge(s_pe.h, pe_b3b + 2 * (H - 1) + 1)
        v.wait_ge(s_act.h, act_b3b + H - 1)
        v.tensor_scalar(rows3[0:1, :], pBC[(H - 1) % 2][0:1, :],
                        EPS_DEN, None, ALU.max)
        v.drain().then_inc(s_dve.h, 1)
        v.wait_ge(s_pe.h, pe_b3b + 2 * (H - 1) + 2)
        v.wait_ge(s_pool.h, pool_b3b + 2 * H)
        v.tensor_tensor(AB_v[:, H - 1, :], S_ks[:], pBC[(H - 1) % 2][:, :],
                        ALU.mult)
        v.drain().then_inc(s_dve.h, 1)
        s_dve.n += 2

    @block.gpsimd
    def _(g):
        if phases < 4:
            return
        for h in range(H):
            g.wait_ge(s_dve.h, scanks3(h))
            g.wait_ge(s_rdq.h, rdq_b3b + 16 * (h + 1))
            if h >= 1:
                g.wait_ge(s_pe.h, pe_b3b + 2 * (h - 1) + 1)
            g.tensor_tensor(pbf[:], q_sb[:], S_ks[:], ALU.mult)
            g.drain().then_inc(s_pool.h, 1)
            g.tensor_tensor(S_ks[:], q_sb[:], S_kv[h % 2][:], ALU.mult)
            g.drain().then_inc(s_pool.h, 1)
            s_pool.n += 2

    @block.tensor
    def _(t):
        if phases < 4:
            return
        for h in range(H):
            t.wait_ge(s_pool.h, pool_b3b + 2 * h + 1)
            if h >= 2:
                t.wait_ge(s_dve.h, outprev3b(h - 2))
            for tg in range(4):
                mm = t.matmul(pBC[h % 2][0:1, tg * 512:(tg + 1) * 512], ones_c[:],
                              pbf[:, tg * 512:(tg + 1) * 512], start=True, stop=True)
            mm.then_inc(s_pe.h, 1)
            s_pe.n += 1
            t.wait_ge(s_act.h, act_b3b + h + 1)
            for tg in range(4):
                mm = t.matmul(pBC[h % 2][:, tg * 512:(tg + 1) * 512], ones_r[:],
                              rden_t[:, tg * 512:(tg + 1) * 512],
                              start=True, stop=True)
            mm.then_inc(s_pe.h, 1)
            s_pe.n += 1
    dve_3b_done = dve_b3b + 4 * H

    S3.close()
    P3.close()
    S3c = ExitStack()
    pW = S3c.enter_context(nc.psum_tensor("pW", [128, 2048], F32))
    pTh = [S3c.enter_context(nc.psum_tensor(f"pTh{i}", [128, 2048], BF16)) for i in range(2)]
    xs3 = S3c.enter_context(nc.sbuf_tensor("xs3", [128, DM], F32))
    x2sb = S3c.enter_context(nc.sbuf_tensor("x2sb", [128, DM], F32))
    attn_sb = [S3c.enter_context(nc.sbuf_tensor(f"attn_sb{i}", [128, DM], F32)) for i in range(2)]
    hf = [S3c.enter_context(nc.sbuf_tensor(f"hf{i}", [128, DM], BF16)) for i in range(2)]
    hstg = S3c.enter_context(nc.sbuf_tensor("hstg", [128, NKC * 512], BF16))
    hstg_v = hstg[:].rearrange("p (k t) -> p k t", k=NKC)
    sp_h_v = sp_h.rearrange("(k p) t -> p k t", p=128)

    pe_b3c = s_pe.n
    act_b3c = s_act.n
    dve_b3c = s_dve.n
    rd_b3c = s_rd.n

    def dbase3(tc):
        return dve_b3c + (6 * tc if tc < 2 else 12 + 7 * (tc - 2))

    def hevac_val(te):
        if te <= NTC - 3:
            return dbase3(te + 2) + 7
        return dve_b3c + 12 + 7 * (NTC - 2) + (te - (NTC - 2)) + 1

    def pe_mm3(tc):
        return pe_b3c + (tc + 1 if tc < 2 else 2 * tc - 1)

    def pe_tr3(te):
        if te <= NTC - 3:
            return pe_b3c + 2 * te + 4
        return pe_b3c + NTC + 1 + te

    @block.scalar
    def _(s):
        if phases < 5:
            return
        for tc in range(NTC):
            if tc == 0:
                s.wait_ge(s_dve.h, dve_3b_done)
            else:
                s.wait_ge(s_dve.h, dbase3(tc - 1) + 4)
            s.dma_start(out=xs3[:], in_=x_d[tc * 128:(tc + 1) * 128, :]) \
                .then_inc(s_rd.h, 16)
            s_rd.n += 16
            s.wait_ge(s_dve.h, dbase3(tc) + 2)
            s.activation(stat[:, 4:5], stat[:, 3:4], AF.Sqrt, bias=epsl[:])
            s.drain().then_inc(s_act.h, 1)
            s.wait_ge(s_dve.h, dbase3(tc) + 5)
            s.activation(stat[:, 4:5], stat[:, 3:4], AF.Sqrt, bias=epsl[:])
            s.drain().then_inc(s_act.h, 1)
            s.wait_ge(s_dve.h, dbase3(tc) + 6)
            if tc >= 2:
                s.wait_ge(s_pe.h, pe_tr3(tc - 2))
            s.activation(hf[tc % 2][:], x2sb[:], AF.Identity,
                         bias=stat[:, 6:7], scale=stat[:, 5:6])
            s.drain().then_inc(s_act.h, 1)
            s_act.n += 3

    @block.vector
    def _(v):
        if phases < 5:
            return
        for tc in range(NTC):
            v.wait_ge(s_pe.h, pe_mm3(tc))
            v.tensor_copy(attn_sb[tc % 2][:], pW[:])
            v.drain().then_inc(s_dve.h, 1)
            ln_stats(v, attn_sb[tc % 2][:])
            v.drain().then_inc(s_dve.h, 1)
            v.wait_ge(s_act.h, act_b3c + 3 * tc + 1)
            ln_finish_dve(v)
            v.drain().then_inc(s_dve.h, 1)
            v.wait_ge(s_rd.h, rd_b3c + 16 * (tc + 1))
            if tc >= 1:
                v.wait_ge(s_x2w.h, 16 * tc)
            v.tensor_scalar(xs3[:], xs3[:], stat[:, 6:7], None, ALU.add)
            v.drain()
            v.scalar_tensor_tensor(x2sb[:], attn_sb[tc % 2][:], stat[:, 5:6],
                                   xs3[:], ALU.mult, ALU.add)
            v.drain().then_inc(s_dve.h, 1)
            ln_stats(v, x2sb[:])
            v.drain().then_inc(s_dve.h, 1)
            v.wait_ge(s_act.h, act_b3c + 3 * tc + 2)
            ln_finish_dve(v)
            v.drain().then_inc(s_dve.h, 1)
            s_dve.n += 6
            if tc >= 2:
                te = tc - 2
                v.wait_ge(s_pe.h, pe_tr3(te))
                if te % 4 == 0 and te >= 4:
                    v.wait_ge(s_hfl.h, 16 * (te // 4))
                sv = pTh[te % 2][:].rearrange("p (k t) -> p k t", k=NKC)
                v.tensor_copy(hstg_v[:, :, (te % 4) * 128:(te % 4) * 128 + 128], sv)
                v.drain().then_inc(s_dve.h, 1)
                s_dve.n += 1
        for te in (NTC - 2, NTC - 1):
            v.wait_ge(s_pe.h, pe_tr3(te))
            if te % 4 == 0 and te >= 4:
                v.wait_ge(s_hfl.h, 16 * (te // 4))
            sv = pTh[te % 2][:].rearrange("p (k t) -> p k t", k=NKC)
            v.tensor_copy(hstg_v[:, :, (te % 4) * 128:(te % 4) * 128 + 128], sv)
            v.drain().then_inc(s_dve.h, 1)
            s_dve.n += 1

    @block.tensor
    def _(t):
        if phases < 5:
            return
        t.wait_ge(s_wos.h, 16)
        t.wait_ge(s_dve.h, dve_3b_done)
        for tc in range(NTC):
            if tc >= 1:
                t.wait_ge(s_dve.h, dbase3(tc - 1) + 1)
            for hk in range(NKC):
                for tg in range(4):
                    mm = t.matmul(pW[:, tg * 512:(tg + 1) * 512],
                                  AB_v[:, hk, tc * 128:(tc + 1) * 128],
                                  wo_v[:, hk, tg * 512:(tg + 1) * 512],
                                  start=(hk == 0), stop=(hk == NKC - 1))
            mm.then_inc(s_pe.h, 1)
            s_pe.n += 1
            if tc >= 2:
                te = tc - 2
                t.wait_ge(s_act.h, act_b3c + 3 * te + 3)
                if te >= 2:
                    t.wait_ge(s_dve.h, hevac_val(te - 2))
                for k in range(NKC):
                    mm = t.transpose(pTh[te % 2][:, k * 128:(k + 1) * 128],
                                     hf[te % 2][:, k * 128:(k + 1) * 128], idb[:])
                mm.then_inc(s_pe.h, 1)
                s_pe.n += 1
        for te in (NTC - 2, NTC - 1):
            t.wait_ge(s_act.h, act_b3c + 3 * te + 3)
            t.wait_ge(s_dve.h, hevac_val(te - 2))
            for k in range(NKC):
                mm = t.transpose(pTh[te % 2][:, k * 128:(k + 1) * 128],
                                 hf[te % 2][:, k * 128:(k + 1) * 128], idb[:])
            mm.then_inc(s_pe.h, 1)
            s_pe.n += 1

    @block.sync
    def _(sync):
        if phases < 5:
            return
        for tc in range(NTC):
            sync.wait_ge(s_dve.h, dbase3(tc) + 4)
            sync.dma_start(out=sp_x2[tc * 128:(tc + 1) * 128, :], in_=x2sb[:]) \
                .then_inc(s_x2w.h, 16)
            s_x2w.n += 16

    @block.gpsimd
    def _(g_):
        if phases < 5:
            return
        for grp in range(NTC // 4):
            g_.wait_ge(s_dve.h, hevac_val(4 * grp + 3))
            g_.dma_start(out=sp_h_v[:, :, grp * 512:(grp + 1) * 512],
                         in_=hstg_v[:, :, :]).then_inc(s_hfl.h, 16)
            s_hfl.n += 16
    dve_3c_done = dve_b3c + 12 + 7 * (NTC - 2) + 2

    S3c.close()
    S3w.close()
    S5 = ExitStack()
    pGU = [S5.enter_context(nc.psum_tensor(f"pGU{i}", [128, 2048], F32)) for i in range(2)]
    hh = S5.enter_context(nc.sbuf_tensor("hh", [128, NKC * 1024], BF16))
    dacc = S5.enter_context(nc.sbuf_tensor("dacc", [128, 8 * 2048], F32))
    gtd = S5.enter_context(nc.sbuf_tensor("gtd", [128, GFF * 1024], BF16))
    wgs = [S5.enter_context(nc.sbuf_tensor(f"wgs{i}", [128, 2048], BF16)) for i in range(2)]
    wus = [S5.enter_context(nc.sbuf_tensor(f"wus{i}", [128, 2048], BF16)) for i in range(2)]
    wdg = S5.enter_context(nc.sbuf_tensor("wdg", [128, GFF * 2048], BF16))
    sil = S5.enter_context(nc.sbuf_tensor("sil", [128, 1024], F32))
    xf = S5.enter_context(nc.sbuf_tensor("xf", [128, DM], F32))
    outst = S5.enter_context(nc.sbuf_tensor("outst", [128, DM], F32))
    hh_v = hh[:].rearrange("p (k t) -> p k t", k=NKC)
    gtd_v = gtd[:].rearrange("p (m t) -> p m t", m=GFF)
    wdg_v = wdg[:].rearrange("p (m n) -> p m n", m=GFF)

    pe_b5 = s_pe.n
    act_b5 = s_act.n
    dve_b5 = s_dve.n
    HALF_PE = NGRP * (GFF + 8)

    def dve_gated(half, g, mi):
        if half == 0:
            return dve_b5 + g * 16 + mi + 1
        if g == 0:
            return dve_b5 + 128 + 4 * mi + 1
        return dve_b5 + 168 + 16 * (g - 1) + mi + 1

    def dve_wadd(half, g, w):
        if half == 0:
            return dve_b5 + g * 16 + 8 + w + 1
        if g == 0:
            return dve_b5 + 161 + w
        return dve_b5 + 168 + 16 * (g - 1) + 8 + w + 1

    def dve_ln4(half, w, i):
        if half == 0:
            return dve_b5 + 128 + 4 * w + 2 + i
        return dve_b5 + 280 + w * 3 + i + 1

    def act_sil(half, g, mi):
        if half == 0:
            return act_b5 + g * 8 + mi + 1
        if g == 0:
            return act_b5 + 64 + 2 * mi + 1
        return act_b5 + 80 + 8 * (g - 1) + mi + 1

    def act_ln4s(half, w):
        if half == 0:
            return act_b5 + 64 + 2 * w + 2
        return act_b5 + 136 + w + 1

    def pe_gu(half, g, mi):
        return pe_b5 + half * HALF_PE + g * (GFF + 8) + mi + 1

    def pe_dw(half, g, w):
        return pe_b5 + half * HALF_PE + g * (GFF + 8) + GFF + w + 1

    def cst_w5(half, g, mi):
        return 48 * (half * (NM // 2) + (g * GFF + mi) // 2 + 1)

    @block.sync
    def _(sync):
        if phases < 6:
            return
        for half in range(2):
            for w in range(8):
                tcg = half * 8 + w
                if tcg == 0:
                    sync.wait_ge(s_dve.h, dve_3c_done)
                    sync.wait_ge(s_hfl.h, 16 * (NTC // 4))
                    sync.wait_ge(s_x2w.h, 16 * NTC)
                else:
                    pv = dve_ln4(half, w - 1, 2) if w >= 1 else dve_ln4(0, 7, 2)
                    sync.wait_ge(s_dve.h, pv)
                sync.dma_start(out=xf[:], in_=sp_x2[tcg * 128:(tcg + 1) * 128, :]) \
                    .then_inc(s_gat.h, 16)
                s_gat.n += 16
                sync.wait_ge(s_dve.h, dve_ln4(half, w, 2))
                sync.dma_start(out=out_d[tcg * 128:(tcg + 1) * 128, :], in_=outst[:]) \
                    .then_inc(s_out.h, 16)
                s_out.n += 16
        sync.wait_ge(s_out.h, s_out.n)

    @block.gpsimd
    def _(g_):
        if phases < 6:
            return
        for half in range(2):
            for g in range(NGRP):
                for mi in range(GFF):
                    m = g * GFF + mi
                    sl = mi % 2
                    gi = g * GFF + mi
                    if half == 0 and gi == 0:
                        g_.wait_ge(s_dve.h, dve_3c_done)
                        g_.wait_ge(s_x2w.h, 16 * NTC)
                        g_.wait_ge(s_hfl.h, 64)
                    if half > 0 or gi >= 2:
                        if gi >= 2:
                            pg, pmi = divmod(gi - 2, GFF)
                            g_.wait_ge(s_dve.h, dve_gated(half, pg, pmi))
                        else:
                            g_.wait_ge(s_dve.h, dve_gated(half - 1, NGRP - 1, GFF - 2 + gi))
                    g_.dma_start(out=wgs[sl][:], in_=wg_d[m], max_dma_last_dim=8192) \
                        .then_inc(s_w5[sl].h, 16)
                    g_.dma_start(out=wus[sl][:], in_=wu_d[m], max_dma_last_dim=8192) \
                        .then_inc(s_w5[sl].h, 16)
                    if mi == 0 and (g >= 1 or half >= 1):
                        if g >= 1:
                            g_.wait_ge(s_pe.h, pe_dw(half, g - 1, 7))
                        else:
                            g_.wait_ge(s_pe.h, pe_dw(half - 1, NGRP - 1, 7))
                    g_.dma_start(out=wdg_v[:, mi, :], in_=wd_d[m],
                                 max_dma_last_dim=8192).then_inc(s_w5[sl].h, 16)

    def emit_silu(s, half, g, mi):
        s.wait_ge(s_pe.h, pe_gu(half, g, mi))
        gi = g * GFF + mi
        if half > 0 or gi >= 1:
            if gi >= 1:
                pg, pmi = divmod(gi - 1, GFF)
                s.wait_ge(s_dve.h, dve_gated(half, pg, pmi))
            else:
                s.wait_ge(s_dve.h, dve_gated(half - 1, NGRP - 1, GFF - 1))
        s.activation(sil[:], pGU[mi % 2][:, 0:1024], AF.Silu)
        s.drain().then_inc(s_act.h, 1)
        s_act.n += 1

    def emit_sqrt4(s, half, w):
        s.wait_ge(s_dve.h, dve_ln4(half, w, 0))
        s.activation(stat[:, 4:5], stat[:, 3:4], AF.Sqrt, bias=epsl[:])
        s.drain().then_inc(s_act.h, 1)
        s_act.n += 1

    @block.scalar
    def _(s):
        if phases < 6:
            return
        s.wait_ge(s_hfl.h, 64)
        s.dma_start(out=hh[:], in_=sp_h_v[:, :, 0:1024]).then_inc(s_hh.h, 16)
        s_hh.n += 16
        for g in range(NGRP):
            for mi in range(GFF):
                emit_silu(s, 0, g, mi)
        s.wait_ge(s_pe.h, pe_gu(0, NGRP - 1, GFF - 1))
        s.dma_start(out=hh[:], in_=sp_h_v[:, :, 1024:2048]).then_inc(s_hh.h, 16)
        s_hh.n += 16
        for mi in range(GFF):
            emit_silu(s, 1, 0, mi)
            emit_sqrt4(s, 0, mi)
        for g in range(1, NGRP):
            for mi in range(GFF):
                emit_silu(s, 1, g, mi)
        for w in range(8):
            emit_sqrt4(s, 1, w)

    @block.tensor
    def _(t):
        if phases < 6:
            return
        for half in range(2):
            t.wait_ge(s_hh.h, 16 * (half + 1))
            if half == 0:
                t.wait_ge(s_dve.h, dve_3c_done)
            for g in range(NGRP):
                for mi in range(GFF):
                    gi = g * GFF + mi
                    t.wait_ge(s_w5[mi % 2].h, cst_w5(half, g, mi))
                    if gi >= 2:
                        pg, pmi = divmod(gi - 2, GFF)
                        t.wait_ge(s_dve.h, dve_gated(half, pg, pmi))
                        if mi < 2 and g >= 1:
                            t.wait_ge(s_dve.h, dve_wadd(half, g - 1, 7))
                    elif half == 1:
                        t.wait_ge(s_dve.h, dve_wadd(0, NGRP - 1, 7))
                    wgv = wgs[mi % 2][:].rearrange("p (k n) -> p k n", k=NKC)
                    wuv = wus[mi % 2][:].rearrange("p (k n) -> p k n", k=NKC)
                    for k in range(NKC):
                        for tg in range(2):
                            mm = t.matmul(pGU[mi % 2][:, tg * 512:(tg + 1) * 512],
                                          wgv[:, k, :],
                                          hh_v[:, k, tg * 512:(tg + 1) * 512],
                                          start=(k == 0), stop=(k == NKC - 1))
                        for tg in range(2):
                            mm = t.matmul(pGU[mi % 2][:, 1024 + tg * 512:1024 + (tg + 1) * 512],
                                          wuv[:, k, :],
                                          hh_v[:, k, tg * 512:(tg + 1) * 512],
                                          start=(k == 0), stop=(k == NKC - 1))
                    mm.then_inc(s_pe.h, 1)
                    s_pe.n += 1
                for w in range(8):
                    t.wait_ge(s_dve.h, dve_gated(half, g, GFF - 1))
                    if w >= 2:
                        t.wait_ge(s_dve.h, dve_wadd(half, g, w - 2))
                    elif g >= 1:
                        t.wait_ge(s_dve.h, dve_wadd(half, g - 1, 7))
                    elif half == 1:
                        t.wait_ge(s_dve.h, dve_wadd(0, NGRP - 1, 7))
                    for mi in range(GFF):
                        for ng in range(4):
                            mm = t.matmul(pGU[w % 2][:, ng * 512:(ng + 1) * 512],
                                          gtd_v[:, mi, w * 128:(w + 1) * 128],
                                          wdg_v[:, mi, ng * 512:(ng + 1) * 512],
                                          start=(mi == 0), stop=(mi == GFF - 1))
                    mm.then_inc(s_pe.h, 1)
                    s_pe.n += 1

    def emit_gated(v, half, g, mi):
        v.wait_ge(s_act.h, act_sil(half, g, mi))
        v.tensor_tensor(gtd_v[:, mi, :], sil[:],
                        pGU[mi % 2][:, 1024:2048], ALU.mult)
        v.drain().then_inc(s_dve.h, 1)
        s_dve.n += 1

    def emit_wadd(v, half, g, w):
        v.wait_ge(s_pe.h, pe_dw(half, g, w))
        if g == 0:
            v.tensor_copy(dacc[:, w * 2048:(w + 1) * 2048], pGU[w % 2][:])
        else:
            v.tensor_tensor(dacc[:, w * 2048:(w + 1) * 2048],
                            dacc[:, w * 2048:(w + 1) * 2048],
                            pGU[w % 2][:], ALU.add)
        v.drain().then_inc(s_dve.h, 1)
        s_dve.n += 1

    def emit_ln4(v, half, w):
        ln_stats(v, dacc[:, w * 2048:(w + 1) * 2048])
        v.drain().then_inc(s_dve.h, 1)
        v.wait_ge(s_act.h, act_ln4s(half, w))
        ln_finish_dve(v)
        v.drain().then_inc(s_dve.h, 1)
        v.wait_ge(s_gat.h, 16 + 16 * (half * 8 + w + 1))
        v.wait_ge(s_out.h, 16 * (half * 8 + w))
        v.tensor_scalar(xf[:], xf[:], stat[:, 6:7], None, ALU.add)
        v.drain()
        v.scalar_tensor_tensor(outst[:], dacc[:, w * 2048:(w + 1) * 2048],
                               stat[:, 5:6], xf[:], ALU.mult, ALU.add)
        v.drain().then_inc(s_dve.h, 1)
        s_dve.n += 3

    @block.vector
    def _(v):
        if phases < 6:
            return
        for g in range(NGRP):
            for mi in range(GFF):
                emit_gated(v, 0, g, mi)
            for w in range(8):
                emit_wadd(v, 0, g, w)
        for mi in range(GFF):
            emit_gated(v, 1, 0, mi)
            emit_ln4(v, 0, mi)
        for w in range(8):
            emit_wadd(v, 1, 0, w)
        for g in range(1, NGRP):
            for mi in range(GFF):
                emit_gated(v, 1, g, mi)
            for w in range(8):
                emit_wadd(v, 1, g, w)
        for w in range(8):
            emit_ln4(v, 1, w)

    if debug:
        @block.sync
        def _(sync):
            if 1 <= phases <= 2:
                sync.wait_ge(s_dve.h, dve_xnf)
                sync.dma_start(out=dbg_xnf, in_=xnf[:]).then_inc(s_out.h, 16)
                s_out.n += 16
            if phases >= 3:
                sync.wait_ge(s_dve.h, dve_car)
                sync.dma_start(out=dbg_carry, in_=carry_cols[:]).then_inc(s_out.h, 16)
                s_out.n += 16
            if 4 <= phases <= 5:
                sync.wait_ge(s_dve.h, dve_3b_done)
                sync.dma_start(out=dbg_ab, in_=AB[:]).then_inc(s_out.h, 16)
                s_out.n += 16
            sync.wait_ge(s_sp.h, s_sp.n)
            sync.wait_ge(s_x2w.h, s_x2w.n)
            sync.wait_ge(s_hfl.h, s_hfl.n)
            sync.wait_ge(s_out.h, s_out.n)

    S5.close()
    ES.close()
    return nc


def _prep_inputs(L0_attn_x, L0_attn_w_qkv, L0_attn_w_out, L0_attn_decay_param,
                 L0_ffn_w_gate, L0_ffn_w_up, L0_ffn_w_down):
    x = np.ascontiguousarray(np.asarray(L0_attn_x, dtype=np.float32).reshape(B * TSEQ, DM))
    w_qkv = np.asarray(L0_attn_w_qkv, dtype=np.float32)
    w_out = np.asarray(L0_attn_w_out, dtype=np.float32)
    decay = np.ascontiguousarray(np.asarray(L0_attn_decay_param, dtype=np.float32).reshape(1, H))
    w_gate = np.asarray(L0_ffn_w_gate, dtype=np.float32)
    w_up = np.asarray(L0_ffn_w_up, dtype=np.float32)
    w_down = np.asarray(L0_ffn_w_down, dtype=np.float32)

    wq_l = np.ascontiguousarray(
        w_qkv.reshape(NKC, 128, 3, 2, 8, 128).transpose(2, 3, 1, 0, 4, 5)
        .reshape(3, 2, 128, 16384))
    wo_l = np.ascontiguousarray(
        w_out.reshape(NKC, 128, DM).transpose(1, 0, 2).reshape(128, NKC * DM))
    wg_l = np.ascontiguousarray(
        w_gate.T.reshape(NKC, 128, NM, 128).transpose(2, 1, 0, 3)
        .reshape(NM, 128, NKC * 128))
    wu_l = np.ascontiguousarray(
        w_up.T.reshape(NKC, 128, NM, 128).transpose(2, 1, 0, 3)
        .reshape(NM, 128, NKC * 128))
    wd_l = np.ascontiguousarray(w_down.T.reshape(NM, 128, DM))

    ident = np.eye(128, dtype=np.float32)
    in_maps = []
    for c in range(NCORES):
        sel = np.zeros((128, NCORES * 128), dtype=np.float32)
        if c % 2 == 1:
            sel[:, (c - 1) * 128:c * 128] = ident
        in_maps.append({
            "x": np.ascontiguousarray(x[c * T:(c + 1) * T, :]),
            "w_qkv": wq_l, "w_out": wo_l, "w_gT": wg_l, "w_uT": wu_l,
            "w_dT": wd_l, "decay": decay, "sel": sel, "ident": ident,
        })
    return in_maps


def kernel(**inputs):
    in_maps = _prep_inputs(**inputs)
    nc = build_program()
    res = run_bass_kernel_spmd(nc, in_maps, core_ids=list(range(NCORES)))
    out = np.concatenate([np.asarray(res.results[c]["out"]) for c in range(NCORES)],
                         axis=0)
    return out.reshape(B, TSEQ, DM).astype(np.float32)


def run_dev(inputs, phases=9, debug=True):
    in_maps = _prep_inputs(**inputs)
    nc = build_program(debug=debug, phases=phases)
    res = run_bass_kernel_spmd(nc, in_maps, core_ids=list(range(NCORES)))
    return res.results


# version 09 (final)
# speedup vs baseline: 1.2631x; 1.0013x over previous
import numpy as np
import ml_dtypes
from contextlib import ExitStack

import concourse.bass as bass
import concourse.mybir as mybir
from concourse.bass_utils import run_bass_kernel_spmd

F32 = mybir.dt.float32
F32R = mybir.dt.float32r
BF16 = mybir.dt.bfloat16
AF = mybir.ActivationFunctionType
ALU = mybir.AluOpType
AX = mybir.AxisListType

NCORES = 8
B, TSEQ, DM = 4, 4096, 2048
T = 2048
H = 16
NKC = 16
NTC = 16
HID = 8192
NM = 64
GFF = 8
NGRP = NM // GFF
EPS_LN = 1e-5
EPS_DEN = 1e-6


class Sem:
    def __init__(self, es, nc, name):
        self.h = es.enter_context(nc.semaphore(name))
        self.n = 0


def bcast_col(ap_col, n):
    row = ap_col.ap[0][0]
    return bass.AP(ap_col.tensor, ap_col.offset, [[row, 128], [0, n]])


def build_program(debug=False, phases=9):
    nc = bass.Bass("TRN2", target_bir_lowering=False)

    x_d = nc.dram_tensor("x", [T, DM], F32, kind="ExternalInput").ap()
    wqkv_d = nc.dram_tensor("w_qkv", [3, 2, 128, 16384], F32, kind="ExternalInput").ap()
    wout_d = nc.dram_tensor("w_out", [128, NKC * DM], F32, kind="ExternalInput").ap()
    wg_d = nc.dram_tensor("w_gT", [NM, 128, NKC * 128], F32, kind="ExternalInput").ap()
    wu_d = nc.dram_tensor("w_uT", [NM, 128, NKC * 128], F32, kind="ExternalInput").ap()
    wd_d = nc.dram_tensor("w_dT", [NM, 128, DM], F32, kind="ExternalInput").ap()
    dec_d = nc.dram_tensor("decay", [1, H], F32, kind="ExternalInput").ap()
    sel_d = nc.dram_tensor("sel", [128, NCORES * 128], F32, kind="ExternalInput").ap()
    id_d = nc.dram_tensor("ident", [128, 128], F32, kind="ExternalInput").ap()
    out_d = nc.dram_tensor("out", [T, DM], F32, kind="ExternalOutput").ap()

    spk = "ExternalOutput" if debug else "Internal"
    sp_q = nc.dram_tensor("sp_q", [DM, T], F32, kind=spk).ap()
    sp_k = nc.dram_tensor("sp_k", [DM, T], F32, kind=spk).ap()
    sp_kv = nc.dram_tensor("sp_kv", [DM, T], F32, kind=spk).ap()
    sp_x2 = nc.dram_tensor("sp_x2", [T, DM], F32, kind=spk).ap()
    sp_h = nc.dram_tensor("sp_h", [DM, T], BF16, kind=spk).ap()
    if debug:
        dbg_xnf = nc.dram_tensor("dbg_xnf", [128, NKC * 2048], BF16, kind="ExternalOutput").ap()
        dbg_ab = nc.dram_tensor("dbg_ab", [128, NKC * 2048], BF16, kind="ExternalOutput").ap()
        dbg_carry = nc.dram_tensor("dbg_carry", [128, 32], F32, kind="ExternalOutput").ap()
    cc_in_t = nc.dram_tensor("cc_in", [128, 32], F32)
    cc_out_t = nc.dram_tensor("cc_out", [NCORES * 128, 32], F32)
    cc_in = cc_in_t.ap()
    cc_out = cc_out_t.ap()

    ES = ExitStack()

    idb = ES.enter_context(nc.sbuf_tensor("idb", [128, 128], BF16))
    idf = ES.enter_context(nc.sbuf_tensor("idf", [128, 128], F32))
    dsb = ES.enter_context(nc.sbuf_tensor("dsb", [1, H], F32))
    dcol = ES.enter_context(nc.sbuf_tensor("dcol", [128, H], F32))
    sel_sb = ES.enter_context(nc.sbuf_tensor("sel_sb", [128, NCORES * 128], F32))
    gat = ES.enter_context(nc.sbuf_tensor("gat", [128, NCORES * 32], F32))
    ones_c = ES.enter_context(nc.sbuf_tensor("ones_c", [128, 1], BF16))
    ones_r = ES.enter_context(nc.sbuf_tensor("ones_r", [1, 128], F32R))
    ones_f = ES.enter_context(nc.sbuf_tensor("ones_f", [1, 128], F32))
    carry_acc = ES.enter_context(nc.sbuf_tensor("carry_acc", [128, 32], F32))
    carry_cols = ES.enter_context(nc.sbuf_tensor("carry_cols", [128, 32], F32))
    stat = ES.enter_context(nc.sbuf_tensor("stat", [128, 8], F32))
    bnb = ES.enter_context(nc.sbuf_tensor("bnb", [128, 24], F32))
    epsl = ES.enter_context(nc.sbuf_tensor("epsl", [128, 1], F32))

    sem = lambda name: Sem(ES, nc, name)
    s_ld = sem("s_ld")
    s_idb = sem("s_idb")
    s_wq2 = [sem("s_wq20"), sem("s_wq21")]
    s_wos = sem("s_wos")
    s_x = [sem("s_x0")]
    s_sp = sem("s_sp")
    s_rd = sem("s_rd")
    s_rdk = sem("s_rdk")
    s_rdq = sem("s_rdq")
    s_ccw = sem("s_ccw")
    s_ccd = sem("s_ccd")
    s_gat = sem("s_gat")
    s_x2w = sem("s_x2w")
    s_hfl = sem("s_hfl")
    s_hh = sem("s_hh")
    s_out = sem("s_out")
    s_w5 = [sem("s_w50"), sem("s_w51")]
    s_pe = sem("s_pe")
    s_dve = sem("s_dve")
    s_act = sem("s_act")
    s_pool = sem("s_pool")

    block = ES.enter_context(nc.Block())

    def ln_stats(v, src_v, n=2048):
        for i in range(n // 512):
            v.bn_stats(bnb[:, 6 * i:6 * i + 6], src_v[:, i * 512:(i + 1) * 512])
        v.drain()
        v.bn_aggr(stat[:, 2:4], bnb[:, 0:6 * (n // 512)])
        v.drain()

    def ln_finish_dve(v):
        v.reciprocal(stat[:, 5:6], stat[:, 4:5])
        v.drain()
        v.scalar_tensor_tensor(stat[:, 6:7], stat[:, 2:3], -1.0,
                               stat[:, 5:6], ALU.mult, ALU.mult)
        v.drain()

    S12 = ExitStack()
    xnf = S12.enter_context(nc.sbuf_tensor("xnf", [128, NKC * 2048], BF16))
    wq2 = [S12.enter_context(nc.sbuf_tensor(f"wq2_{i}", [128, 16384], BF16)) for i in range(2)]
    stg = [S12.enter_context(nc.sbuf_tensor(f"stg{i}", [128, 2048], F32)) for i in range(2)]
    e_st = S12.enter_context(nc.sbuf_tensor("e_st", [128, 2048], F32))
    r_st = S12.enter_context(nc.sbuf_tensor("r_st", [128, 2048], F32))
    kre = S12.enter_context(nc.sbuf_tensor("kre", [128, 2048], F32))
    S_scr = S12.enter_context(nc.sbuf_tensor("S_scr", [128, 2048], F32))
    S1 = ExitStack()
    xs = S1.enter_context(nc.sbuf_tensor("xs", [128, DM], F32))
    xn_tm = [S1.enter_context(nc.sbuf_tensor(f"xn_tm{i}", [128, DM], BF16)) for i in range(2)]

    P01 = ExitStack()
    pS = P01.enter_context(nc.psum_tensor("pS", [128, 512], F32))
    pT = [P01.enter_context(nc.psum_tensor(f"pT{i}", [128, 2048], BF16)) for i in range(2)]

    xnf_v = xnf[:].rearrange("p (k t) -> p k t", k=NKC)
    wq2_v = [w[:].rearrange("p (k m c) -> p k m c", k=NKC, m=8) for w in wq2]

    @block.sync
    def _(sync):
        sync.dma_start(out=idf[:], in_=id_d[:]).then_inc(s_ld.h, 16)
        sync.dma_start(out=dsb[:], in_=dec_d[:]).then_inc(s_ld.h, 16)
        sync.dma_start(out=sel_sb[:], in_=sel_d[:]).then_inc(s_ld.h, 16)
    s_ld.n = 48

    @block.gpsimd
    def _(g):
        g.dma_start(out=idb[:], in_=id_d[:]).then_inc(s_idb.h, 16)
    s_idb.n = 16

    @block.vector
    def _(v):
        v.memset(ones_c[:], 1.0)
        v.memset(ones_f[:], 1.0)
        v.memset(carry_acc[:], 0.0)
        v.memset(epsl[:], EPS_LN)
        v.drain().then_inc(s_dve.h, 1)
    s_dve.n += 1
    dve_consts = s_dve.n

    @block.scalar
    def _(s):
        s.wait_ge(s_ld.h, 48)
        s.activation(dsb[:], dsb[:], AF.Sigmoid)
        s.wait_ge(s_dve.h, dve_consts)
        s.activation(ones_r[:], ones_f[:], AF.Identity)
        s.drain().then_inc(s_act.h, 1)
    s_act.n += 1
    act_sig = s_act.n

    @block.tensor
    def _(t):
        t.wait_ge(s_act.h, act_sig)
        t.matmul(pS[:, 0:H], ones_f[:], dsb[:], start=True, stop=True).then_inc(s_pe.h, 1)
    s_pe.n += 1
    pe_dcol = s_pe.n

    @block.vector
    def _(v):
        v.wait_ge(s_pe.h, pe_dcol)
        v.tensor_copy(dcol[:], pS[:, 0:H])
        v.drain().then_inc(s_dve.h, 1)
    s_dve.n += 1

    act_b1 = s_act.n
    dve_b1 = s_dve.n
    pe_b1 = s_pe.n

    @block.sync
    def _(sync):
        if phases < 1:
            return
        for tc in range(NTC):
            if tc >= 1:
                sync.wait_ge(s_act.h, act_b1 + 2 * (tc - 1) + 2)
            sync.dma_start(out=xs[:], in_=x_d[tc * 128:(tc + 1) * 128, :]) \
                .then_inc(s_x[0].h, 16)
            s_x[0].n += 16

    @block.scalar
    def _(s):
        if phases < 1:
            return
        for tc in range(NTC):
            s.wait_ge(s_dve.h, dve_b1 + 3 * tc + 1)
            s.activation(stat[:, 4:5], stat[:, 3:4], AF.Sqrt, bias=epsl[:])
            s.drain().then_inc(s_act.h, 1)
            s.wait_ge(s_dve.h, dve_b1 + 3 * tc + 2)
            if tc >= 2:
                s.wait_ge(s_pe.h, pe_b1 + tc - 1)
            s.activation(xn_tm[tc % 2][:], xs[:], AF.Identity,
                         bias=stat[:, 6:7], scale=stat[:, 5:6])
            s.drain().then_inc(s_act.h, 1)
            s_act.n += 2

    @block.vector
    def _(v):
        if phases < 1:
            return
        for tc in range(NTC):
            v.wait_ge(s_x[0].h, 16 * (tc + 1))
            ln_stats(v, xs[:])
            v.drain().then_inc(s_dve.h, 1)
            v.wait_ge(s_act.h, act_b1 + 2 * tc + 1)
            ln_finish_dve(v)
            v.drain().then_inc(s_dve.h, 1)
            v.wait_ge(s_pe.h, pe_b1 + tc + 1)
            src = pT[tc % 2][:].rearrange("p (k t) -> p k t", k=NKC)
            v.tensor_copy(xnf_v[:, :, tc * 128:(tc + 1) * 128], src)
            v.drain().then_inc(s_dve.h, 1)
            s_dve.n += 3

    @block.tensor
    def _(t):
        if phases < 1:
            return
        t.wait_ge(s_idb.h, 16)
        for tc in range(NTC):
            t.wait_ge(s_act.h, act_b1 + 2 * tc + 2)
            if tc >= 2:
                t.wait_ge(s_dve.h, dve_b1 + 3 * (tc - 2) + 3)
            for k in range(NKC):
                mm = t.transpose(pT[tc % 2][:, k * 128:(k + 1) * 128],
                                 xn_tm[tc % 2][:, k * 128:(k + 1) * 128], idb[:])
            mm.then_inc(s_pe.h, 1)
            s_pe.n += 1
    dve_xnf = dve_b1 + 3 * NTC

    P01.close()
    S1.close()
    P2 = ExitStack()
    pQ = [P2.enter_context(nc.psum_tensor(f"pQ{i}", [128, 2048], F32)) for i in range(2)]

    QUADS = [1, 2, 0]
    pe_b2 = s_pe.n
    act_b2 = s_act.n
    pool_b2 = s_pool.n

    def mmdone2(g):
        return pe_b2 + g + 1

    def v_tt(m):
        return dve_xnf + H + 2 * m + 1

    def v_scan(m):
        return dve_xnf + H + 2 * m + 2

    def psum2_free(g):
        qi, m = divmod(g, H)
        if qi == 0:
            return (s_act, act_b2 + m + 1)
        if qi == 1:
            return (s_dve, v_tt(m))
        return (s_act, act_b2 + H + m + 1)

    def stgdone2(g):
        qi, m = divmod(g, H)
        if qi == 0:
            return (s_dve, dve_xnf + m + 1)
        if qi == 1:
            return (s_dve, v_tt(m))
        return (s_dve, dve_xnf + 3 * H + m + 1)

    def stream_idx(g):
        return 2 * (g // H) + (g % H) // 8

    @block.gpsimd
    def _(g_):
        if phases < 2:
            return
        for idx in range(6):
            qi, mh = divmod(idx, 2)
            if idx >= 2:
                g_.wait_ge(s_pe.h, mmdone2(8 * (idx - 2) + 7))
            g_.dma_start(out=wq2[idx % 2][:], in_=wqkv_d[QUADS[qi], mh],
                         max_dma_last_dim=8192).then_inc(s_wq2[idx % 2].h, 16)
            s_wq2[idx % 2].n += 16

    @block.tensor
    def _(t):
        if phases < 2:
            return
        for qi in range(3):
            for m in range(H):
                g = qi * H + m
                idx = stream_idx(g)
                if m % 8 == 0:
                    t.wait_ge(s_wq2[idx % 2].h, 16 * (idx // 2 + 1))
                if g == 0:
                    t.wait_ge(s_dve.h, dve_xnf)
                if g >= 2:
                    fs, fv = psum2_free(g - 2)
                    t.wait_ge(fs.h, fv)
                wv = wq2_v[idx % 2]
                for k in range(NKC):
                    for tg in range(4):
                        mm = t.matmul(pQ[g % 2][:, tg * 512:(tg + 1) * 512],
                                      wv[:, k, m % 8, :],
                                      xnf_v[:, k, tg * 512:(tg + 1) * 512],
                                      start=(k == 0), stop=(k == NKC - 1))
                mm.then_inc(s_pe.h, 1)
                s_pe.n += 1

    @block.scalar
    def _(s):
        if phases < 2:
            return
        for m in range(H):
            s.wait_ge(s_pe.h, mmdone2(m))
            if m >= 1:
                s.wait_ge(s_dve.h, dve_xnf + m)
            s.activation(e_st[:], pQ[m % 2][:], AF.Exp)
            s.activation(r_st[:], pQ[m % 2][:], AF.Relu)
            s.drain().then_inc(s_act.h, 1)
            s_act.n += 1

    @block.scalar
    def _(s):
        if phases < 2:
            return
        for m in range(H):
            s.wait_ge(s_sp.h, 16 * H)
            if m >= 1:
                s.wait_ge(s_dve.h, v_scan(m - 1))
            s.dma_start(out=kre[:], in_=sp_k[m * 128:(m + 1) * 128, :]) \
                .then_inc(s_rd.h, 16)
            s_rd.n += 16

    @block.scalar
    def _(s):
        if phases < 2:
            return
        for m in range(H):
            g = 2 * H + m
            s.wait_ge(s_pe.h, mmdone2(g))
            if m == 0:
                s.wait_ge(s_dve.h, dve_xnf + H)
            else:
                s.wait_ge(s_dve.h, dve_xnf + 3 * H + m)
            s.activation(e_st[:], pQ[g % 2][:], AF.Exp)
            s.activation(r_st[:], pQ[g % 2][:], AF.Relu)
            s.drain().then_inc(s_act.h, 1)
            s_act.n += 1

    @block.vector
    def _(v):
        if phases < 2:
            return
        for m in range(H):
            v.wait_ge(s_act.h, act_b2 + m + 1)
            if m >= 2:
                v.wait_ge(s_sp.h, 16 * m)
            v.scalar_tensor_tensor(stg[m % 2][:], e_st[:], 1.0,
                                   r_st[:], ALU.min, ALU.add)
            v.drain().then_inc(s_dve.h, 1)
            s_dve.n += 1
        for m in range(H):
            g = H + m
            v.wait_ge(s_pe.h, mmdone2(g))
            v.wait_ge(s_rd.h, 16 * (m + 1))
            v.wait_ge(s_sp.h, 16 * (H + m))
            if m >= 2:
                v.wait_ge(s_dve.h, v_scan(m - 2))
            v.tensor_tensor(stg[m % 2][:], pQ[g % 2][:], kre[:], ALU.mult)
            v.drain().then_inc(s_dve.h, 1)
            dbc = bcast_col(dcol[:, m:m + 1], T)
            v.tensor_tensor_scan(S_scr[:], dbc, stg[m % 2][:], 0.0, ALU.mult, ALU.add)
            v.drain()
            v.tensor_copy(carry_acc[:, m:m + 1], S_scr[:, T - 1:T])
            v.drain()
            v.tensor_tensor_scan(S_scr[:], dbc, kre[:], 0.0, ALU.mult, ALU.add)
            v.drain()
            v.tensor_copy(carry_acc[:, H + m:H + m + 1], S_scr[:, T - 1:T])
            v.drain().then_inc(s_dve.h, 1)
            s_dve.n += 2
        for m in range(H):
            v.wait_ge(s_act.h, act_b2 + H + m + 1)
            v.wait_ge(s_sp.h, 16 * (2 * H + m))
            v.scalar_tensor_tensor(stg[m % 2][:], e_st[:], 1.0,
                                   r_st[:], ALU.min, ALU.add)
            v.drain().then_inc(s_dve.h, 1)
            s_dve.n += 1

    pool_all2 = s_pool.n

    spill_of = {}

    @block.sync
    def _(sync):
        if phases < 2:
            return
        targets = [sp_k, sp_kv, sp_q]
        for qi in range(3):
            for m in range(H):
                g = qi * H + m
                fs, fv = stgdone2(g)
                sync.wait_ge(fs.h, fv)
                sync.dma_start(out=targets[qi][m * 128:(m + 1) * 128, :],
                               in_=stg[m % 2][:]).then_inc(s_sp.h, 16)
                s_sp.n += 16
                spill_of[g] = s_sp.n
    sp_all2 = s_sp.n

    @block.sync
    def _(sync):
        if phases < 3:
            return
        sync.wait_ge(s_dve.h, v_scan(H - 1))
        sync.dma_start(out=cc_in, in_=carry_acc[:]).then_inc(s_ccw.h, 16)
        s_ccw.n += 16

    @block.gpsimd
    def _(g_):
        if phases < 3:
            return
        g_.wait_ge(s_ccw.h, 16)
        g_.collective_compute(
            "AllGather", ALU.bypass,
            replica_groups=[list(range(NCORES))],
            ins=[cc_in_t.ap().opt()], outs=[cc_out_t.ap().opt()],
        ).then_inc(s_ccd.h)
        s_ccd.n += 1

    @block.sync
    def _(sync):
        if phases < 3:
            return
        sync.wait_ge(s_ccd.h, 1)
        sync.dma_start(out=gat[:].rearrange("p (r j) -> p r j", r=NCORES),
                       in_=cc_out.rearrange("(r p) j -> p r j", p=128)) \
            .then_inc(s_gat.h, 16)
        s_gat.n += 16

    @block.tensor
    def _(t):
        if phases < 3:
            return
        t.wait_ge(s_gat.h, 16)
        for gg in (46, 47):
            fs, fv = psum2_free(gg)
            t.wait_ge(fs.h, fv)
        for r in range(NCORES):
            mm = t.matmul(pQ[0][:, 0:32], sel_sb[:, r * 128:(r + 1) * 128],
                          gat[:, r * 32:(r + 1) * 32],
                          start=(r == 0), stop=(r == NCORES - 1))
        mm.then_inc(s_pe.h, 1)
        s_pe.n += 1
    pe_sel = s_pe.n

    @block.vector
    def _(v):
        if phases < 3:
            return
        v.wait_ge(s_pe.h, pe_sel)
        v.tensor_copy(carry_cols[:], pQ[0][:, 0:32])
        v.drain().then_inc(s_dve.h, 1)
        s_dve.n += 1
    dve_car = s_dve.n

    S12.close()
    P2.close()

    S3w = ExitStack()
    AB = S3w.enter_context(nc.sbuf_tensor("AB", [128, NKC * 2048], BF16))
    wo_sb = S3w.enter_context(nc.sbuf_tensor("wo_sb", [128, NKC * 2048], BF16))
    AB_v = AB[:].rearrange("p (k t) -> p k t", k=NKC)
    wo_v = wo_sb[:].rearrange("p (k n) -> p k n", k=NKC)

    S3 = ExitStack()
    kv_sb = S3.enter_context(nc.sbuf_tensor("kv_sb", [128, T], F32))
    k_sb = S3.enter_context(nc.sbuf_tensor("k_sb", [128, T], F32))
    q_sb = S3.enter_context(nc.sbuf_tensor("q_sb", [128, T], F32))
    S_kv = [S3.enter_context(nc.sbuf_tensor(f"S_kv{i}", [128, T], F32)) for i in range(2)]
    S_ks = S3.enter_context(nc.sbuf_tensor("S_ks", [128, T], F32))
    pbf = S3.enter_context(nc.sbuf_tensor("pbf", [128, T], BF16))
    rows3 = S3.enter_context(nc.sbuf_tensor("rows3", [128, T], F32))
    rden_t = S3.enter_context(nc.sbuf_tensor("rden_t", [1, T], F32R))
    P3 = ExitStack()
    pBC = [P3.enter_context(nc.psum_tensor(f"pBC{i}", [128, 2048], F32)) for i in range(2)]

    @block.gpsimd
    def _(g):
        if phases < 4:
            return
        g.wait_ge(s_sp.h, sp_all2)
        g.dma_start(out=wo_sb[:], in_=wout_d[:], max_dma_last_dim=8192) \
            .then_inc(s_wos.h, 16)
        s_wos.n = 16

    dve_b3b = s_dve.n
    pool_b3b = s_pool.n
    act_b3b = s_act.n
    pe_b3b = s_pe.n
    rd_b3b = s_rd.n
    rdk_b3b = s_rdk.n
    rdq_b3b = s_rdq.n

    def scankv3(h):
        return dve_b3b + (1 if h == 0 else 5 * h - 1)

    def scanks3(h):
        return dve_b3b + (2 if h == 0 else 5 * h + 2)

    def pbf3(h):
        return dve_b3b + (3 if h == 0 else 5 * h + 3)

    def clipv3b(h):
        return dve_b3b + (5 * (h + 1) if h <= H - 2 else 5 * H - 1)

    def outprev3b(h):
        return dve_b3b + (5 * (h + 1) + 1 if h <= H - 2 else 5 * H)

    @block.scalar
    def _(s):
        if phases < 4:
            return
        for h in range(H):
            if h >= 1:
                s.wait_ge(s_dve.h, scankv3(h - 1))
            else:
                s.wait_ge(s_dve.h, dve_car)
                s.wait_ge(s_pool.h, pool_all2)
                s.wait_ge(s_sp.h, sp_all2)
            s.wait_ge(s_sp.h, spill_of[2 * H + h])
            s.dma_start(out=kv_sb[:], in_=sp_kv[h * 128:(h + 1) * 128, :]) \
                .then_inc(s_rdk.h, 16)
            s_rdk.n += 16
            if h >= 1:
                s.wait_ge(s_dve.h, clipv3b(h - 1))
                s.activation(rows3[32:33, :], rows3[0:1, :], AF.Ln)
                s.drain()
                s.wait_ge(s_pe.h, pe_b3b + 2 * (h - 1))
                s.activation(rden_t[:], rows3[32:33, :], AF.Exp, scale=-1.0)
                s.drain().then_inc(s_act.h, 1)
                s_act.n += 1
        s.wait_ge(s_dve.h, clipv3b(H - 1))
        s.activation(rows3[32:33, :], rows3[0:1, :], AF.Ln)
        s.drain()
        s.wait_ge(s_pe.h, pe_b3b + 2 * (H - 1))
        s.activation(rden_t[:], rows3[32:33, :], AF.Exp, scale=-1.0)
        s.drain().then_inc(s_act.h, 1)
        s_act.n += 1

    @block.sync
    def _(sync):
        if phases < 4:
            return
        for h in range(H):
            if h >= 1:
                sync.wait_ge(s_dve.h, scanks3(h - 1))
            else:
                sync.wait_ge(s_dve.h, dve_car)
                sync.wait_ge(s_pool.h, pool_all2)
                sync.wait_ge(s_sp.h, sp_all2)
            sync.wait_ge(s_sp.h, spill_of[2 * H + h])
            sync.dma_start(out=k_sb[:], in_=sp_k[h * 128:(h + 1) * 128, :]) \
                .then_inc(s_rd.h, 16)
            if h >= 1:
                sync.wait_ge(s_pool.h, pool_b3b + h)
            sync.dma_start(out=q_sb[:], in_=sp_q[h * 128:(h + 1) * 128, :]) \
                .then_inc(s_rdq.h, 16)
            s_rd.n += 16
            s_rdq.n += 16

    @block.vector
    def _(v):
        if phases < 4:
            return
        for h in range(H):
            v.wait_ge(s_rdk.h, rdk_b3b + 16 * (h + 1))
            if h >= 2:
                v.wait_ge(s_pool.h, pool_b3b + h - 1)
            dbc = bcast_col(dcol[:, h:h + 1], T)
            v.tensor_tensor_scan(S_kv[h % 2][:], dbc, kv_sb[:],
                                 carry_cols[:, h:h + 1], ALU.mult, ALU.add)
            v.drain().then_inc(s_dve.h, 1)
            s_dve.n += 1
            if h >= 1:
                v.wait_ge(s_pe.h, pe_b3b + 2 * (h - 1) + 1)
                if h >= 2:
                    v.wait_ge(s_act.h, act_b3b + h - 1)
                v.tensor_scalar(rows3[0:1, :], pBC[(h - 1) % 2][0:1, :],
                                EPS_DEN, None, ALU.max)
                v.drain().then_inc(s_dve.h, 1)
                v.wait_ge(s_pe.h, pe_b3b + 2 * (h - 1) + 2)
                v.wait_ge(s_pool.h, pool_b3b + h)
                v.tensor_tensor(AB_v[:, h - 1, :], S_ks[:], pBC[(h - 1) % 2][:, :],
                                ALU.mult)
                v.drain().then_inc(s_dve.h, 1)
                s_dve.n += 2
            v.wait_ge(s_rd.h, rd_b3b + 16 * (h + 1))
            v.tensor_tensor_scan(S_ks[:], dbc, k_sb[:],
                                 carry_cols[:, H + h:H + h + 1], ALU.mult, ALU.add)
            v.drain().then_inc(s_dve.h, 1)
            v.wait_ge(s_rdq.h, rdq_b3b + 16 * (h + 1))
            v.tensor_tensor(pbf[:], q_sb[:], S_ks[:], ALU.mult)
            v.drain().then_inc(s_dve.h, 1)
            s_dve.n += 2
        v.wait_ge(s_pe.h, pe_b3b + 2 * (H - 1) + 1)
        v.wait_ge(s_act.h, act_b3b + H - 1)
        v.tensor_scalar(rows3[0:1, :], pBC[(H - 1) % 2][0:1, :],
                        EPS_DEN, None, ALU.max)
        v.drain().then_inc(s_dve.h, 1)
        v.wait_ge(s_pe.h, pe_b3b + 2 * (H - 1) + 2)
        v.wait_ge(s_pool.h, pool_b3b + H)
        v.tensor_tensor(AB_v[:, H - 1, :], S_ks[:], pBC[(H - 1) % 2][:, :],
                        ALU.mult)
        v.drain().then_inc(s_dve.h, 1)
        s_dve.n += 2

    @block.gpsimd
    def _(g):
        if phases < 4:
            return
        for h in range(H):
            g.wait_ge(s_dve.h, pbf3(h))
            g.tensor_tensor(S_ks[:], q_sb[:], S_kv[h % 2][:], ALU.mult)
            g.drain().then_inc(s_pool.h, 1)
            s_pool.n += 1

    @block.tensor
    def _(t):
        if phases < 4:
            return
        for h in range(H):
            t.wait_ge(s_dve.h, pbf3(h))
            if h >= 2:
                t.wait_ge(s_dve.h, outprev3b(h - 2))
            for tg in range(4):
                mm = t.matmul(pBC[h % 2][0:1, tg * 512:(tg + 1) * 512], ones_c[:],
                              pbf[:, tg * 512:(tg + 1) * 512], start=True, stop=True)
            mm.then_inc(s_pe.h, 1)
            s_pe.n += 1
            t.wait_ge(s_act.h, act_b3b + h + 1)
            for tg in range(4):
                mm = t.matmul(pBC[h % 2][:, tg * 512:(tg + 1) * 512], ones_r[:],
                              rden_t[:, tg * 512:(tg + 1) * 512],
                              start=True, stop=True)
            mm.then_inc(s_pe.h, 1)
            s_pe.n += 1
    dve_3b_done = dve_b3b + 5 * H

    S3.close()
    P3.close()
    S3c = ExitStack()
    pW = S3c.enter_context(nc.psum_tensor("pW", [128, 2048], F32))
    pTh = [S3c.enter_context(nc.psum_tensor(f"pTh{i}", [128, 2048], BF16)) for i in range(2)]
    xs3 = S3c.enter_context(nc.sbuf_tensor("xs3", [128, DM], F32))
    x2sb = S3c.enter_context(nc.sbuf_tensor("x2sb", [128, DM], F32))
    attn_sb = [S3c.enter_context(nc.sbuf_tensor(f"attn_sb{i}", [128, DM], F32)) for i in range(2)]
    hf = [S3c.enter_context(nc.sbuf_tensor(f"hf{i}", [128, DM], BF16)) for i in range(2)]
    hstg = S3c.enter_context(nc.sbuf_tensor("hstg", [128, NKC * 512], BF16))
    hstg_v = hstg[:].rearrange("p (k t) -> p k t", k=NKC)
    sp_h_v = sp_h.rearrange("(k p) t -> p k t", p=128)

    pe_b3c = s_pe.n
    act_b3c = s_act.n
    dve_b3c = s_dve.n
    rd_b3c = s_rd.n

    def dbase3(tc):
        return dve_b3c + (6 * tc if tc < 2 else 12 + 7 * (tc - 2))

    def hevac_val(te):
        if te <= NTC - 3:
            return dbase3(te + 2) + 7
        return dve_b3c + 12 + 7 * (NTC - 2) + (te - (NTC - 2)) + 1

    def pe_mm3(tc):
        return pe_b3c + (tc + 1 if tc < 2 else 2 * tc - 1)

    def pe_tr3(te):
        if te <= NTC - 3:
            return pe_b3c + 2 * te + 4
        return pe_b3c + NTC + 1 + te

    @block.scalar
    def _(s):
        if phases < 5:
            return
        for tc in range(NTC):
            if tc == 0:
                s.wait_ge(s_dve.h, dve_3b_done)
            else:
                s.wait_ge(s_dve.h, dbase3(tc - 1) + 4)
            s.dma_start(out=xs3[:], in_=x_d[tc * 128:(tc + 1) * 128, :]) \
                .then_inc(s_rd.h, 16)
            s_rd.n += 16
            s.wait_ge(s_dve.h, dbase3(tc) + 2)
            s.activation(stat[:, 4:5], stat[:, 3:4], AF.Sqrt, bias=epsl[:])
            s.drain().then_inc(s_act.h, 1)
            s.wait_ge(s_dve.h, dbase3(tc) + 5)
            s.activation(stat[:, 4:5], stat[:, 3:4], AF.Sqrt, bias=epsl[:])
            s.drain().then_inc(s_act.h, 1)
            s.wait_ge(s_dve.h, dbase3(tc) + 6)
            if tc >= 2:
                s.wait_ge(s_pe.h, pe_tr3(tc - 2))
            s.activation(hf[tc % 2][:], x2sb[:], AF.Identity,
                         bias=stat[:, 6:7], scale=stat[:, 5:6])
            s.drain().then_inc(s_act.h, 1)
            s_act.n += 3

    @block.vector
    def _(v):
        if phases < 5:
            return
        for tc in range(NTC):
            v.wait_ge(s_pe.h, pe_mm3(tc))
            v.tensor_copy(attn_sb[tc % 2][:], pW[:])
            v.drain().then_inc(s_dve.h, 1)
            ln_stats(v, attn_sb[tc % 2][:])
            v.drain().then_inc(s_dve.h, 1)
            v.wait_ge(s_act.h, act_b3c + 3 * tc + 1)
            ln_finish_dve(v)
            v.drain().then_inc(s_dve.h, 1)
            v.wait_ge(s_rd.h, rd_b3c + 16 * (tc + 1))
            if tc >= 1:
                v.wait_ge(s_x2w.h, 16 * tc)
            v.tensor_scalar(xs3[:], xs3[:], stat[:, 6:7], None, ALU.add)
            v.drain()
            v.scalar_tensor_tensor(x2sb[:], attn_sb[tc % 2][:], stat[:, 5:6],
                                   xs3[:], ALU.mult, ALU.add)
            v.drain().then_inc(s_dve.h, 1)
            ln_stats(v, x2sb[:])
            v.drain().then_inc(s_dve.h, 1)
            v.wait_ge(s_act.h, act_b3c + 3 * tc + 2)
            ln_finish_dve(v)
            v.drain().then_inc(s_dve.h, 1)
            s_dve.n += 6
            if tc >= 2:
                te = tc - 2
                v.wait_ge(s_pe.h, pe_tr3(te))
                if te % 4 == 0 and te >= 4:
                    v.wait_ge(s_hfl.h, 16 * (te // 4))
                sv = pTh[te % 2][:].rearrange("p (k t) -> p k t", k=NKC)
                v.tensor_copy(hstg_v[:, :, (te % 4) * 128:(te % 4) * 128 + 128], sv)
                v.drain().then_inc(s_dve.h, 1)
                s_dve.n += 1
        for te in (NTC - 2, NTC - 1):
            v.wait_ge(s_pe.h, pe_tr3(te))
            if te % 4 == 0 and te >= 4:
                v.wait_ge(s_hfl.h, 16 * (te // 4))
            sv = pTh[te % 2][:].rearrange("p (k t) -> p k t", k=NKC)
            v.tensor_copy(hstg_v[:, :, (te % 4) * 128:(te % 4) * 128 + 128], sv)
            v.drain().then_inc(s_dve.h, 1)
            s_dve.n += 1

    @block.tensor
    def _(t):
        if phases < 5:
            return
        t.wait_ge(s_wos.h, 16)
        t.wait_ge(s_dve.h, dve_3b_done)
        for tc in range(NTC):
            if tc >= 1:
                t.wait_ge(s_dve.h, dbase3(tc - 1) + 1)
            for hk in range(NKC):
                for tg in range(4):
                    mm = t.matmul(pW[:, tg * 512:(tg + 1) * 512],
                                  AB_v[:, hk, tc * 128:(tc + 1) * 128],
                                  wo_v[:, hk, tg * 512:(tg + 1) * 512],
                                  start=(hk == 0), stop=(hk == NKC - 1))
            mm.then_inc(s_pe.h, 1)
            s_pe.n += 1
            if tc >= 2:
                te = tc - 2
                t.wait_ge(s_act.h, act_b3c + 3 * te + 3)
                if te >= 2:
                    t.wait_ge(s_dve.h, hevac_val(te - 2))
                for k in range(NKC):
                    mm = t.transpose(pTh[te % 2][:, k * 128:(k + 1) * 128],
                                     hf[te % 2][:, k * 128:(k + 1) * 128], idb[:])
                mm.then_inc(s_pe.h, 1)
                s_pe.n += 1
        for te in (NTC - 2, NTC - 1):
            t.wait_ge(s_act.h, act_b3c + 3 * te + 3)
            t.wait_ge(s_dve.h, hevac_val(te - 2))
            for k in range(NKC):
                mm = t.transpose(pTh[te % 2][:, k * 128:(k + 1) * 128],
                                 hf[te % 2][:, k * 128:(k + 1) * 128], idb[:])
            mm.then_inc(s_pe.h, 1)
            s_pe.n += 1

    @block.sync
    def _(sync):
        if phases < 5:
            return
        for tc in range(NTC):
            sync.wait_ge(s_dve.h, dbase3(tc) + 4)
            sync.dma_start(out=sp_x2[tc * 128:(tc + 1) * 128, :], in_=x2sb[:]) \
                .then_inc(s_x2w.h, 16)
            s_x2w.n += 16

    @block.gpsimd
    def _(g_):
        if phases < 5:
            return
        for grp in range(NTC // 4):
            g_.wait_ge(s_dve.h, hevac_val(4 * grp + 3))
            g_.dma_start(out=sp_h_v[:, :, grp * 512:(grp + 1) * 512],
                         in_=hstg_v[:, :, :]).then_inc(s_hfl.h, 16)
            s_hfl.n += 16
    dve_3c_done = dve_b3c + 12 + 7 * (NTC - 2) + 2

    S3c.close()
    S3w.close()
    S5 = ExitStack()
    pGU = [S5.enter_context(nc.psum_tensor(f"pGU{i}", [128, 2048], F32)) for i in range(2)]
    hh = S5.enter_context(nc.sbuf_tensor("hh", [128, NKC * 1024], BF16))
    dacc = S5.enter_context(nc.sbuf_tensor("dacc", [128, 8 * 2048], F32))
    gtd = S5.enter_context(nc.sbuf_tensor("gtd", [128, GFF * 1024], BF16))
    wgs = [S5.enter_context(nc.sbuf_tensor(f"wgs{i}", [128, 2048], BF16)) for i in range(2)]
    wus = [S5.enter_context(nc.sbuf_tensor(f"wus{i}", [128, 2048], BF16)) for i in range(2)]
    wdg = S5.enter_context(nc.sbuf_tensor("wdg", [128, GFF * 2048], BF16))
    sil = S5.enter_context(nc.sbuf_tensor("sil", [128, 1024], F32))
    xf = S5.enter_context(nc.sbuf_tensor("xf", [128, DM], F32))
    outst = S5.enter_context(nc.sbuf_tensor("outst", [128, DM], F32))
    hh_v = hh[:].rearrange("p (k t) -> p k t", k=NKC)
    gtd_v = gtd[:].rearrange("p (m t) -> p m t", m=GFF)
    wdg_v = wdg[:].rearrange("p (m n) -> p m n", m=GFF)

    pe_b5 = s_pe.n
    act_b5 = s_act.n
    dve_b5 = s_dve.n
    HALF_PE = NGRP * (GFF + 8)

    def dve_gated(half, g, mi):
        if half == 0:
            return dve_b5 + g * 16 + mi + 1
        if g == 0:
            return dve_b5 + 128 + 4 * mi + 1
        return dve_b5 + 168 + 16 * (g - 1) + mi + 1

    def dve_wadd(half, g, w):
        if half == 0:
            return dve_b5 + g * 16 + 8 + w + 1
        if g == 0:
            return dve_b5 + 161 + w
        return dve_b5 + 168 + 16 * (g - 1) + 8 + w + 1

    def dve_ln4(half, w, i):
        if half == 0:
            return dve_b5 + 128 + 4 * w + 2 + i
        return dve_b5 + 280 + w * 3 + i + 1

    def act_sil(half, g, mi):
        if half == 0:
            return act_b5 + g * 8 + mi + 1
        if g == 0:
            return act_b5 + 64 + 2 * mi + 1
        return act_b5 + 80 + 8 * (g - 1) + mi + 1

    def act_ln4s(half, w):
        if half == 0:
            return act_b5 + 64 + 2 * w + 2
        return act_b5 + 136 + w + 1

    def pe_gu(half, g, mi):
        return pe_b5 + half * HALF_PE + g * (GFF + 8) + mi + 1

    def pe_dw(half, g, w):
        return pe_b5 + half * HALF_PE + g * (GFF + 8) + GFF + w + 1

    def cst_w5(half, g, mi):
        return 48 * (half * (NM // 2) + (g * GFF + mi) // 2 + 1)

    @block.sync
    def _(sync):
        if phases < 6:
            return
        for half in range(2):
            for w in range(8):
                tcg = half * 8 + w
                if tcg == 0:
                    sync.wait_ge(s_dve.h, dve_3c_done)
                    sync.wait_ge(s_hfl.h, 16 * (NTC // 4))
                    sync.wait_ge(s_x2w.h, 16 * NTC)
                else:
                    pv = dve_ln4(half, w - 1, 2) if w >= 1 else dve_ln4(0, 7, 2)
                    sync.wait_ge(s_dve.h, pv)
                sync.dma_start(out=xf[:], in_=sp_x2[tcg * 128:(tcg + 1) * 128, :]) \
                    .then_inc(s_gat.h, 16)
                s_gat.n += 16
                sync.wait_ge(s_dve.h, dve_ln4(half, w, 2))
                sync.dma_start(out=out_d[tcg * 128:(tcg + 1) * 128, :], in_=outst[:]) \
                    .then_inc(s_out.h, 16)
                s_out.n += 16
        sync.wait_ge(s_out.h, s_out.n)

    @block.gpsimd
    def _(g_):
        if phases < 6:
            return
        for half in range(2):
            for g in range(NGRP):
                for mi in range(GFF):
                    m = g * GFF + mi
                    sl = mi % 2
                    gi = g * GFF + mi
                    if half == 0 and gi == 0:
                        g_.wait_ge(s_dve.h, dve_3c_done)
                        g_.wait_ge(s_x2w.h, 16 * NTC)
                        g_.wait_ge(s_hfl.h, 64)
                    if half > 0 or gi >= 2:
                        if gi >= 2:
                            pg, pmi = divmod(gi - 2, GFF)
                            g_.wait_ge(s_dve.h, dve_gated(half, pg, pmi))
                        else:
                            g_.wait_ge(s_dve.h, dve_gated(half - 1, NGRP - 1, GFF - 2 + gi))
                    g_.dma_start(out=wgs[sl][:], in_=wg_d[m], max_dma_last_dim=8192) \
                        .then_inc(s_w5[sl].h, 16)
                    g_.dma_start(out=wus[sl][:], in_=wu_d[m], max_dma_last_dim=8192) \
                        .then_inc(s_w5[sl].h, 16)
                    if mi == 0 and (g >= 1 or half >= 1):
                        if g >= 1:
                            g_.wait_ge(s_pe.h, pe_dw(half, g - 1, 7))
                        else:
                            g_.wait_ge(s_pe.h, pe_dw(half - 1, NGRP - 1, 7))
                    g_.dma_start(out=wdg_v[:, mi, :], in_=wd_d[m],
                                 max_dma_last_dim=8192).then_inc(s_w5[sl].h, 16)

    def emit_silu(s, half, g, mi):
        s.wait_ge(s_pe.h, pe_gu(half, g, mi))
        gi = g * GFF + mi
        if half > 0 or gi >= 1:
            if gi >= 1:
                pg, pmi = divmod(gi - 1, GFF)
                s.wait_ge(s_dve.h, dve_gated(half, pg, pmi))
            else:
                s.wait_ge(s_dve.h, dve_gated(half - 1, NGRP - 1, GFF - 1))
        s.activation(sil[:], pGU[mi % 2][:, 0:1024], AF.Silu)
        s.drain().then_inc(s_act.h, 1)
        s_act.n += 1

    def emit_sqrt4(s, half, w):
        s.wait_ge(s_dve.h, dve_ln4(half, w, 0))
        s.activation(stat[:, 4:5], stat[:, 3:4], AF.Sqrt, bias=epsl[:])
        s.drain().then_inc(s_act.h, 1)
        s_act.n += 1

    @block.scalar
    def _(s):
        if phases < 6:
            return
        s.wait_ge(s_hfl.h, 64)
        s.dma_start(out=hh[:], in_=sp_h_v[:, :, 0:1024]).then_inc(s_hh.h, 16)
        s_hh.n += 16
        for g in range(NGRP):
            for mi in range(GFF):
                emit_silu(s, 0, g, mi)
        s.wait_ge(s_pe.h, pe_gu(0, NGRP - 1, GFF - 1))
        s.dma_start(out=hh[:], in_=sp_h_v[:, :, 1024:2048]).then_inc(s_hh.h, 16)
        s_hh.n += 16
        for mi in range(GFF):
            emit_silu(s, 1, 0, mi)
            emit_sqrt4(s, 0, mi)
        for g in range(1, NGRP):
            for mi in range(GFF):
                emit_silu(s, 1, g, mi)
        for w in range(8):
            emit_sqrt4(s, 1, w)

    @block.tensor
    def _(t):
        if phases < 6:
            return
        for half in range(2):
            t.wait_ge(s_hh.h, 16 * (half + 1))
            if half == 0:
                t.wait_ge(s_dve.h, dve_3c_done)
            for g in range(NGRP):
                for mi in range(GFF):
                    gi = g * GFF + mi
                    t.wait_ge(s_w5[mi % 2].h, cst_w5(half, g, mi))
                    if gi >= 2:
                        pg, pmi = divmod(gi - 2, GFF)
                        t.wait_ge(s_dve.h, dve_gated(half, pg, pmi))
                        if mi < 2 and g >= 1:
                            t.wait_ge(s_dve.h, dve_wadd(half, g - 1, 7))
                    elif half == 1:
                        t.wait_ge(s_dve.h, dve_wadd(0, NGRP - 1, 7))
                    wgv = wgs[mi % 2][:].rearrange("p (k n) -> p k n", k=NKC)
                    wuv = wus[mi % 2][:].rearrange("p (k n) -> p k n", k=NKC)
                    for k in range(NKC):
                        for tg in range(2):
                            mm = t.matmul(pGU[mi % 2][:, tg * 512:(tg + 1) * 512],
                                          wgv[:, k, :],
                                          hh_v[:, k, tg * 512:(tg + 1) * 512],
                                          start=(k == 0), stop=(k == NKC - 1))
                        for tg in range(2):
                            mm = t.matmul(pGU[mi % 2][:, 1024 + tg * 512:1024 + (tg + 1) * 512],
                                          wuv[:, k, :],
                                          hh_v[:, k, tg * 512:(tg + 1) * 512],
                                          start=(k == 0), stop=(k == NKC - 1))
                    mm.then_inc(s_pe.h, 1)
                    s_pe.n += 1
                for w in range(8):
                    t.wait_ge(s_dve.h, dve_gated(half, g, GFF - 1))
                    if w >= 2:
                        t.wait_ge(s_dve.h, dve_wadd(half, g, w - 2))
                    elif g >= 1:
                        t.wait_ge(s_dve.h, dve_wadd(half, g - 1, 7))
                    elif half == 1:
                        t.wait_ge(s_dve.h, dve_wadd(0, NGRP - 1, 7))
                    for mi in range(GFF):
                        for ng in range(4):
                            mm = t.matmul(pGU[w % 2][:, ng * 512:(ng + 1) * 512],
                                          gtd_v[:, mi, w * 128:(w + 1) * 128],
                                          wdg_v[:, mi, ng * 512:(ng + 1) * 512],
                                          start=(mi == 0), stop=(mi == GFF - 1))
                    mm.then_inc(s_pe.h, 1)
                    s_pe.n += 1

    def emit_gated(v, half, g, mi):
        v.wait_ge(s_act.h, act_sil(half, g, mi))
        v.tensor_tensor(gtd_v[:, mi, :], sil[:],
                        pGU[mi % 2][:, 1024:2048], ALU.mult)
        v.drain().then_inc(s_dve.h, 1)
        s_dve.n += 1

    def emit_wadd(v, half, g, w):
        v.wait_ge(s_pe.h, pe_dw(half, g, w))
        if g == 0:
            v.tensor_copy(dacc[:, w * 2048:(w + 1) * 2048], pGU[w % 2][:])
        else:
            v.tensor_tensor(dacc[:, w * 2048:(w + 1) * 2048],
                            dacc[:, w * 2048:(w + 1) * 2048],
                            pGU[w % 2][:], ALU.add)
        v.drain().then_inc(s_dve.h, 1)
        s_dve.n += 1

    def emit_ln4(v, half, w):
        ln_stats(v, dacc[:, w * 2048:(w + 1) * 2048])
        v.drain().then_inc(s_dve.h, 1)
        v.wait_ge(s_act.h, act_ln4s(half, w))
        ln_finish_dve(v)
        v.drain().then_inc(s_dve.h, 1)
        v.wait_ge(s_gat.h, 16 + 16 * (half * 8 + w + 1))
        v.wait_ge(s_out.h, 16 * (half * 8 + w))
        v.tensor_scalar(xf[:], xf[:], stat[:, 6:7], None, ALU.add)
        v.drain()
        v.scalar_tensor_tensor(outst[:], dacc[:, w * 2048:(w + 1) * 2048],
                               stat[:, 5:6], xf[:], ALU.mult, ALU.add)
        v.drain().then_inc(s_dve.h, 1)
        s_dve.n += 3

    @block.vector
    def _(v):
        if phases < 6:
            return
        for g in range(NGRP):
            for mi in range(GFF):
                emit_gated(v, 0, g, mi)
            for w in range(8):
                emit_wadd(v, 0, g, w)
        for mi in range(GFF):
            emit_gated(v, 1, 0, mi)
            emit_ln4(v, 0, mi)
        for w in range(8):
            emit_wadd(v, 1, 0, w)
        for g in range(1, NGRP):
            for mi in range(GFF):
                emit_gated(v, 1, g, mi)
            for w in range(8):
                emit_wadd(v, 1, g, w)
        for w in range(8):
            emit_ln4(v, 1, w)

    if debug:
        @block.sync
        def _(sync):
            if 1 <= phases <= 2:
                sync.wait_ge(s_dve.h, dve_xnf)
                sync.dma_start(out=dbg_xnf, in_=xnf[:]).then_inc(s_out.h, 16)
                s_out.n += 16
            if phases >= 3:
                sync.wait_ge(s_dve.h, dve_car)
                sync.dma_start(out=dbg_carry, in_=carry_cols[:]).then_inc(s_out.h, 16)
                s_out.n += 16
            if 4 <= phases <= 5:
                sync.wait_ge(s_dve.h, dve_3b_done)
                sync.dma_start(out=dbg_ab, in_=AB[:]).then_inc(s_out.h, 16)
                s_out.n += 16
            sync.wait_ge(s_sp.h, s_sp.n)
            sync.wait_ge(s_x2w.h, s_x2w.n)
            sync.wait_ge(s_hfl.h, s_hfl.n)
            sync.wait_ge(s_out.h, s_out.n)

    S5.close()
    ES.close()
    return nc


def _prep_inputs(L0_attn_x, L0_attn_w_qkv, L0_attn_w_out, L0_attn_decay_param,
                 L0_ffn_w_gate, L0_ffn_w_up, L0_ffn_w_down):
    x = np.ascontiguousarray(np.asarray(L0_attn_x, dtype=np.float32).reshape(B * TSEQ, DM))
    w_qkv = np.asarray(L0_attn_w_qkv, dtype=np.float32)
    w_out = np.asarray(L0_attn_w_out, dtype=np.float32)
    decay = np.ascontiguousarray(np.asarray(L0_attn_decay_param, dtype=np.float32).reshape(1, H))
    w_gate = np.asarray(L0_ffn_w_gate, dtype=np.float32)
    w_up = np.asarray(L0_ffn_w_up, dtype=np.float32)
    w_down = np.asarray(L0_ffn_w_down, dtype=np.float32)

    wq_l = np.ascontiguousarray(
        w_qkv.reshape(NKC, 128, 3, 2, 8, 128).transpose(2, 3, 1, 0, 4, 5)
        .reshape(3, 2, 128, 16384))
    wo_l = np.ascontiguousarray(
        w_out.reshape(NKC, 128, DM).transpose(1, 0, 2).reshape(128, NKC * DM))
    wg_l = np.ascontiguousarray(
        w_gate.T.reshape(NKC, 128, NM, 128).transpose(2, 1, 0, 3)
        .reshape(NM, 128, NKC * 128))
    wu_l = np.ascontiguousarray(
        w_up.T.reshape(NKC, 128, NM, 128).transpose(2, 1, 0, 3)
        .reshape(NM, 128, NKC * 128))
    wd_l = np.ascontiguousarray(w_down.T.reshape(NM, 128, DM))

    ident = np.eye(128, dtype=np.float32)
    in_maps = []
    for c in range(NCORES):
        sel = np.zeros((128, NCORES * 128), dtype=np.float32)
        if c % 2 == 1:
            sel[:, (c - 1) * 128:c * 128] = ident
        in_maps.append({
            "x": np.ascontiguousarray(x[c * T:(c + 1) * T, :]),
            "w_qkv": wq_l, "w_out": wo_l, "w_gT": wg_l, "w_uT": wu_l,
            "w_dT": wd_l, "decay": decay, "sel": sel, "ident": ident,
        })
    return in_maps


def kernel(**inputs):
    in_maps = _prep_inputs(**inputs)
    nc = build_program()
    res = run_bass_kernel_spmd(nc, in_maps, core_ids=list(range(NCORES)))
    out = np.concatenate([np.asarray(res.results[c]["out"]) for c in range(NCORES)],
                         axis=0)
    return out.reshape(B, TSEQ, DM).astype(np.float32)


def run_dev(inputs, phases=9, debug=True):
    in_maps = _prep_inputs(**inputs)
    nc = build_program(debug=debug, phases=phases)
    res = run_bass_kernel_spmd(nc, in_maps, core_ids=list(range(NCORES)))
    return res.results
